# version 02
# baseline (speedup 1.0000x reference)
import contextlib
import numpy as np
import concourse.bass as bass
import concourse.mybir as mybir
from concourse.bass_utils import run_bass_kernel_spmd

F32 = mybir.dt.float32
BF16 = mybir.dt.bfloat16
U16 = mybir.dt.uint16
U8 = mybir.dt.uint8
AF = mybir.ActivationFunctionType
ALU = mybir.AluOpType
AX = mybir.AxisListType

H, CD, UP = 2048, 1024, 1024
NH, DH, DR = 16, 128, 64
I2 = 1536
RSC = 0.08838834764831845
ENG = {"v": "vector", "s": "scalar", "t": "tensor", "g": "gpsimd", "sy": "sync"}


class Tok:
    __slots__ = ("key", "val")

    def __init__(self, key, val):
        self.key = key
        self.val = val


class Ctx:
    def __init__(self, nc):
        self.nc = nc
        self.sems = {}
        self.cnt = {}
        self.hwm = {}
        self.guards = []

    def sem(self, key):
        if key not in self.sems:
            g = self.nc.semaphore(f"S_{key}")
            self.sems[key] = g.__enter__()
            self.guards.append(g)
            self.cnt[key] = 0
        return self.sems[key]


class Q:
    def __init__(self, ctx, name, eng):
        self.ctx = ctx
        self.name = name
        self.eng = eng
        ctx.sem(name)

    def wait(self, *deps):
        best = {}
        for d in deps:
            if d is None:
                continue
            for dd in (d if isinstance(d, (list, tuple)) else [d]):
                if dd is not None and (dd.key not in best or best[dd.key] < dd.val):
                    best[dd.key] = dd.val
        for key, val in best.items():
            hk = (self.name, key)
            if self.ctx.hwm.get(hk, 0) >= val:
                continue
            self.eng.wait_ge(self.ctx.sem(key), val)
            self.ctx.hwm[hk] = val

    def op(self, fn, *args, deps=(), inc=1, **kwargs):
        self.wait(*deps)
        inst = fn(*args, **kwargs)
        if inc:
            self.ctx.cnt[self.name] += inc
            inst.then_inc(self.ctx.sem(self.name), inc)
        return Tok(self.name, self.ctx.cnt[self.name])

    def dma(self, key, out, in_, deps=(), **kwargs):
        self.wait(*deps)
        inst = self.eng.dma_start(out, in_, **kwargs)
        self.ctx.sem(key)
        self.ctx.cnt[key] += 16
        inst.then_inc(self.ctx.sem(key), 16)
        return Tok(key, self.ctx.cnt[key])

    def mm_group(self, out_ap, pairs, deps=(), tile_position=None, start=True,
                 stop=True):
        kw = {} if tile_position is None else {"tile_position": tile_position}
        n = len(pairs)
        inst = None
        for i, (l, r) in enumerate(pairs):
            if i == 0:
                self.wait(*deps)
            inst = self.eng.matmul(out_ap, l, r, start=(start and i == 0),
                                   stop=(stop and i == n - 1), **kw)
        self.ctx.cnt[self.name] += 1
        inst.then_inc(self.ctx.sem(self.name), 1)
        return Tok(self.name, self.ctx.cnt[self.name])


class Buf:
    __slots__ = ("w", "r")

    def __init__(self):
        self.w = []
        self.r = []

    def rd(self):
        return list(self.w)

    def ord(self, tok):
        self.r.append(tok)

    def wd(self):
        return self.w + self.r

    def owr(self, tok):
        self.w = [tok]
        self.r = []

    def owp(self, tok):
        self.w.append(tok)


class Prog:
    def __init__(self, nc, ctx):
        self.nc = nc
        self.ctx = ctx
        self.steps = []

    def add(self, eng, fn):
        self.steps.append((eng, fn))

    def drain_dma(self):
        def mk(q):
            q.wait(*[Tok(k, v) for k, v in list(self.ctx.cnt.items())
                     if k not in ("v", "s", "t", "g", "sy") and v > 0])
        self.add("sy", mk)
        self.add("g", mk)

    def flush(self, name):
        self.drain_dma()
        steps, self.steps = self.steps, None
        with self.nc.Block(name) as blk:
            i = 0
            while i < len(steps):
                eng = steps[i][0]
                fns = [steps[i][1]]
                while i + 1 < len(steps) and steps[i + 1][0] == eng:
                    i += 1
                    fns.append(steps[i][1])

                def sec(engine, fns=fns, eng=eng):
                    q = Q(self.ctx, eng, engine)
                    for f in fns:
                        f(q)
                getattr(blk, ENG[eng])(sec)
                i += 1
        self.steps = []


def build_nc(S_all):
    Tc = S_all // 2
    NTT = Tc // 128
    NTTA = S_all // 128
    TW = min(512, Tc)
    TNC = Tc // TW
    G4 = min(4, NTT)
    NACC = (NTT // G4) * 4

    nc = bass.Bass("TRN2", target_bir_lowering=False)
    dt = lambda n, s, d: nc.dram_tensor(n, s, d, kind="ExternalInput").ap()
    x_loc = dt("x_loc", [Tc, H], F32)
    x_oth = dt("x_oth", [Tc, H], F32)
    wdkv = dt("wdkv", [H, CD], F32)
    wdq = dt("wdq", [H, CD], F32)
    wkr = dt("wkr", [H, NH * DR], F32)
    wuk = dt("wuk", [CD, UP], F32)
    wuv = dt("wuv", [CD, NH * DH], F32)
    wuq = dt("wuq", [CD, UP], F32)
    wqr = dt("wqr", [UP, NH * DR], F32)
    wout = dt("wout", [NH * DH, H], F32)
    gw = dt("gw", [H, 16], F32)
    gu16 = dt("gu16", [17, 512, 3072], U16)
    gsc = dt("gsc", [17, 4, 4 * 3072], F32)
    d16 = dt("d16", [17, 384, 2048], U16)
    dsc = dt("dsc", [17, 4, 3 * 2048], F32)
    cos_a = dt("cos_a", [S_all, 512], F32)
    sin_a = dt("sin_a", [S_all, 512], F32)
    identd = dt("identd", [128, 128], F32)
    ind4d = dt("ind4d", [4, 128], F32)

    out_h = nc.dram_tensor("out", [Tc, H], F32, kind="ExternalOutput").ap()

    di = lambda n, s, d: nc.dram_tensor(n, s, d, kind="Internal").ap()
    xn_sp = di("xn_sp", [NTTA, 128, H], BF16)
    ckv_sp = di("ckv_sp", [NTTA, 128, CD], BF16)
    cq_sp = di("cq_sp", [NTTA, 128, CD], BF16)
    v_sp = di("v_sp", [NTTA, 128, NH * DH], BF16)
    kf_sp = di("kf_sp", [NTT, 128, NH * 128], BF16)
    s_sp = di("s_sp", [NTT, 128, H], BF16)
    gacc = di("gacc", [4, NTT, 128, 512], F32)

    ctx = Ctx(nc)
    pr = Prog(nc, ctx)
    B = Buf
    es = contextlib.ExitStack()

    def sb(name, shape, dtyp):
        return es.enter_context(nc.sbuf_tensor(name, shape, dtyp))

    def ps_(name, shape, dtyp):
        return es.enter_context(nc.psum_tensor(name, shape, dtyp))

    ident = nc.alloc_sbuf_tensor("ident", [128, 128], BF16)
    ind4 = nc.alloc_sbuf_tensor("ind4", [4, 128], BF16)
    gw_sb = nc.alloc_sbuf_tensor("gw_sb", [128, 16 * 16], BF16)
    logit = nc.alloc_sbuf_tensor("logit", [128, NTT * 16], F32)
    cw = nc.alloc_sbuf_tensor("cw", [128, NTT * 16], F32)
    tau = nc.alloc_sbuf_tensor("tau", [128, NTT], F32)
    msb = nc.alloc_sbuf_tensor("msb", [128, NH * DH], BF16)
    neg = nc.alloc_sbuf_tensor("neg", [128, 16], F32)
    lw = nc.alloc_sbuf_tensor("lw", [128, 16], F32)
    sel = nc.alloc_sbuf_tensor("sel", [128, 16], F32)
    ela = nc.alloc_sbuf_tensor("ela", [128, 16], F32)
    mku = nc.alloc_sbuf_tensor("mku", [128, 16], U8)
    st8 = nc.alloc_sbuf_tensor("st8", [128, 8], F32)
    cst = nc.alloc_sbuf_tensor("cst", [128, 2], F32)

    (b_const, b_logit, b_cw, b_tau, b_msb, b_lw, b_sel, b_ela, b_mku,
     b_st8) = (B() for _ in range(10))
    (b_xnsp, b_ckvsp, b_cqsp, b_vsp, b_kfsp, b_ssp, b_gacc) = (B() for _ in range(7))

    def p0(q):
        q.dma("dcst", ident[:], identd)
        q.dma("dcst", ind4[:], ind4d)
        t = q.dma("dcst", gw_sb[:].rearrange("p (a n) -> p a n", n=16),
                  gw.rearrange("(a p) n -> p a n", p=128))
        b_const.owr(t)
        b_const.owp(q.op(q.eng.memset, neg[:], -1e30))
        b_const.owp(q.op(q.eng.memset, cst[:, 0:1], -0.5))
        b_const.owp(q.op(q.eng.memset, cst[:, 1:2], 1e-5))
    pr.add("g", p0)
    pr.flush("p0")

    def x_tile(gtt):
        if gtt < NTT:
            return x_loc[gtt * 128:(gtt + 1) * 128, :]
        return x_oth[(gtt - NTT) * 128:(gtt - NTT + 1) * 128, :]

    def fm_proj_steps(wbuf, b_w, inb, b_in, ob, b_o, ktn, ntile, mm2, b_m2,
                      extra=None):
        for half in range(0, ntile, 2):
            def tstep(q, half=half):
                for n in (half, half + 1):
                    if n >= ntile:
                        break
                    psl = n % 2
                    pairs = [(wbuf[:, (kt * ntile + n) * 128:(kt * ntile + n) * 128 + 128],
                              inb[:, kt * 128:(kt + 1) * 128]) for kt in range(ktn)]
                    t = q.mm_group(mm2[psl][:, 0:128], pairs,
                                   deps=b_in.w + b_m2[psl].wd() + b_w.w)
                    b_m2[psl].owr(t)
                    b_in.ord(t)
            pr.add("t", tstep)

            def vstep(q, half=half):
                for n in (half, half + 1):
                    if n >= ntile:
                        break
                    psl = n % 2
                    tv = q.op(q.eng.tensor_copy, ob[:, n * 128:(n + 1) * 128],
                              mm2[psl][:, 0:128],
                              deps=b_m2[psl].rd() + (b_o.wd() if n == 0 else []))
                    if n == 0:
                        b_o.owr(tv)
                    else:
                        b_o.owp(tv)
                    b_m2[psl].ord(tv)
            pr.add("v", vstep)
        if extra is not None:
            extra()

    def rope_steps(src, b_src, dstv, b_dst, csb, b_cs, ssbt, b_ss, rp1, b_r1,
                   rp2, b_r2):
        def f(q):
            sv = src[:].rearrange("p (h j) -> p h j", j=64)
            cv = csb[:].rearrange("p (h j) -> p h j", j=32)
            sv2 = ssbt[:].rearrange("p (h j) -> p h j", j=32)
            r1v = rp1[:].rearrange("p (h j) -> p h j", j=32)
            r2v = rp2[:].rearrange("p (h j) -> p h j", j=16)
            t0 = q.op(q.eng.tensor_copy, dstv[:, :, 32:64], sv[:, :, 32:64],
                      deps=b_src.w + b_cs.w + b_ss.w + b_dst.w + b_dst.r)
            b_dst.owp(t0)
            t1 = q.op(q.eng.tensor_tensor, r1v, sv[:, :, 0:32], cv, ALU.mult,
                      deps=b_r1.wd())
            b_r1.owr(t1)
            t2 = q.op(q.eng.tensor_tensor, r2v, sv[:, :, 16:32], sv2[:, :, 0:16],
                      ALU.mult, deps=b_r2.wd())
            b_r2.owr(t2)
            t3 = q.op(q.eng.tensor_tensor, dstv[:, :, 0:16], r1v[:, :, 0:16], r2v,
                      ALU.subtract, deps=[t1, t2])
            b_dst.owp(t3)
            b_r2.ord(t3)
            t4 = q.op(q.eng.tensor_tensor, r2v, sv[:, :, 0:16], sv2[:, :, 16:32],
                      ALU.mult, deps=b_r2.wd())
            b_r2.owr(t4)
            t5 = q.op(q.eng.tensor_tensor, dstv[:, :, 16:32], r1v[:, :, 16:32], r2v,
                      ALU.add, deps=[t4])
            b_dst.owp(t5)
            b_src.ord(t5)
            b_r1.ord(t5)
            b_r2.ord(t5)
            b_cs.ord(t5)
            b_ss.ord(t5)
        pr.add("v", f)

    with contextlib.ExitStack() as st_:
        sbp = lambda n, s, d: st_.enter_context(nc.sbuf_tensor(n, s, d))
        psp = lambda n, s, d: st_.enter_context(nc.psum_tensor(n, s, d))
        wA = sbp("wA", [128, 16 * CD], BF16)
        xt = [sbp(f"xt{i}", [128, H], F32) for i in range(2)]
        xnt = [sbp(f"xnt{i}", [128, H], BF16) for i in range(2)]
        sqs = sbp("sqs", [128, H], BF16)
        xnp = [sbp(f"xnp{i}", [128, H], BF16) for i in range(2)]
        ckvb = sbp("ckvb", [128, CD], BF16)
        pips = psp("pips", [128, 512], BF16)
        mm2 = [psp(f"mm2_{i}", [128, 128], F32) for i in range(2)]
        lgps = psp("lgps", [128, 16], F32)
        b_wA = B()
        b_xt = [B(), B()]
        b_xnt = [B(), B()]
        b_sq = B()
        b_xnp = [B(), B()]
        b_ckv = B()
        b_pips = B()
        b_m2 = [B(), B()]
        b_lg = B()

        def a1_w(q):
            b_wA.owr(q.dma("dw", wA[:].rearrange("p (a n) -> p a n", n=CD),
                           wdkv.rearrange("(a p) n -> p a n", p=128)))
        pr.add("g", a1_w)

        def mk_xload(gtt, sl):
            def f(q):
                t = q.dma(f"dx{sl}", xt[sl][:], x_tile(gtt), deps=b_xt[sl].wd())
                b_xt[sl].owr(t)
            return f
        pr.add("sy", mk_xload(0, 0))
        if NTTA > 1:
            pr.add("sy", mk_xload(1, 1))

        for gtt in range(NTTA):
            sl = gtt % 2

            def ln_v1(q, sl=sl):
                t = q.op(q.eng.reduce_sum, st8[:, 0:1], xt[sl][:], axis=AX.X,
                         deps=b_xt[sl].rd() + b_st8.wd())
                b_st8.owr(t)
                b_xt[sl].ord(t)
            pr.add("v", ln_v1)

            def ln_s1(q, sl=sl):
                t = q.op(q.eng.activation, sqs[:], xt[sl][:], AF.Square,
                         accum_out=st8[:, 1:2],
                         deps=b_xt[sl].rd() + b_sq.wd() + b_st8.w + b_st8.r)
                b_sq.owr(t)
                b_st8.owp(t)
                b_xt[sl].ord(t)
            pr.add("s", ln_s1)

            def ln_v2(q):
                t1 = q.op(q.eng.tensor_scalar, st8[:, 2:3], st8[:, 0:1], 1.0 / H,
                          None, ALU.mult, deps=b_st8.w)
                t2 = q.op(q.eng.tensor_scalar, st8[:, 3:4], st8[:, 2:3],
                          st8[:, 2:3], None, ALU.mult, deps=[t1])
                t3 = q.op(q.eng.scalar_tensor_tensor, st8[:, 4:5], st8[:, 1:2],
                          1.0 / H, st8[:, 3:4], ALU.mult, ALU.subtract, deps=[t2])
                b_st8.owp(t3)
            pr.add("v", ln_v2)

            def ln_s2(q):
                t1 = q.op(q.eng.activation, st8[:, 5:6], st8[:, 4:5], AF.Ln,
                          bias=cst[:, 1:2], deps=b_st8.w + b_const.w)
                t2 = q.op(q.eng.activation, st8[:, 6:7], st8[:, 5:6], AF.Exp,
                          scale=cst[:, 0:1], deps=[t1])
                b_st8.owp(t2)
            pr.add("s", ln_s2)

            def ln_v3(q, sl=sl):
                t = q.op(q.eng.tensor_scalar, xnt[sl][:], xt[sl][:], st8[:, 2:3],
                         st8[:, 6:7], ALU.subtract, ALU.mult,
                         deps=b_st8.w + b_xnt[sl].wd() + b_xt[sl].rd())
                b_xnt[sl].owr(t)
                b_xt[sl].ord(t)
                b_st8.ord(t)
            pr.add("v", ln_v3)

            if gtt + 2 < NTTA:
                pr.add("sy", mk_xload(gtt + 2, sl))

            for grp in range(4):
                def pi_t(q, sl=sl, grp=grp):
                    tk = None
                    for j in range(4):
                        tu = grp * 4 + j
                        u, k = tu // 4, tu % 4
                        tk = q.op(q.eng.transpose, pips[:, j * 128:(j + 1) * 128],
                                  xnt[sl][:, (512 * u + k):(512 * u + k + 509):4],
                                  ident[:],
                                  deps=(b_xnt[sl].w + b_pips.wd() + b_const.w)
                                  if j == 0 else (), inc=(1 if j == 3 else 0))
                    b_pips.owr(tk)
                    b_xnt[sl].ord(tk)
                pr.add("t", pi_t)

                def pi_v(q, sl=sl, grp=grp):
                    t = q.op(q.eng.tensor_copy, xnp[sl][:, grp * 512:(grp + 1) * 512],
                             pips[:],
                             deps=b_pips.rd() + (b_xnp[sl].wd() if grp == 0 else []))
                    if grp == 0:
                        b_xnp[sl].owr(t)
                    else:
                        b_xnp[sl].owp(t)
                    b_pips.ord(t)
                pr.add("v", pi_v)

            def xn_spill(q, gtt=gtt, sl=sl):
                t = q.dma(f"sxn{sl}", xn_sp[gtt], xnp[sl][:], deps=b_xnp[sl].w)
                b_xnp[sl].ord(t)
                b_xnsp.owp(t)
            pr.add("sy", xn_spill)

            for half in range(0, 8, 2):
                def ckv_t(q, sl=sl, half=half):
                    for n in (half, half + 1):
                        psl = n % 2
                        pairs = [(wA[:, (kt * 8 + n) * 128:(kt * 8 + n) * 128 + 128],
                                  xnp[sl][:, kt * 128:(kt + 1) * 128])
                                 for kt in range(16)]
                        t = q.mm_group(mm2[psl][:], pairs,
                                       deps=b_xnp[sl].w + b_m2[psl].wd() + b_wA.w)
                        b_m2[psl].owr(t)
                        b_xnp[sl].ord(t)
                pr.add("t", ckv_t)

                def ckv_v(q, half=half):
                    for n in (half, half + 1):
                        psl = n % 2
                        tv = q.op(q.eng.tensor_copy, ckvb[:, n * 128:(n + 1) * 128],
                                  mm2[psl][:],
                                  deps=b_m2[psl].rd() + (b_ckv.wd() if n == 0 else []))
                        if n == 0:
                            b_ckv.owr(tv)
                        else:
                            b_ckv.owp(tv)
                        b_m2[psl].ord(tv)
                pr.add("v", ckv_v)

            if gtt < NTT:
                def lg_t(q, gtt=gtt, sl=sl):
                    pairs = [(xnp[sl][:, kt * 128:(kt + 1) * 128],
                              gw_sb[:, kt * 16:(kt + 1) * 16]) for kt in range(16)]
                    t = q.mm_group(lgps[:], pairs,
                                   deps=b_xnp[sl].w + b_lg.wd() + b_const.w)
                    b_lg.owr(t)
                    b_xnp[sl].ord(t)
                pr.add("t", lg_t)

                def lg_v(q, gtt=gtt):
                    tv = q.op(q.eng.tensor_copy, logit[:, gtt * 16:(gtt + 1) * 16],
                              lgps[:], deps=b_lg.rd())
                    b_logit.owp(tv)
                    b_lg.ord(tv)
                pr.add("v", lg_v)

            def ckv_spill(q, gtt=gtt):
                t = q.dma("scv", ckv_sp[gtt], ckvb[:], deps=b_ckv.w)
                b_ckv.ord(t)
                b_ckvsp.owp(t)
            pr.add("sy", ckv_spill)

        def top4_v1(q):
            for tt in range(NTT):
                t0 = q.op(q.eng.tensor_copy, lw[:], logit[:, tt * 16:(tt + 1) * 16],
                          deps=b_logit.w + b_lw.wd())
                b_lw.owr(t0)
                b_logit.ord(t0)
                for r in range(4):
                    dst = st8[:, 0:1] if r < 3 else tau[:, tt:tt + 1]
                    t1 = q.op(q.eng.reduce_max, dst, lw[:], axis=AX.X,
                              deps=[t0] + (b_st8.wd() if r < 3 else b_tau.wd()))
                    if r < 3:
                        b_st8.owr(t1)
                        t2 = q.op(q.eng.tensor_scalar, mku[:], lw[:], dst, None,
                                  ALU.is_ge, deps=[t1] + b_mku.wd())
                        b_mku.owr(t2)
                        t0 = q.op(q.eng.copy_predicated, lw[:], mku[:], neg[:],
                                  deps=[t2] + b_const.w)
                        b_lw.owr(t0)
                    else:
                        b_tau.owp(t1)
                        b_lw.ord(t1)
        pr.add("v", top4_v1)

        def top4_pair(tt):
            def fs(q, tt=tt):
                t = q.op(q.eng.activation, ela[:], logit[:, tt * 16:(tt + 1) * 16],
                         AF.Exp, deps=b_logit.w + b_ela.wd() + b_tau.w)
                b_ela.owr(t)
            def fv(q, tt=tt):
                t1 = q.op(q.eng.tensor_scalar, sel[:], logit[:, tt * 16:(tt + 1) * 16],
                          tau[:, tt:tt + 1], None, ALU.is_ge,
                          deps=b_tau.w + b_logit.w + b_sel.wd())
                b_sel.owr(t1)
                t2 = q.op(q.eng.tensor_tensor, sel[:], ela[:], sel[:], ALU.mult,
                          deps=[t1] + b_ela.w)
                b_sel.owr(t2)
                b_ela.ord(t2)
                t3 = q.op(q.eng.reduce_sum, st8[:, 0:1], sel[:], axis=AX.X,
                          deps=[t2] + b_st8.wd())
                t4 = q.op(q.eng.reciprocal, st8[:, 1:2], st8[:, 0:1], deps=[t3])
                b_st8.owr(t4)
                t5 = q.op(q.eng.tensor_scalar, cw[:, tt * 16:(tt + 1) * 16], sel[:],
                          st8[:, 1:2], None, ALU.mult, deps=[t4])
                b_cw.owp(t5)
                b_sel.ord(t5)
                b_st8.ord(t5)
            return fs, fv
        for tt in range(NTT):
            fs, fv = top4_pair(tt)
            pr.add("s", fs)
            pr.add("v", fv)
        pr.flush("a1")

    with contextlib.ExitStack() as st_:
        sbp = lambda n, s, d: st_.enter_context(nc.sbuf_tensor(n, s, d))
        psp = lambda n, s, d: st_.enter_context(nc.psum_tensor(n, s, d))
        wA2 = sbp("wA2", [128, 16 * CD], BF16)
        inb = [sbp(f"a2in{i}", [128, H], BF16) for i in range(2)]
        ob = sbp("a2ob", [128, CD], BF16)
        mm2 = [psp(f"a2mm{i}", [128, 128], F32) for i in range(2)]
        b_w, b_in, b_o = B(), [B(), B()], B()
        b_m2 = [B(), B()]

        def a2_w(q):
            b_w.owr(q.dma("dw", wA2[:].rearrange("p (a n) -> p a n", n=CD),
                          wdq.rearrange("(a p) n -> p a n", p=128)))
        pr.add("g", a2_w)

        for gtt in range(NTTA):
            sl = gtt % 2

            def iload(q, gtt=gtt, sl=sl):
                t = q.dma(f"da2{sl}", inb[sl][:], xn_sp[gtt],
                          deps=b_in[sl].wd() + b_xnsp.w)
                b_in[sl].owr(t)
            pr.add("sy", iload)
            fm_proj_steps(wA2, b_w, inb[sl], b_in[sl], ob, b_o, 16, 8, mm2, b_m2)

            def ospill(q, gtt=gtt):
                t = q.dma("scq", cq_sp[gtt], ob[:], deps=b_o.w)
                b_o.ord(t)
                b_cqsp.owp(t)
            pr.add("sy", ospill)
        pr.flush("a2")

    with contextlib.ExitStack() as st_:
        sbp = lambda n, s, d: st_.enter_context(nc.sbuf_tensor(n, s, d))
        psp = lambda n, s, d: st_.enter_context(nc.psum_tensor(n, s, d))
        wA3 = sbp("wA3", [128, 16 * 1024], BF16)
        inb = [sbp(f"a3in{i}", [128, H], BF16) for i in range(2)]
        krt = sbp("krt", [128, 1024], BF16)
        krot = sbp("krot", [128, 1024], BF16)
        csb = sbp("csb", [128, 512], BF16)
        snb = sbp("snb", [128, 512], BF16)
        rp1 = sbp("rp1", [128, 512], BF16)
        rp2 = sbp("rp2", [128, 256], BF16)
        kch = sbp("kch", [128, NH * 128], BF16)
        p5 = [psp(f"a3p5{i}", [128, 512], F32) for i in range(2)]
        tpsb = psp("a3tps", [128, 512], BF16)
        b_w, b_in = B(), [B(), B()]
        b_krt, b_krot, b_cs, b_ss, b_r1, b_r2, b_kch, b_tps = (B() for _ in range(8))
        b_p5 = [B(), B()]

        def a3_w(q):
            b_w.owr(q.dma("dw", wA3[:].rearrange("p (a n) -> p a n", n=1024),
                          wkr.rearrange("(a p) n -> p a n", p=128)))
        pr.add("g", a3_w)

        for gtt in range(NTT):
            sl = gtt % 2

            def iload(q, gtt=gtt, sl=sl):
                t = q.dma(f"da3{sl}", inb[sl][:], xn_sp[gtt],
                          deps=b_in[sl].wd() + b_xnsp.w)
                b_in[sl].owr(t)
            pr.add("sy", iload)

            def csload(q, gtt=gtt):
                q.dma("da3c", csb[:], cos_a[gtt * 128:(gtt + 1) * 128, :],
                      deps=b_cs.wd() + b_ss.wd())
                ts_ = q.dma("da3c", snb[:], sin_a[gtt * 128:(gtt + 1) * 128, :])
                b_cs.owr(ts_)
                b_ss.owr(ts_)
            pr.add("g", csload)

            def a3_t(q, sl=sl):
                for n in range(2):
                    pairs = [(inb[sl][:, kt * 128:(kt + 1) * 128],
                              wA3[:, kt * 1024 + n * 512: kt * 1024 + n * 512 + 512])
                             for kt in range(16)]
                    t = q.mm_group(p5[n][:], pairs,
                                   deps=b_in[sl].w + b_p5[n].wd() + b_w.w)
                    b_p5[n].owr(t)
                    b_in[sl].ord(t)
            pr.add("t", a3_t)

            def a3_v(q):
                for n in range(2):
                    tv = q.op(q.eng.tensor_copy, krt[:, n * 512:(n + 1) * 512],
                              p5[n][:],
                              deps=b_p5[n].rd() + (b_krt.wd() if n == 0 else []))
                    if n == 0:
                        b_krt.owr(tv)
                    else:
                        b_krt.owp(tv)
                    b_p5[n].ord(tv)
            pr.add("v", a3_v)

            krotv = krot[:].rearrange("p (h j) -> p h j", j=64)
            rope_steps(krt, b_krt, krotv, b_krot, csb, b_cs, snb, b_ss,
                       rp1, b_r1, rp2, b_r2)

            for hg in range(4):
                def a3_tr(q, hg=hg):
                    tk = None
                    for j in range(4):
                        h = hg * 4 + j
                        tk = q.op(q.eng.transpose,
                                  tpsb[64:128, j * 128:(j + 1) * 128],
                                  krot[:, h * 64:(h + 1) * 64], ident[:],
                                  tile_position=(0, 64),
                                  deps=(b_krot.w + b_tps.wd() + b_const.w)
                                  if j == 0 else (), inc=(1 if j == 3 else 0))
                    b_tps.owr(tk)
                    b_krot.ord(tk)
                pr.add("t", a3_tr)

                def a3_ev(q, hg=hg):
                    t = q.op(q.eng.tensor_copy,
                             kch[64:128, hg * 512:(hg + 1) * 512], tpsb[64:128, :],
                             deps=b_tps.rd() + (b_kch.wd() if hg == 0 else []))
                    if hg == 0:
                        b_kch.owr(t)
                    else:
                        b_kch.owp(t)
                    b_tps.ord(t)
                pr.add("v", a3_ev)

            def a3_spill(q, gtt=gtt):
                t = q.dma("skf", kf_sp[gtt, 64:128, :], kch[64:128, :],
                          deps=b_kch.w)
                b_kch.ord(t)
                b_kfsp.owp(t)
            pr.add("sy", a3_spill)
        pr.flush("a3")

    with contextlib.ExitStack() as st_:
        sbp = lambda n, s, d: st_.enter_context(nc.sbuf_tensor(n, s, d))
        psp = lambda n, s, d: st_.enter_context(nc.psum_tensor(n, s, d))
        wB1 = sbp("wB1", [128, 8 * 2048], BF16)
        cin = [sbp(f"b1in{i}", [128, CD], BF16) for i in range(2)]
        vtb = sbp("vtb", [128, 2048], BF16)
        p5 = [psp(f"b1p5{i}", [128, 512], F32) for i in range(2)]
        b_w, b_in, b_vt = B(), [B(), B()], B()
        b_p5 = [B(), B()]

        def b1_w(q):
            b_w.owr(q.dma("dw", wB1[:].rearrange("p (a n) -> p a n", n=2048),
                          wuv.rearrange("(a p) n -> p a n", p=128)))
        pr.add("g", b1_w)

        for gtt in range(NTTA):
            sl = gtt % 2

            def iload(q, gtt=gtt, sl=sl):
                t = q.dma(f"db1{sl}", cin[sl][:], ckv_sp[gtt],
                          deps=b_in[sl].wd() + b_ckvsp.w)
                b_in[sl].owr(t)
            pr.add("sy", iload)

            for half in range(0, 4, 2):
                def b1_t(q, sl=sl, half=half):
                    for n in (half, half + 1):
                        psl = n % 2
                        pairs = [(cin[sl][:, kt * 128:(kt + 1) * 128],
                                  wB1[:, kt * 2048 + n * 512: kt * 2048 + n * 512 + 512])
                                 for kt in range(8)]
                        t = q.mm_group(p5[psl][:], pairs,
                                       deps=b_in[sl].w + b_p5[psl].wd() + b_w.w)
                        b_p5[psl].owr(t)
                        b_in[sl].ord(t)
                pr.add("t", b1_t)

                def b1_v(q, half=half):
                    for n in (half, half + 1):
                        psl = n % 2
                        tv = q.op(q.eng.tensor_copy, vtb[:, n * 512:(n + 1) * 512],
                                  p5[psl][:],
                                  deps=b_p5[psl].rd() + (b_vt.wd() if n == 0 else []))
                        if n == 0:
                            b_vt.owr(tv)
                        else:
                            b_vt.owp(tv)
                        b_p5[psl].ord(tv)
                pr.add("v", b1_v)

            def vspill(q, gtt=gtt):
                t = q.dma("sv", v_sp[gtt], vtb[:], deps=b_vt.w)
                b_vt.ord(t)
                b_vsp.owp(t)
            pr.add("sy", vspill)
        pr.flush("b1")

    with contextlib.ExitStack() as st_:
        sbp = lambda n, s, d: st_.enter_context(nc.sbuf_tensor(n, s, d))
        psp = lambda n, s, d: st_.enter_context(nc.psum_tensor(n, s, d))
        wQ = sbp("wQ", [128, 8 * 1024], BF16)
        wR = sbp("wR", [128, 8 * 1024], BF16)
        cin = [sbp(f"b2in{i}", [128, CD], BF16) for i in range(2)]
        vin = [sbp(f"vin{i}", [128, 2048], BF16) for i in range(2)]
        qcf = sbp("qcf", [128, 1024], BF16)
        qrt = sbp("qrt", [128, 1024], BF16)
        qtb = sbp("qtb", [128, 2048], BF16)
        csb = sbp("b2cs", [128, 512], BF16)
        snb = sbp("b2sn", [128, 512], BF16)
        rp1 = sbp("b2r1", [128, 512], BF16)
        rp2 = sbp("b2r2", [128, 256], BF16)
        mm2 = [psp(f"b2mm{i}", [128, 128], F32) for i in range(2)]
        p5 = [psp(f"b2p5{i}", [128, 512], F32) for i in range(2)]
        mps = psp("mps", [128, 2048], F32)
        b_wq, b_wr = B(), B()
        b_in, b_vin = [B(), B()], [B(), B()]
        b_qcf, b_qrt, b_qt, b_cs, b_ss, b_r1, b_r2, b_mps = (B() for _ in range(8))
        b_m2 = [B(), B()]
        b_p5 = [B(), B()]

        def b2_w(q):
            q.dma("dw", wQ[:].rearrange("p (a n) -> p a n", n=1024),
                  wuq.rearrange("(a p) n -> p a n", p=128))
            t = q.dma("dw", wR[:].rearrange("p (a n) -> p a n", n=1024),
                      wqr.rearrange("(a p) n -> p a n", p=128))
            b_wq.owr(t)
            b_wr.owr(t)
        pr.add("g", b2_w)

        for gtt in range(NTTA):
            sl = gtt % 2

            def iload(q, gtt=gtt, sl=sl):
                t = q.dma(f"db2{sl}", cin[sl][:], cq_sp[gtt],
                          deps=b_in[sl].wd() + b_cqsp.w)
                b_in[sl].owr(t)
                tv = q.dma(f"db2v{sl}", vin[sl][:], v_sp[gtt],
                           deps=b_vin[sl].wd() + b_vsp.w)
                b_vin[sl].owr(tv)
            pr.add("sy", iload)

            def csload(q, gtt=gtt):
                q.dma("db2c", csb[:], cos_a[gtt * 128:(gtt + 1) * 128, :],
                      deps=b_cs.wd() + b_ss.wd())
                ts_ = q.dma("db2c", snb[:], sin_a[gtt * 128:(gtt + 1) * 128, :])
                b_cs.owr(ts_)
                b_ss.owr(ts_)
            pr.add("g", csload)

            fm_proj_steps(wQ, b_wq, cin[sl], b_in[sl], qcf, b_qcf, 8, 8, mm2, b_m2)

            def qr_t(q):
                for n in range(2):
                    pairs = [(qcf[:, kt * 128:(kt + 1) * 128],
                              wR[:, kt * 1024 + n * 512: kt * 1024 + n * 512 + 512])
                             for kt in range(8)]
                    t = q.mm_group(p5[n][:], pairs,
                                   deps=b_qcf.w + b_p5[n].wd() + b_wr.w)
                    b_p5[n].owr(t)
                    b_qcf.ord(t)
            pr.add("t", qr_t)

            def qr_v(q):
                for n in range(2):
                    tv = q.op(q.eng.tensor_copy, qrt[:, n * 512:(n + 1) * 512],
                              p5[n][:],
                              deps=b_p5[n].rd() + (b_qrt.wd() if n == 0 else []))
                    if n == 0:
                        b_qrt.owr(tv)
                    else:
                        b_qrt.owp(tv)
                    b_p5[n].ord(tv)
            pr.add("v", qr_v)

            def qc_t(q, sl=sl):
                for n in range(2):
                    pairs = [(cin[sl][:, kt * 128:(kt + 1) * 128],
                              wQ[:, kt * 1024 + n * 512: kt * 1024 + n * 512 + 512])
                             for kt in range(8)]
                    t = q.mm_group(p5[n][:], pairs,
                                   deps=b_in[sl].w + b_p5[n].wd() + b_wq.w)
                    b_p5[n].owr(t)
                    b_in[sl].ord(t)
            pr.add("t", qc_t)

            def qc_v(q):
                for n in range(2):
                    src = p5[n][:].rearrange("p (h j) -> p h j", j=64)
                    dstv = qtb[:].rearrange("p (h j) -> p h j", j=128)[:, n * 8:(n + 1) * 8, 0:64]
                    tv = q.op(q.eng.tensor_copy, dstv, src,
                              deps=b_p5[n].rd() + (b_qt.wd() if n == 0 else []))
                    if n == 0:
                        b_qt.owr(tv)
                    else:
                        b_qt.owp(tv)
                    b_p5[n].ord(tv)
            pr.add("v", qc_v)

            qtv = qtb[:].rearrange("p (h j) -> p h j", j=128)[:, :, 64:128]
            rope_steps(qrt, b_qrt, qtv, b_qt, csb, b_cs, snb, b_ss, rp1, b_r1,
                       rp2, b_r2)

            def b2_m(q, gtt=gtt, sl=sl):
                t = None
                for h in range(NH):
                    t = q.mm_group(mps[:, h * 128:(h + 1) * 128],
                                   [(qtb[:, h * 128:(h + 1) * 128],
                                     vin[sl][:, h * 128:(h + 1) * 128])],
                                   deps=(b_qt.w + b_vin[sl].w +
                                         (b_mps.wd() if gtt == 0 else []))
                                   if h == 0 else (),
                                   start=(gtt == 0 and h % 4 == 0),
                                   stop=(gtt == NTTA - 1 and h % 4 == 3))
                b_mps.owp(t)
                b_qt.ord(t)
                b_vin[sl].ord(t)
            pr.add("t", b2_m)

        def b2_mev(q):
            t = q.op(q.eng.tensor_scalar, msb[:], mps[:], RSC, None, ALU.mult,
                     deps=b_mps.w + b_msb.wd())
            b_msb.owr(t)
            b_mps.ord(t)
        pr.add("v", b2_mev)
        pr.flush("b2")

    with contextlib.ExitStack() as st_:
        sbp = lambda n, s, d: st_.enter_context(nc.sbuf_tensor(n, s, d))
        psp = lambda n, s, d: st_.enter_context(nc.psum_tensor(n, s, d))
        wB3 = sbp("wB3", [128, 8 * 1024], BF16)
        cin = [sbp(f"b3in{i}", [128, CD], BF16) for i in range(2)]
        kch = sbp("b3kch", [128, NH * 128], BF16)
        p5 = [psp(f"b3p5{i}", [128, 512], F32) for i in range(2)]
        b_w, b_in, b_kch = B(), [B(), B()], B()
        b_p5 = [B(), B()]

        def b3_w(q):
            b_w.owr(q.dma("dw", wB3[:].rearrange("p (a n) -> p a n", n=1024),
                          wuk.rearrange("(a p) n -> p a n", p=128)))
        pr.add("g", b3_w)

        for gtt in range(NTT):
            sl = gtt % 2

            def iload(q, gtt=gtt, sl=sl):
                t = q.dma(f"db3{sl}", cin[sl][:], ckv_sp[gtt],
                          deps=b_in[sl].wd() + b_ckvsp.w)
                b_in[sl].owr(t)
            pr.add("sy", iload)

            for hg in range(4):
                def b3_t(q, sl=sl, hg=hg):
                    psl = hg % 2
                    tk = None
                    for j in range(4):
                        h = hg * 4 + j
                        pairs = [(wB3[:, kt * 1024 + h * 64: kt * 1024 + h * 64 + 64],
                                  cin[sl][:, kt * 128:(kt + 1) * 128])
                                 for kt in range(8)]
                        tk = q.mm_group(p5[psl][0:64, j * 128:(j + 1) * 128], pairs,
                                        deps=(b_in[sl].w + b_p5[psl].wd() + b_w.w)
                                        if j == 0 else ())
                    b_p5[psl].owr(tk)
                    b_in[sl].ord(tk)
                pr.add("t", b3_t)

                def b3_v(q, hg=hg):
                    psl = hg % 2
                    t = q.op(q.eng.tensor_copy, kch[0:64, hg * 512:(hg + 1) * 512],
                             p5[psl][0:64, :],
                             deps=b_p5[psl].rd() + (b_kch.wd() if hg == 0 else []))
                    if hg == 0:
                        b_kch.owr(t)
                    else:
                        b_kch.owp(t)
                    b_p5[psl].ord(t)
                pr.add("v", b3_v)

            def b3_spill(q, gtt=gtt):
                t = q.dma("skf2", kf_sp[gtt, 0:64, :], kch[0:64, :], deps=b_kch.w)
                b_kch.ord(t)
                b_kfsp.owp(t)
            pr.add("sy", b3_spill)
        pr.flush("b3")

    with contextlib.ExitStack() as st_:
        sbp = lambda n, s, d: st_.enter_context(nc.sbuf_tensor(n, s, d))
        psp = lambda n, s, d: st_.enter_context(nc.psum_tensor(n, s, d))
        wO = sbp("wO", [128, 16 * 2048], BF16)
        kin = [sbp(f"kin{i}", [128, 2048], BF16) for i in range(2)]
        outf = sbp("outf", [128, 2048], BF16)
        s_st = sbp("s_st", [128, 2048], BF16)
        p5 = [psp(f"p2p5{i}", [128, 512], F32) for i in range(2)]
        b_w, b_kin, b_of, b_sst = B(), [B(), B()], B(), B()
        b_p5 = [B(), B()]

        def p2_w(q):
            b_w.owr(q.dma("dw", wO[:].rearrange("p (a n) -> p a n", n=2048),
                          wout.rearrange("(a p) n -> p a n", p=128)))
        pr.add("g", p2_w)

        for gtt in range(NTT):
            sl = gtt % 2

            def iload(q, gtt=gtt, sl=sl):
                t = q.dma(f"dp2{sl}", kin[sl][:], kf_sp[gtt],
                          deps=b_kin[sl].wd() + b_kfsp.w)
                b_kin[sl].owr(t)
            pr.add("sy", iload)

            for hg in range(4):
                def of_t(q, sl=sl, hg=hg):
                    psl = hg % 2
                    tk = None
                    for j in range(4):
                        h = hg * 4 + j
                        tk = q.mm_group(p5[psl][:, j * 128:(j + 1) * 128],
                                        [(msb[:, h * 128:(h + 1) * 128],
                                          kin[sl][:, h * 128:(h + 1) * 128])],
                                        deps=(b_msb.w + b_kin[sl].w + b_p5[psl].wd())
                                        if j == 0 else ())
                    b_p5[psl].owr(tk)
                    b_kin[sl].ord(tk)
                pr.add("t", of_t)

                def of_v(q, hg=hg):
                    psl = hg % 2
                    t = q.op(q.eng.tensor_copy, outf[:, hg * 512:(hg + 1) * 512],
                             p5[psl][:],
                             deps=b_p5[psl].rd() + (b_of.wd() if hg == 0 else []))
                    if hg == 0:
                        b_of.owr(t)
                    else:
                        b_of.owp(t)
                    b_p5[psl].ord(t)
                pr.add("v", of_v)

            for half in range(0, 4, 2):
                def s_t(q, half=half):
                    for n in (half, half + 1):
                        psl = n % 2
                        pairs = [(outf[:, kt * 128:(kt + 1) * 128],
                                  wO[:, kt * 2048 + n * 512: kt * 2048 + n * 512 + 512])
                                 for kt in range(16)]
                        t = q.mm_group(p5[psl][:], pairs,
                                       deps=b_of.w + b_p5[psl].wd() + b_w.w)
                        b_p5[psl].owr(t)
                        b_of.ord(t)
                pr.add("t", s_t)

                def s_v(q, half=half):
                    for n in (half, half + 1):
                        psl = n % 2
                        tv = q.op(q.eng.tensor_copy, s_st[:, n * 512:(n + 1) * 512],
                                  p5[psl][:],
                                  deps=b_p5[psl].rd() + (b_sst.wd() if n == 0 else []))
                        if n == 0:
                            b_sst.owr(tv)
                        else:
                            b_sst.owp(tv)
                        b_p5[psl].ord(tv)
                pr.add("v", s_v)

            def sspill(q, gtt=gtt):
                t = q.dma("ssp", s_sp[gtt], s_st[:], deps=b_sst.w)
                b_sst.ord(t)
                b_ssp.owp(t)
            pr.add("sy", sspill)
        pr.flush("p2")

    with contextlib.ExitStack() as st_:
        sbp = lambda n, s, d: st_.enter_context(nc.sbuf_tensor(n, s, d))
        psp = lambda n, s, d: st_.enter_context(nc.psum_tensor(n, s, d))
        xn_f = sbp("xn_f", [128, 16 * Tc], BF16)
        act = sbp("act", [128, 12 * Tc], BF16)
        u16b = sbp("u16b", [128, 4 * 512], U16)
        wdq_ = sbp("wdq_", [128, 16 * 512], BF16)
        sclb = sbp("sclb", [128, 4 * 512], BF16)
        gss = sbp("gss", [4, 4 * 512], BF16)
        t7b = sbp("t7b", [128, 512], U16)
        mbb = [sbp(f"mbb{i}", [128, 512], U16) for i in range(4)]
        sgb = [sbp(f"sgb{i}", [128, 512], U16) for i in range(4)]
        pcb = [sbp(f"pcb{i}", [128, 512], BF16) for i in range(4)]
        stg = [sbp(f"stg{i}", [128, G4 * 512], F32) for i in range(2)]
        p5 = [psp(f"p3p5{i}", [128, 512], F32) for i in range(2)]
        sclp = psp("sclp", [128, 2048], F32)
        b_xnf, b_act, b_u16, b_wd, b_scl, b_gss = (B() for _ in range(6))
        b_t7 = B()
        b_mb = [B() for _ in range(4)]
        b_sg = [B() for _ in range(4)]
        b_pc = [B() for _ in range(4)]
        b_stg = [B(), B()]
        b_sclp = B()
        b_p5 = [B(), B()]

        def ffn_xn(q):
            xv = xn_f[:].rearrange("p (a n) -> p a n", n=Tc)
            for tt in range(NTT):
                t = q.dma("dxnf", xv[:, :, tt * 128:(tt + 1) * 128],
                          xn_sp[tt].rearrange("p (a n) -> p a n", n=128),
                          deps=(b_xnf.wd() if tt == 0 else []) + b_xnsp.w)
                b_xnf.owp(t)
        pr.add("sy", ffn_xn)

        order_g = [0, 3, 1, 4, 2, 5]
        actv = act[:].rearrange("p (a n) -> p a n", n=Tc)
        xnv = xn_f[:].rearrange("p (a n) -> p a n", n=Tc)

        def dequant_steps(nu, scoffs):
            def f_sc(q):
                tk = None
                for u in range(nu):
                    tk = q.mm_group(sclp[:, u * 512:(u + 1) * 512],
                                    [(ind4[:], gss[0:4, u * 512:(u + 1) * 512])],
                                    deps=(b_sclp.wd() + b_gss.w + b_const.w)
                                    if u == 0 else ())
                b_sclp.owr(tk)
                b_gss.ord(tk)
            pr.add("t", f_sc)

            def f_scv(q):
                t = q.op(q.eng.tensor_copy, sclb[:, 0:nu * 512], sclp[:, 0:nu * 512],
                         deps=b_sclp.rd() + b_scl.wd())
                b_scl.owr(t)
                b_sclp.ord(t)
            pr.add("v", f_scv)

            for u in range(nu):
                def f_v1(q, u=u):
                    ub = u16b[:, u * 512:(u + 1) * 512]
                    for k in range(4):
                        t1 = q.op(q.eng.tensor_scalar, t7b[:], ub, 4 * k, 7,
                                  ALU.logical_shift_right, ALU.bitwise_and,
                                  deps=b_u16.w + b_t7.wd())
                        b_t7.owr(t1)
                        b_u16.ord(t1)
                        t2 = q.op(q.eng.tensor_scalar, mbb[k][:], t7b[:], 64, 16128,
                                  ALU.mult, ALU.add, deps=[t1] + b_mb[k].wd())
                        b_mb[k].owr(t2)
                        b_t7.ord(t2)
                        t3 = q.op(q.eng.tensor_scalar, sgb[k][:], ub, 12 - 4 * k,
                                  32768, ALU.logical_shift_left, ALU.bitwise_and,
                                  deps=b_sg[k].wd())
                        b_sg[k].owr(t3)
                        b_u16.ord(t3)
                pr.add("v", f_v1)

                def f_s(q, u=u):
                    for k in range(4):
                        t = q.op(q.eng.activation, pcb[k][:],
                                 mbb[k][:].bitcast(BF16), AF.Relu, bias=1.0,
                                 scale=-1.0, deps=b_mb[k].w + b_pc[k].wd())
                        b_pc[k].owr(t)
                        b_mb[k].ord(t)
                pr.add("s", f_s)

                def f_v2(q, u=u):
                    for k in range(4):
                        kt = 4 * u + k
                        wsl = wdq_[:, kt * 512:(kt + 1) * 512]
                        t1 = q.op(q.eng.tensor_tensor, wsl, mbb[k][:].bitcast(BF16),
                                  pcb[k][:], ALU.subtract,
                                  deps=b_pc[k].w + b_mb[k].w +
                                  (b_wd.wd() if (u == 0 and k == 0) else []))
                        t2 = q.op(q.eng.tensor_tensor, wsl.bitcast(U16),
                                  wsl.bitcast(U16), sgb[k][:], ALU.bitwise_or,
                                  deps=[t1] + b_sg[k].w)
                        t3 = q.op(q.eng.tensor_tensor, wsl, wsl,
                                  sclb[:, u * 512:(u + 1) * 512], ALU.mult,
                                  deps=[t2] + b_scl.w)
                        if u == 0 and k == 0:
                            b_wd.owr(t3)
                        else:
                            b_wd.owp(t3)
                        b_pc[k].ord(t3)
                        b_sg[k].ord(t3)
                        b_mb[k].ord(t3)
                        b_scl.ord(t3)
                pr.add("v", f_v2)

        for e in range(17):
            for nb in order_g:
                def gu_load(q, e=e, nb=nb):
                    t = q.dma("du", u16b[:].rearrange("p (a n) -> p a n", n=512),
                              gu16[e, :, nb * 512:(nb + 1) * 512]
                              .rearrange("(a p) n -> p a n", p=128),
                              deps=b_u16.wd())
                    b_u16.owr(t)
                pr.add("sy", gu_load)

                def gss_load(q, e=e, nb=nb):
                    tg = q.dma("dgs", gss[:].rearrange("g (u n) -> g u n", n=512),
                               gsc[e][0:4, :].rearrange("g (u n) -> g u n", n=3072)
                               [:, :, nb * 512:(nb + 1) * 512],
                               deps=b_gss.wd())
                    b_gss.owr(tg)
                pr.add("g", gss_load)

                dequant_steps(4, None)

                is_up = nb >= 3
                jbase = (nb - 3) * 4 if is_up else nb * 4
                for ntl in range(4):
                    j = jbase + ntl
                    for tc0 in range(0, TNC, 2):
                        tcs = [tc for tc in (tc0, tc0 + 1) if tc < TNC]

                        def gu_t(q, ntl=ntl, tcs=tcs):
                            for tc in tcs:
                                psl = tc % 2
                                pairs = [(wdq_[:, kt * 512 + ntl * 128:
                                               kt * 512 + ntl * 128 + 128],
                                          xnv[:, kt, tc * TW:(tc + 1) * TW])
                                         for kt in range(16)]
                                t = q.mm_group(p5[psl][:, 0:TW], pairs,
                                               deps=b_wd.w + b_xnf.w + b_p5[psl].wd())
                                b_p5[psl].owr(t)
                                b_wd.ord(t)
                                b_xnf.ord(t)
                        pr.add("t", gu_t)

                        if not is_up:
                            def gu_s(q, j=j, tcs=tcs):
                                for tc in tcs:
                                    psl = tc % 2
                                    t = q.op(q.eng.activation,
                                             actv[:, j, tc * TW:(tc + 1) * TW],
                                             p5[psl][:, 0:TW], AF.Silu,
                                             deps=b_p5[psl].rd() + b_act.w + b_act.r)
                                    b_act.owp(t)
                                    b_p5[psl].ord(t)
                            pr.add("s", gu_s)
                        else:
                            def gu_v(q, j=j, tcs=tcs):
                                for tc in tcs:
                                    psl = tc % 2
                                    asl = actv[:, j, tc * TW:(tc + 1) * TW]
                                    t = q.op(q.eng.tensor_tensor, asl, asl,
                                             p5[psl][:, 0:TW], ALU.mult,
                                             deps=b_p5[psl].rd() + b_act.w)
                                    b_act.owp(t)
                                    b_p5[psl].ord(t)
                            pr.add("v", gu_v)

            for hb in range(4):
                def dn_load(q, e=e, hb=hb):
                    t = q.dma("du", u16b[:, 0:3 * 512].rearrange("p (a n) -> p a n", n=512),
                              d16[e, :, hb * 512:(hb + 1) * 512]
                              .rearrange("(a p) n -> p a n", p=128),
                              deps=b_u16.wd())
                    b_u16.owr(t)
                pr.add("sy", dn_load)

                def dss_load(q, e=e, hb=hb):
                    tg = q.dma("dgs", gss[0:4, 0:3 * 512].rearrange("g (u n) -> g u n", n=512),
                               dsc[e][0:4, :].rearrange("g (u n) -> g u n", n=2048)
                               [:, :, hb * 512:(hb + 1) * 512],
                               deps=b_gss.wd())
                    b_gss.owr(tg)
                pr.add("g", dss_load)

                dequant_steps(3, None)

                for tg in range(NTT // G4):
                    for ti0 in range(0, G4, 2):
                        tis = [ti for ti in (ti0, ti0 + 1) if ti < G4]

                        def dn_t(q, tg=tg, hb=hb, tis=tis):
                            for ti in tis:
                                tt = tg * G4 + ti
                                psl = ti % 2
                                pairs = [(actv[:, kt, tt * 128:(tt + 1) * 128],
                                          wdq_[:, kt * 512:(kt + 1) * 512])
                                         for kt in range(12)]
                                t = q.mm_group(p5[psl][:], pairs,
                                               deps=b_act.w + b_wd.w + b_p5[psl].wd())
                                b_p5[psl].owr(t)
                                b_wd.ord(t)
                                b_act.ord(t)
                        pr.add("t", dn_t)

                        def dn_v(q, tg=tg, hb=hb, e=e, tis=tis):
                            ssl = tg % 2
                            for ti in tis:
                                tt = tg * G4 + ti
                                psl = ti % 2
                                dst = stg[ssl][:, ti * 512:(ti + 1) * 512]
                                if e < 16:
                                    t = q.op(q.eng.tensor_scalar, dst, p5[psl][:],
                                             cw[:, tt * 16 + e:tt * 16 + e + 1], None,
                                             ALU.mult,
                                             deps=b_p5[psl].rd() + (b_stg[ssl].wd() if ti == 0 else []) + b_cw.w)
                                else:
                                    t = q.op(q.eng.tensor_copy, dst, p5[psl][:],
                                             deps=b_p5[psl].rd() + (b_stg[ssl].wd() if ti == 0 else []))
                                b_stg[ssl].owr(t) if ti == 0 else b_stg[ssl].owp(t)
                                b_p5[psl].ord(t)
                        pr.add("v", dn_v)

                    def dn_acc(q, tg=tg, hb=hb, e=e):
                        ssl = tg % 2
                        kw = {} if e == 0 else {"accum_op": ALU.add}
                        gate = [Tok("ga", 16 * NACC * e)] if e > 0 else []
                        t = q.dma("ga",
                                  gacc[hb, tg * G4:(tg + 1) * G4]
                                  .rearrange("a p n -> p a n"),
                                  stg[ssl][:].rearrange("p (a n) -> p a n", n=512),
                                  deps=b_stg[ssl].w + gate, **kw)
                        b_stg[ssl].ord(t)
                        b_gacc.owp(t)
                    pr.add("g", dn_acc)
        pr.flush("p3")
        t_ga = Tok("ga", ctx.cnt.get("ga", 0))

    with contextlib.ExitStack() as st_:
        sbp = lambda n, s, d: st_.enter_context(nc.sbuf_tensor(n, s, d))
        gac = sbp("gac", [128, H], F32)
        gsl = sbp("gsl", [128, H], BF16)
        s_in = sbp("s_in", [128, H], BF16)
        x_in = sbp("x_in", [128, H], F32)
        sgf = sbp("sgf", [128, H], F32)
        b_gac, b_gsl, b_sin, b_xin, b_sgf = (B() for _ in range(5))

        for tt in range(NTT):
            def p4_l(q, tt=tt):
                gv = gac[:].rearrange("p (a n) -> p a n", n=512)
                t1 = q.dma("dp4g", gv, gacc[:, tt].rearrange("a p n -> p a n"),
                           deps=b_gac.wd() + [t_ga])
                b_gac.owr(t1)
                t2 = q.dma("dp4s", s_in[:], s_sp[tt], deps=b_sin.wd() + b_ssp.w)
                b_sin.owr(t2)
                t3 = q.dma("dp4x", x_in[:], x_loc[tt * 128:(tt + 1) * 128, :],
                           deps=b_xin.wd())
                b_xin.owr(t3)
            pr.add("sy", p4_l)

            def p4_s(q):
                t = q.op(q.eng.activation, gsl[:], gac[:], AF.Silu,
                         deps=b_gac.w + b_gsl.wd())
                b_gsl.owr(t)
                b_gac.ord(t)
            pr.add("s", p4_s)

            def p4_v(q):
                t1 = q.op(q.eng.tensor_tensor, sgf[:], s_in[:], gsl[:], ALU.mult,
                          deps=b_gsl.w + b_sin.w + b_sgf.wd())
                b_sin.ord(t1)
                b_gsl.ord(t1)
                t2 = q.op(q.eng.tensor_tensor, sgf[:], x_in[:], sgf[:], ALU.add,
                          deps=[t1] + b_xin.w)
                b_sgf.owr(t2)
                b_xin.ord(t2)
            pr.add("v", p4_v)

            def p4_o(q, tt=tt):
                t = q.dma("dout", out_h[tt * 128:(tt + 1) * 128, :], sgf[:],
                          deps=b_sgf.w)
                b_sgf.ord(t)
            pr.add("sy", p4_o)

        def fin(q):
            q.wait(Tok("dout", ctx.cnt["dout"]))
        pr.add("sy", fin)
        pr.flush("p4")

    return nc


def _mk_pi(n):
    r = np.arange(n)
    tile, p = r // 128, r % 128
    u, k = tile // 4, tile % 4
    return 512 * u + 4 * p + k


def _pack_u16(packed):
    k8, n = packed.shape
    return np.ascontiguousarray(
        packed.view(np.uint16).reshape(k8, n, 2).transpose(0, 2, 1).reshape(k8 * 2, n))


def _resc(s, nu):
    g, n = s.shape
    return np.ascontiguousarray(
        s.reshape(nu, 4, n).transpose(1, 0, 2).reshape(4, nu * n))


def prep_inputs(inputs):
    x = np.asarray(inputs["L0_attn_x"], dtype=np.float32)
    B_, S, H_ = x.shape
    PIH = _mk_pi(H)
    PII = _mk_pi(I2)
    colsI = np.concatenate([PII, I2 + PII])

    wdkv = np.ascontiguousarray(np.asarray(inputs["L0_attn_W_dkv"])[PIH])
    wdq = np.ascontiguousarray(np.asarray(inputs["L0_attn_W_dq"])[PIH])
    wkr = np.ascontiguousarray(np.asarray(inputs["L0_attn_W_kr"])[PIH])
    wuk = np.ascontiguousarray(np.asarray(inputs["L0_attn_W_uk"]))
    wuv = np.ascontiguousarray(np.asarray(inputs["L0_attn_W_uv"]))
    wuq = np.ascontiguousarray(np.asarray(inputs["L0_attn_W_uq"]))
    wqr = np.ascontiguousarray(np.asarray(inputs["L0_attn_W_qr"]))
    wout = np.ascontiguousarray(np.asarray(inputs["L0_attn_W_out"]))
    gwT = np.ascontiguousarray(np.asarray(inputs["L0_ffn_gate_w"]).T[PIH])

    gup = np.asarray(inputs["L0_ffn_gate_up_packed"])
    gus = np.asarray(inputs["L0_ffn_gate_up_scales"])
    dnp = np.asarray(inputs["L0_ffn_down_packed"])
    dns = np.asarray(inputs["L0_ffn_down_scales"])
    sgup = np.asarray(inputs["L0_ffn_shared_gate_up_packed"])
    sgus = np.asarray(inputs["L0_ffn_shared_gate_up_scales"])
    sdnp = np.asarray(inputs["L0_ffn_shared_down_packed"])
    sdns = np.asarray(inputs["L0_ffn_shared_down_scales"])

    gu16 = np.stack([_pack_u16(gup[e])[:, colsI] for e in range(16)]
                    + [_pack_u16(sgup)[:, colsI]])
    gscv = np.stack([_resc(gus[e][:, colsI], 4) for e in range(16)]
                    + [_resc(sgus[:, colsI], 4)]).astype(np.float32)
    d16 = np.stack([_pack_u16(dnp[e]) for e in range(16)] + [_pack_u16(sdnp)])
    dscv = np.stack([_resc(dns[e], 3) for e in range(16)]
                    + [_resc(sdns, 3)]).astype(np.float32)

    half_dim = DR // 2
    inv = (1.0 / 10000.0 ** (np.arange(0, half_dim, 2, dtype=np.float32)
                             / np.float32(half_dim))).astype(np.float32)
    tpos = (np.arange(S, dtype=np.float32) / np.float32(40.0))
    fr = np.outer(tpos, inv).astype(np.float32)
    rot = np.concatenate([fr, fr], axis=1)
    cosT = np.tile(np.cos(rot), (1, NH)).astype(np.float32)
    sinT = np.tile(np.sin(rot), (1, NH)).astype(np.float32)

    identd = np.eye(128, dtype=np.float32)
    ind4d = (np.arange(128)[None, :] // 32 == np.arange(4)[:, None]
             ).astype(np.float32)

    Tc = S // 2
    shared = dict(wdkv=wdkv, wdq=wdq, wkr=wkr, wuk=wuk, wuv=wuv, wuq=wuq,
                  wqr=wqr, wout=wout, gw=gwT, gu16=gu16, gsc=gscv, d16=d16,
                  dsc=dscv, identd=identd, ind4d=ind4d)
    in_maps = []
    for c in range(8):
        b, hf = c // 2, c % 2
        oth = 1 - hf
        m = dict(shared)
        m["x_loc"] = np.ascontiguousarray(x[b, hf * Tc:(hf + 1) * Tc])
        m["x_oth"] = np.ascontiguousarray(x[b, oth * Tc:(oth + 1) * Tc])
        m["cos_a"] = np.ascontiguousarray(np.concatenate(
            [cosT[hf * Tc:(hf + 1) * Tc], cosT[oth * Tc:(oth + 1) * Tc]]))
        m["sin_a"] = np.ascontiguousarray(np.concatenate(
            [sinT[hf * Tc:(hf + 1) * Tc], sinT[oth * Tc:(oth + 1) * Tc]]))
        in_maps.append(m)

    return in_maps, S, (B_, H_)


def kernel(**inputs):
    in_maps, S, (B_, H_) = prep_inputs(inputs)
    Tc = S // 2
    nc = build_nc(S)
    res = run_bass_kernel_spmd(nc, in_maps, core_ids=list(range(8)))
    out = np.empty((B_, S, H_), dtype=np.float32)
    for c in range(8):
        b, hf = c // 2, c % 2
        out[b, hf * Tc:(hf + 1) * Tc, :] = res.results[c]["out"]
    return out


# version 09
# speedup vs baseline: 1.2458x; 1.2458x over previous
import contextlib
import numpy as np
import concourse.bass as bass
import concourse.mybir as mybir
from concourse.bass_utils import run_bass_kernel_spmd

F32 = mybir.dt.float32
BF16 = mybir.dt.bfloat16
U16 = mybir.dt.uint16
U8 = mybir.dt.uint8
AF = mybir.ActivationFunctionType
ALU = mybir.AluOpType
AX = mybir.AxisListType

H, CD, UP = 2048, 1024, 1024
NH, DH, DR = 16, 128, 64
I2 = 1536
RSC = 0.08838834764831845
ENG = {"v": "vector", "s": "scalar", "t": "tensor", "g": "gpsimd", "sy": "sync"}


class Tok:
    __slots__ = ("key", "val")

    def __init__(self, key, val):
        self.key = key
        self.val = val


class Ctx:
    def __init__(self, nc):
        self.nc = nc
        self.sems = {}
        self.cnt = {}
        self.hwm = {}
        self.guards = []

    def sem(self, key):
        if key not in self.sems:
            g = self.nc.semaphore(f"S_{key}")
            self.sems[key] = g.__enter__()
            self.guards.append(g)
            self.cnt[key] = 0
        return self.sems[key]


class Q:
    def __init__(self, ctx, name, eng):
        self.ctx = ctx
        self.name = name
        self.eng = eng
        ctx.sem(name)

    def wait(self, *deps):
        best = {}
        for d in deps:
            if d is None:
                continue
            for dd in (d if isinstance(d, (list, tuple)) else [d]):
                if dd is not None and (dd.key not in best or best[dd.key] < dd.val):
                    best[dd.key] = dd.val
        for key, val in best.items():
            hk = (self.name, key)
            if self.ctx.hwm.get(hk, 0) >= val:
                continue
            self.eng.wait_ge(self.ctx.sem(key), val)
            self.ctx.hwm[hk] = val

    def op(self, fn, *args, deps=(), inc=1, **kwargs):
        self.wait(*deps)
        inst = fn(*args, **kwargs)
        if inc:
            self.ctx.cnt[self.name] += inc
            inst.then_inc(self.ctx.sem(self.name), inc)
        return Tok(self.name, self.ctx.cnt[self.name])

    def dma(self, key, out, in_, deps=(), **kwargs):
        self.wait(*deps)
        inst = self.eng.dma_start(out, in_, **kwargs)
        self.ctx.sem(key)
        self.ctx.cnt[key] += 16
        inst.then_inc(self.ctx.sem(key), 16)
        return Tok(key, self.ctx.cnt[key])

    def mm_group(self, out_ap, pairs, deps=(), tile_position=None, start=True,
                 stop=True):
        kw = {} if tile_position is None else {"tile_position": tile_position}
        n = len(pairs)
        inst = None
        for i, (l, r) in enumerate(pairs):
            if i == 0:
                self.wait(*deps)
            inst = self.eng.matmul(out_ap, l, r, start=(start and i == 0),
                                   stop=(stop and i == n - 1), **kw)
        self.ctx.cnt[self.name] += 1
        inst.then_inc(self.ctx.sem(self.name), 1)
        return Tok(self.name, self.ctx.cnt[self.name])


class Buf:
    __slots__ = ("w", "r")

    def __init__(self):
        self.w = []
        self.r = []

    def rd(self):
        return list(self.w)

    def ord(self, tok):
        self.r.append(tok)

    def wd(self):
        return self.w + self.r

    def owr(self, tok):
        self.w = [tok]
        self.r = []

    def owp(self, tok):
        self.w.append(tok)


class Prog:
    def __init__(self, nc, ctx):
        self.nc = nc
        self.ctx = ctx
        self.steps = []

    def add(self, eng, fn):
        self.steps.append((eng, fn))

    def drain_dma(self):
        def mk(q):
            q.wait(*[Tok(k, v) for k, v in list(self.ctx.cnt.items())
                     if k not in ("v", "s", "t", "g", "sy") and v > 0])
        self.add("sy", mk)
        self.add("g", mk)

    def flush(self, name):
        self.drain_dma()
        steps, self.steps = self.steps, None
        with self.nc.Block(name) as blk:
            i = 0
            while i < len(steps):
                eng = steps[i][0]
                fns = [steps[i][1]]
                while i + 1 < len(steps) and steps[i + 1][0] == eng:
                    i += 1
                    fns.append(steps[i][1])

                def sec(engine, fns=fns, eng=eng):
                    q = Q(self.ctx, eng, engine)
                    for f in fns:
                        f(q)
                getattr(blk, ENG[eng])(sec)
                i += 1
        self.steps = []


def build_nc(S_all):
    Tc = S_all // 2
    NTT = Tc // 128
    NTTA = S_all // 128
    TW = min(512, Tc)
    TNC = Tc // TW
    G4 = min(2, NTT)
    NACC = (NTT // G4) * 4

    nc = bass.Bass("TRN2", target_bir_lowering=False)
    dt = lambda n, s, d: nc.dram_tensor(n, s, d, kind="ExternalInput").ap()
    x_loc = dt("x_loc", [Tc, H], F32)
    x_oth = dt("x_oth", [Tc, H], F32)
    wdkv = dt("wdkv", [H, CD], F32)
    wdq = dt("wdq", [H, CD], F32)
    wkr = dt("wkr", [H, NH * DR], F32)
    wuk = dt("wuk", [CD, UP], F32)
    wuv = dt("wuv", [CD, NH * DH], F32)
    wuq = dt("wuq", [CD, UP], F32)
    wqr = dt("wqr", [UP, NH * DR], F32)
    wout = dt("wout", [NH * DH, H], F32)
    gw = dt("gw", [H, 16], F32)
    gu16 = dt("gu16", [17, 6, 128, 2048], U16)
    gsc = dt("gsc", [17, 6, 4, 2048], F32)
    d16 = dt("d16", [17, 4, 128, 1536], U16)
    dsc = dt("dsc", [17, 4, 4, 1536], F32)
    cos_a = dt("cos_a", [S_all, 512], F32)
    sin_a = dt("sin_a", [S_all, 512], F32)
    identd = dt("identd", [128, 128], F32)
    ind4d = dt("ind4d", [4, 128], F32)

    out_h = nc.dram_tensor("out", [Tc, H], F32, kind="ExternalOutput").ap()

    di = lambda n, s, d: nc.dram_tensor(n, s, d, kind="Internal").ap()
    xn_sp = di("xn_sp", [NTTA, 128, H], BF16)
    ckv_sp = di("ckv_sp", [NTTA, 128, CD], BF16)
    cq_sp = di("cq_sp", [NTTA, 128, CD], BF16)
    v_sp = di("v_sp", [NTTA, 128, NH * DH], BF16)
    kf_sp = di("kf_sp", [NTT, 128, NH * 128], BF16)
    s_sp = di("s_sp", [NTT, 128, H], BF16)
    gacc = di("gacc", [4, NTT, 128, 512], F32)

    ctx = Ctx(nc)
    pr = Prog(nc, ctx)
    B = Buf
    es = contextlib.ExitStack()

    def sb(name, shape, dtyp):
        return es.enter_context(nc.sbuf_tensor(name, shape, dtyp))

    def ps_(name, shape, dtyp):
        return es.enter_context(nc.psum_tensor(name, shape, dtyp))

    ident = nc.alloc_sbuf_tensor("ident", [128, 128], BF16)
    ind4 = nc.alloc_sbuf_tensor("ind4", [4, 128], BF16)
    gw_sb = nc.alloc_sbuf_tensor("gw_sb", [128, 16 * 16], BF16)
    logit = nc.alloc_sbuf_tensor("logit", [128, NTT * 16], F32)
    cw = nc.alloc_sbuf_tensor("cw", [128, NTT * 16], F32)
    tau = nc.alloc_sbuf_tensor("tau", [128, NTT], F32)
    msb = nc.alloc_sbuf_tensor("msb", [128, NH * DH], BF16)
    neg = nc.alloc_sbuf_tensor("neg", [128, 16], F32)
    lw = nc.alloc_sbuf_tensor("lw", [128, 16], F32)
    sel = nc.alloc_sbuf_tensor("sel", [128, 16], F32)
    ela = nc.alloc_sbuf_tensor("ela", [128, 16], F32)
    mku = nc.alloc_sbuf_tensor("mku", [128, 16], U8)
    st8 = nc.alloc_sbuf_tensor("st8", [128, 8], F32)
    cst = nc.alloc_sbuf_tensor("cst", [128, 2], F32)

    (b_const, b_logit, b_cw, b_tau, b_msb, b_lw, b_sel, b_ela, b_mku,
     b_st8) = (B() for _ in range(10))
    (b_xnsp, b_ckvsp, b_cqsp, b_vsp, b_kfsp, b_ssp, b_gacc) = (B() for _ in range(7))

    def p0(q):
        q.dma("dcst", ident[:], identd)
        q.dma("dcst", ind4[:], ind4d)
        t = q.dma("dcst", gw_sb[:].rearrange("p (a n) -> p a n", n=16),
                  gw.rearrange("(a p) n -> p a n", p=128))
        b_const.owr(t)
        b_const.owp(q.op(q.eng.memset, neg[:], -1e30))
        b_const.owp(q.op(q.eng.memset, cst[:, 0:1], -0.5))
        b_const.owp(q.op(q.eng.memset, cst[:, 1:2], 1e-5))
    pr.add("g", p0)
    pr.flush("p0")

    def x_tile(gtt):
        if gtt < NTT:
            return x_loc[gtt * 128:(gtt + 1) * 128, :]
        return x_oth[(gtt - NTT) * 128:(gtt - NTT + 1) * 128, :]

    def fm_proj_steps(wbuf, b_w, inb, b_in, ob, b_o, ktn, ntile, mm2, b_m2,
                      extra=None):
        for half in range(0, ntile, 2):
            def tstep(q, half=half):
                for n in (half, half + 1):
                    if n >= ntile:
                        break
                    psl = n % 2
                    pairs = [(wbuf[:, (kt * ntile + n) * 128:(kt * ntile + n) * 128 + 128],
                              inb[:, kt * 128:(kt + 1) * 128]) for kt in range(ktn)]
                    t = q.mm_group(mm2[psl][:, 0:128], pairs,
                                   deps=b_in.w + b_m2[psl].wd() + b_w.w)
                    b_m2[psl].owr(t)
                    b_in.ord(t)
            pr.add("t", tstep)

            def vstep(q, half=half):
                for n in (half, half + 1):
                    if n >= ntile:
                        break
                    psl = n % 2
                    tv = q.op(q.eng.tensor_copy, ob[:, n * 128:(n + 1) * 128],
                              mm2[psl][:, 0:128],
                              deps=b_m2[psl].rd() + (b_o.wd() if n == 0 else []))
                    if n == 0:
                        b_o.owr(tv)
                    else:
                        b_o.owp(tv)
                    b_m2[psl].ord(tv)
            pr.add("v", vstep)
        if extra is not None:
            extra()

    def rope_steps(src, b_src, dstv, b_dst, csb, b_cs, ssbt, b_ss, rp1, b_r1,
                   rp2, b_r2):
        def f(q):
            sv = src[:].rearrange("p (h j) -> p h j", j=64)
            cv = csb[:].rearrange("p (h j) -> p h j", j=32)
            sv2 = ssbt[:].rearrange("p (h j) -> p h j", j=32)
            r1v = rp1[:].rearrange("p (h j) -> p h j", j=32)
            r2v = rp2[:].rearrange("p (h j) -> p h j", j=16)
            t0 = q.op(q.eng.tensor_copy, dstv[:, :, 32:64], sv[:, :, 32:64],
                      deps=b_src.w + b_cs.w + b_ss.w + b_dst.w + b_dst.r)
            b_dst.owp(t0)
            t1 = q.op(q.eng.tensor_tensor, r1v, sv[:, :, 0:32], cv, ALU.mult,
                      deps=b_r1.wd())
            b_r1.owr(t1)
            t2 = q.op(q.eng.tensor_tensor, r2v, sv[:, :, 16:32], sv2[:, :, 0:16],
                      ALU.mult, deps=b_r2.wd())
            b_r2.owr(t2)
            t3 = q.op(q.eng.tensor_tensor, dstv[:, :, 0:16], r1v[:, :, 0:16], r2v,
                      ALU.subtract, deps=[t1, t2])
            b_dst.owp(t3)
            b_r2.ord(t3)
            t4 = q.op(q.eng.tensor_tensor, r2v, sv[:, :, 0:16], sv2[:, :, 16:32],
                      ALU.mult, deps=b_r2.wd())
            b_r2.owr(t4)
            t5 = q.op(q.eng.tensor_tensor, dstv[:, :, 16:32], r1v[:, :, 16:32], r2v,
                      ALU.add, deps=[t4])
            b_dst.owp(t5)
            b_src.ord(t5)
            b_r1.ord(t5)
            b_r2.ord(t5)
            b_cs.ord(t5)
            b_ss.ord(t5)
        pr.add("v", f)

    with contextlib.ExitStack() as st_:
        sbp = lambda n, s, d: st_.enter_context(nc.sbuf_tensor(n, s, d))
        psp = lambda n, s, d: st_.enter_context(nc.psum_tensor(n, s, d))
        wA = sbp("wA", [128, 16 * CD], BF16)
        xt = [sbp(f"xt{i}", [128, H], F32) for i in range(2)]
        xnt = [sbp(f"xnt{i}", [128, H], BF16) for i in range(2)]
        sqs = sbp("sqs", [128, H], BF16)
        xnp = [sbp(f"xnp{i}", [128, H], BF16) for i in range(2)]
        ckvb = sbp("ckvb", [128, CD], BF16)
        pips = psp("pips", [128, 512], BF16)
        mm2 = [psp(f"mm2_{i}", [128, 128], F32) for i in range(2)]
        lgps = psp("lgps", [128, 16], F32)
        b_wA = B()
        b_xt = [B(), B()]
        b_xnt = [B(), B()]
        b_sq = B()
        b_xnp = [B(), B()]
        b_ckv = B()
        b_pips = B()
        b_m2 = [B(), B()]
        b_lg = B()

        def a1_w(q):
            b_wA.owr(q.dma("dw", wA[:].rearrange("p (a n) -> p a n", n=CD),
                           wdkv.rearrange("(a p) n -> p a n", p=128)))
        pr.add("g", a1_w)

        def mk_xload(gtt, sl):
            def f(q):
                t = q.dma(f"dx{sl}", xt[sl][:], x_tile(gtt), deps=b_xt[sl].wd())
                b_xt[sl].owr(t)
            return f
        pr.add("sy", mk_xload(0, 0))
        if NTTA > 1:
            pr.add("sy", mk_xload(1, 1))

        for gtt in range(NTTA):
            sl = gtt % 2

            def ln_v1(q, sl=sl):
                t = q.op(q.eng.reduce_sum, st8[:, 0:1], xt[sl][:], axis=AX.X,
                         deps=b_xt[sl].rd() + b_st8.wd())
                b_st8.owr(t)
                b_xt[sl].ord(t)
            pr.add("v", ln_v1)

            def ln_s1(q, sl=sl):
                t = q.op(q.eng.activation, sqs[:], xt[sl][:], AF.Square,
                         accum_out=st8[:, 1:2],
                         deps=b_xt[sl].rd() + b_sq.wd() + b_st8.w + b_st8.r)
                b_sq.owr(t)
                b_st8.owp(t)
                b_xt[sl].ord(t)
            pr.add("s", ln_s1)

            def ln_v2(q):
                t1 = q.op(q.eng.tensor_scalar, st8[:, 2:3], st8[:, 0:1], 1.0 / H,
                          None, ALU.mult, deps=b_st8.w)
                t2 = q.op(q.eng.tensor_scalar, st8[:, 3:4], st8[:, 2:3],
                          st8[:, 2:3], None, ALU.mult, deps=[t1])
                t3 = q.op(q.eng.scalar_tensor_tensor, st8[:, 4:5], st8[:, 1:2],
                          1.0 / H, st8[:, 3:4], ALU.mult, ALU.subtract, deps=[t2])
                b_st8.owp(t3)
            pr.add("v", ln_v2)

            def ln_s2(q):
                t1 = q.op(q.eng.activation, st8[:, 5:6], st8[:, 4:5], AF.Ln,
                          bias=cst[:, 1:2], deps=b_st8.w + b_const.w)
                t2 = q.op(q.eng.activation, st8[:, 6:7], st8[:, 5:6], AF.Exp,
                          scale=cst[:, 0:1], deps=[t1])
                b_st8.owp(t2)
            pr.add("s", ln_s2)

            def ln_v3(q, sl=sl):
                t = q.op(q.eng.tensor_scalar, xnt[sl][:], xt[sl][:], st8[:, 2:3],
                         st8[:, 6:7], ALU.subtract, ALU.mult,
                         deps=b_st8.w + b_xnt[sl].wd() + b_xt[sl].rd())
                b_xnt[sl].owr(t)
                b_xt[sl].ord(t)
                b_st8.ord(t)
            pr.add("v", ln_v3)

            if gtt + 2 < NTTA:
                pr.add("sy", mk_xload(gtt + 2, sl))

            for grp in range(4):
                def pi_t(q, sl=sl, grp=grp):
                    tk = None
                    for j in range(4):
                        tu = grp * 4 + j
                        u, k = tu // 4, tu % 4
                        tk = q.op(q.eng.transpose, pips[:, j * 128:(j + 1) * 128],
                                  xnt[sl][:, (512 * u + k):(512 * u + k + 509):4],
                                  ident[:],
                                  deps=(b_xnt[sl].w + b_pips.wd() + b_const.w)
                                  if j == 0 else (), inc=(1 if j == 3 else 0))
                    b_pips.owr(tk)
                    b_xnt[sl].ord(tk)
                pr.add("t", pi_t)

                def pi_v(q, sl=sl, grp=grp):
                    t = q.op(q.eng.tensor_copy, xnp[sl][:, grp * 512:(grp + 1) * 512],
                             pips[:],
                             deps=b_pips.rd() + (b_xnp[sl].wd() if grp == 0 else []))
                    if grp == 0:
                        b_xnp[sl].owr(t)
                    else:
                        b_xnp[sl].owp(t)
                    b_pips.ord(t)
                pr.add("v", pi_v)

            def xn_spill(q, gtt=gtt, sl=sl):
                t = q.dma(f"sxn{sl}", xn_sp[gtt], xnp[sl][:], deps=b_xnp[sl].w)
                b_xnp[sl].ord(t)
                b_xnsp.owp(t)
            pr.add("sy", xn_spill)

            for half in range(0, 8, 2):
                def ckv_t(q, sl=sl, half=half):
                    for n in (half, half + 1):
                        psl = n % 2
                        pairs = [(wA[:, (kt * 8 + n) * 128:(kt * 8 + n) * 128 + 128],
                                  xnp[sl][:, kt * 128:(kt + 1) * 128])
                                 for kt in range(16)]
                        t = q.mm_group(mm2[psl][:], pairs,
                                       deps=b_xnp[sl].w + b_m2[psl].wd() + b_wA.w)
                        b_m2[psl].owr(t)
                        b_xnp[sl].ord(t)
                pr.add("t", ckv_t)

                def ckv_v(q, half=half):
                    for n in (half, half + 1):
                        psl = n % 2
                        tv = q.op(q.eng.tensor_copy, ckvb[:, n * 128:(n + 1) * 128],
                                  mm2[psl][:],
                                  deps=b_m2[psl].rd() + (b_ckv.wd() if n == 0 else []))
                        if n == 0:
                            b_ckv.owr(tv)
                        else:
                            b_ckv.owp(tv)
                        b_m2[psl].ord(tv)
                pr.add("v", ckv_v)

            if gtt < NTT:
                def lg_t(q, gtt=gtt, sl=sl):
                    pairs = [(xnp[sl][:, kt * 128:(kt + 1) * 128],
                              gw_sb[:, kt * 16:(kt + 1) * 16]) for kt in range(16)]
                    t = q.mm_group(lgps[:], pairs,
                                   deps=b_xnp[sl].w + b_lg.wd() + b_const.w)
                    b_lg.owr(t)
                    b_xnp[sl].ord(t)
                pr.add("t", lg_t)

                def lg_v(q, gtt=gtt):
                    tv = q.op(q.eng.tensor_copy, logit[:, gtt * 16:(gtt + 1) * 16],
                              lgps[:], deps=b_lg.rd())
                    b_logit.owp(tv)
                    b_lg.ord(tv)
                pr.add("v", lg_v)

            def ckv_spill(q, gtt=gtt):
                t = q.dma("scv", ckv_sp[gtt], ckvb[:], deps=b_ckv.w)
                b_ckv.ord(t)
                b_ckvsp.owp(t)
            pr.add("sy", ckv_spill)

        def top4_v1(q):
            for tt in range(NTT):
                t0 = q.op(q.eng.tensor_copy, lw[:], logit[:, tt * 16:(tt + 1) * 16],
                          deps=b_logit.w + b_lw.wd())
                b_lw.owr(t0)
                b_logit.ord(t0)
                for r in range(4):
                    dst = st8[:, 0:1] if r < 3 else tau[:, tt:tt + 1]
                    t1 = q.op(q.eng.reduce_max, dst, lw[:], axis=AX.X,
                              deps=[t0] + (b_st8.wd() if r < 3 else b_tau.wd()))
                    if r < 3:
                        b_st8.owr(t1)
                        t2 = q.op(q.eng.tensor_scalar, mku[:], lw[:], dst, None,
                                  ALU.is_ge, deps=[t1] + b_mku.wd())
                        b_mku.owr(t2)
                        t0 = q.op(q.eng.copy_predicated, lw[:], mku[:], neg[:],
                                  deps=[t2] + b_const.w)
                        b_lw.owr(t0)
                    else:
                        b_tau.owp(t1)
                        b_lw.ord(t1)
        pr.add("v", top4_v1)

        def top4_pair(tt):
            def fs(q, tt=tt):
                t = q.op(q.eng.activation, ela[:], logit[:, tt * 16:(tt + 1) * 16],
                         AF.Exp, deps=b_logit.w + b_ela.wd() + b_tau.w)
                b_ela.owr(t)
            def fv(q, tt=tt):
                t1 = q.op(q.eng.tensor_scalar, sel[:], logit[:, tt * 16:(tt + 1) * 16],
                          tau[:, tt:tt + 1], None, ALU.is_ge,
                          deps=b_tau.w + b_logit.w + b_sel.wd())
                b_sel.owr(t1)
                t2 = q.op(q.eng.tensor_tensor, sel[:], ela[:], sel[:], ALU.mult,
                          deps=[t1] + b_ela.w)
                b_sel.owr(t2)
                b_ela.ord(t2)
                t3 = q.op(q.eng.reduce_sum, st8[:, 0:1], sel[:], axis=AX.X,
                          deps=[t2] + b_st8.wd())
                t4 = q.op(q.eng.reciprocal, st8[:, 1:2], st8[:, 0:1], deps=[t3])
                b_st8.owr(t4)
                t5 = q.op(q.eng.tensor_scalar, cw[:, tt * 16:(tt + 1) * 16], sel[:],
                          st8[:, 1:2], None, ALU.mult, deps=[t4])
                b_cw.owp(t5)
                b_sel.ord(t5)
                b_st8.ord(t5)
            return fs, fv
        for tt in range(NTT):
            fs, fv = top4_pair(tt)
            pr.add("s", fs)
            pr.add("v", fv)
        pr.flush("a1")

    with contextlib.ExitStack() as st_:
        sbp = lambda n, s, d: st_.enter_context(nc.sbuf_tensor(n, s, d))
        psp = lambda n, s, d: st_.enter_context(nc.psum_tensor(n, s, d))
        wA2 = sbp("wA2", [128, 16 * CD], BF16)
        inb = [sbp(f"a2in{i}", [128, H], BF16) for i in range(2)]
        ob = sbp("a2ob", [128, CD], BF16)
        mm2 = [psp(f"a2mm{i}", [128, 128], F32) for i in range(2)]
        b_w, b_in, b_o = B(), [B(), B()], B()
        b_m2 = [B(), B()]

        def a2_w(q):
            b_w.owr(q.dma("dw", wA2[:].rearrange("p (a n) -> p a n", n=CD),
                          wdq.rearrange("(a p) n -> p a n", p=128)))
        pr.add("g", a2_w)

        for gtt in range(NTTA):
            sl = gtt % 2

            def iload(q, gtt=gtt, sl=sl):
                t = q.dma(f"da2{sl}", inb[sl][:], xn_sp[gtt],
                          deps=b_in[sl].wd() + b_xnsp.w)
                b_in[sl].owr(t)
            pr.add("sy", iload)
            fm_proj_steps(wA2, b_w, inb[sl], b_in[sl], ob, b_o, 16, 8, mm2, b_m2)

            def ospill(q, gtt=gtt):
                t = q.dma("scq", cq_sp[gtt], ob[:], deps=b_o.w)
                b_o.ord(t)
                b_cqsp.owp(t)
            pr.add("sy", ospill)
        pr.flush("a2")

    with contextlib.ExitStack() as st_:
        sbp = lambda n, s, d: st_.enter_context(nc.sbuf_tensor(n, s, d))
        psp = lambda n, s, d: st_.enter_context(nc.psum_tensor(n, s, d))
        wA3 = sbp("wA3", [128, 16 * 1024], BF16)
        inb = [sbp(f"a3in{i}", [128, H], BF16) for i in range(2)]
        krt = sbp("krt", [128, 1024], BF16)
        krot = sbp("krot", [128, 1024], BF16)
        csb = sbp("csb", [128, 512], BF16)
        snb = sbp("snb", [128, 512], BF16)
        rp1 = sbp("rp1", [128, 512], BF16)
        rp2 = sbp("rp2", [128, 256], BF16)
        kch = sbp("kch", [128, NH * 128], BF16)
        p5 = [psp(f"a3p5{i}", [128, 512], F32) for i in range(2)]
        tpsb = psp("a3tps", [128, 512], BF16)
        b_w, b_in = B(), [B(), B()]
        b_krt, b_krot, b_cs, b_ss, b_r1, b_r2, b_kch, b_tps = (B() for _ in range(8))
        b_p5 = [B(), B()]

        def a3_w(q):
            b_w.owr(q.dma("dw", wA3[:].rearrange("p (a n) -> p a n", n=1024),
                          wkr.rearrange("(a p) n -> p a n", p=128)))
        pr.add("g", a3_w)

        for gtt in range(NTT):
            sl = gtt % 2

            def iload(q, gtt=gtt, sl=sl):
                t = q.dma(f"da3{sl}", inb[sl][:], xn_sp[gtt],
                          deps=b_in[sl].wd() + b_xnsp.w)
                b_in[sl].owr(t)
            pr.add("sy", iload)

            def csload(q, gtt=gtt):
                q.dma("da3c", csb[:], cos_a[gtt * 128:(gtt + 1) * 128, :],
                      deps=b_cs.wd() + b_ss.wd())
                ts_ = q.dma("da3c", snb[:], sin_a[gtt * 128:(gtt + 1) * 128, :])
                b_cs.owr(ts_)
                b_ss.owr(ts_)
            pr.add("g", csload)

            def a3_t(q, sl=sl):
                for n in range(2):
                    pairs = [(inb[sl][:, kt * 128:(kt + 1) * 128],
                              wA3[:, kt * 1024 + n * 512: kt * 1024 + n * 512 + 512])
                             for kt in range(16)]
                    t = q.mm_group(p5[n][:], pairs,
                                   deps=b_in[sl].w + b_p5[n].wd() + b_w.w)
                    b_p5[n].owr(t)
                    b_in[sl].ord(t)
            pr.add("t", a3_t)

            def a3_v(q):
                for n in range(2):
                    tv = q.op(q.eng.tensor_copy, krt[:, n * 512:(n + 1) * 512],
                              p5[n][:],
                              deps=b_p5[n].rd() + (b_krt.wd() if n == 0 else []))
                    if n == 0:
                        b_krt.owr(tv)
                    else:
                        b_krt.owp(tv)
                    b_p5[n].ord(tv)
            pr.add("v", a3_v)

            krotv = krot[:].rearrange("p (h j) -> p h j", j=64)
            rope_steps(krt, b_krt, krotv, b_krot, csb, b_cs, snb, b_ss,
                       rp1, b_r1, rp2, b_r2)

            for hg in range(4):
                def a3_tr(q, hg=hg):
                    tk = None
                    for j in range(4):
                        h = hg * 4 + j
                        tk = q.op(q.eng.transpose,
                                  tpsb[64:128, j * 128:(j + 1) * 128],
                                  krot[:, h * 64:(h + 1) * 64], ident[:],
                                  tile_position=(0, 64),
                                  deps=(b_krot.w + b_tps.wd() + b_const.w)
                                  if j == 0 else (), inc=(1 if j == 3 else 0))
                    b_tps.owr(tk)
                    b_krot.ord(tk)
                pr.add("t", a3_tr)

                def a3_ev(q, hg=hg):
                    t = q.op(q.eng.tensor_copy,
                             kch[64:128, hg * 512:(hg + 1) * 512], tpsb[64:128, :],
                             deps=b_tps.rd() + (b_kch.wd() if hg == 0 else []))
                    if hg == 0:
                        b_kch.owr(t)
                    else:
                        b_kch.owp(t)
                    b_tps.ord(t)
                pr.add("v", a3_ev)

            def a3_spill(q, gtt=gtt):
                t = q.dma("skf", kf_sp[gtt, 64:128, :], kch[64:128, :],
                          deps=b_kch.w)
                b_kch.ord(t)
                b_kfsp.owp(t)
            pr.add("sy", a3_spill)
        pr.flush("a3")

    with contextlib.ExitStack() as st_:
        sbp = lambda n, s, d: st_.enter_context(nc.sbuf_tensor(n, s, d))
        psp = lambda n, s, d: st_.enter_context(nc.psum_tensor(n, s, d))
        wB1 = sbp("wB1", [128, 8 * 2048], BF16)
        cin = [sbp(f"b1in{i}", [128, CD], BF16) for i in range(2)]
        vtb = sbp("vtb", [128, 2048], BF16)
        p5 = [psp(f"b1p5{i}", [128, 512], F32) for i in range(2)]
        b_w, b_in, b_vt = B(), [B(), B()], B()
        b_p5 = [B(), B()]

        def b1_w(q):
            b_w.owr(q.dma("dw", wB1[:].rearrange("p (a n) -> p a n", n=2048),
                          wuv.rearrange("(a p) n -> p a n", p=128)))
        pr.add("g", b1_w)

        for gtt in range(NTTA):
            sl = gtt % 2

            def iload(q, gtt=gtt, sl=sl):
                t = q.dma(f"db1{sl}", cin[sl][:], ckv_sp[gtt],
                          deps=b_in[sl].wd() + b_ckvsp.w)
                b_in[sl].owr(t)
            pr.add("sy", iload)

            for half in range(0, 4, 2):
                def b1_t(q, sl=sl, half=half):
                    for n in (half, half + 1):
                        psl = n % 2
                        pairs = [(cin[sl][:, kt * 128:(kt + 1) * 128],
                                  wB1[:, kt * 2048 + n * 512: kt * 2048 + n * 512 + 512])
                                 for kt in range(8)]
                        t = q.mm_group(p5[psl][:], pairs,
                                       deps=b_in[sl].w + b_p5[psl].wd() + b_w.w)
                        b_p5[psl].owr(t)
                        b_in[sl].ord(t)
                pr.add("t", b1_t)

                def b1_v(q, half=half):
                    for n in (half, half + 1):
                        psl = n % 2
                        tv = q.op(q.eng.tensor_copy, vtb[:, n * 512:(n + 1) * 512],
                                  p5[psl][:],
                                  deps=b_p5[psl].rd() + (b_vt.wd() if n == 0 else []))
                        if n == 0:
                            b_vt.owr(tv)
                        else:
                            b_vt.owp(tv)
                        b_p5[psl].ord(tv)
                pr.add("v", b1_v)

            def vspill(q, gtt=gtt):
                t = q.dma("sv", v_sp[gtt], vtb[:], deps=b_vt.w)
                b_vt.ord(t)
                b_vsp.owp(t)
            pr.add("sy", vspill)
        pr.flush("b1")

    with contextlib.ExitStack() as st_:
        sbp = lambda n, s, d: st_.enter_context(nc.sbuf_tensor(n, s, d))
        psp = lambda n, s, d: st_.enter_context(nc.psum_tensor(n, s, d))
        wQ = sbp("wQ", [128, 8 * 1024], BF16)
        wR = sbp("wR", [128, 8 * 1024], BF16)
        cin = [sbp(f"b2in{i}", [128, CD], BF16) for i in range(2)]
        vin = [sbp(f"vin{i}", [128, 2048], BF16) for i in range(2)]
        qcf = sbp("qcf", [128, 1024], BF16)
        qrt = sbp("qrt", [128, 1024], BF16)
        qtb = sbp("qtb", [128, 2048], BF16)
        csb = sbp("b2cs", [128, 512], BF16)
        snb = sbp("b2sn", [128, 512], BF16)
        rp1 = sbp("b2r1", [128, 512], BF16)
        rp2 = sbp("b2r2", [128, 256], BF16)
        mm2 = [psp(f"b2mm{i}", [128, 128], F32) for i in range(2)]
        p5 = [psp(f"b2p5{i}", [128, 512], F32) for i in range(2)]
        mps = psp("mps", [128, 2048], F32)
        b_wq, b_wr = B(), B()
        b_in, b_vin = [B(), B()], [B(), B()]
        b_qcf, b_qrt, b_qt, b_cs, b_ss, b_r1, b_r2, b_mps = (B() for _ in range(8))
        b_m2 = [B(), B()]
        b_p5 = [B(), B()]

        def b2_w(q):
            q.dma("dw", wQ[:].rearrange("p (a n) -> p a n", n=1024),
                  wuq.rearrange("(a p) n -> p a n", p=128))
            t = q.dma("dw", wR[:].rearrange("p (a n) -> p a n", n=1024),
                      wqr.rearrange("(a p) n -> p a n", p=128))
            b_wq.owr(t)
            b_wr.owr(t)
        pr.add("g", b2_w)

        for gtt in range(NTTA):
            sl = gtt % 2

            def iload(q, gtt=gtt, sl=sl):
                t = q.dma(f"db2{sl}", cin[sl][:], cq_sp[gtt],
                          deps=b_in[sl].wd() + b_cqsp.w)
                b_in[sl].owr(t)
                tv = q.dma(f"db2v{sl}", vin[sl][:], v_sp[gtt],
                           deps=b_vin[sl].wd() + b_vsp.w)
                b_vin[sl].owr(tv)
            pr.add("sy", iload)

            def csload(q, gtt=gtt):
                q.dma("db2c", csb[:], cos_a[gtt * 128:(gtt + 1) * 128, :],
                      deps=b_cs.wd() + b_ss.wd())
                ts_ = q.dma("db2c", snb[:], sin_a[gtt * 128:(gtt + 1) * 128, :])
                b_cs.owr(ts_)
                b_ss.owr(ts_)
            pr.add("g", csload)

            fm_proj_steps(wQ, b_wq, cin[sl], b_in[sl], qcf, b_qcf, 8, 8, mm2, b_m2)

            def qr_t(q):
                for n in range(2):
                    pairs = [(qcf[:, kt * 128:(kt + 1) * 128],
                              wR[:, kt * 1024 + n * 512: kt * 1024 + n * 512 + 512])
                             for kt in range(8)]
                    t = q.mm_group(p5[n][:], pairs,
                                   deps=b_qcf.w + b_p5[n].wd() + b_wr.w)
                    b_p5[n].owr(t)
                    b_qcf.ord(t)
            pr.add("t", qr_t)

            def qr_v(q):
                for n in range(2):
                    tv = q.op(q.eng.tensor_copy, qrt[:, n * 512:(n + 1) * 512],
                              p5[n][:],
                              deps=b_p5[n].rd() + (b_qrt.wd() if n == 0 else []))
                    if n == 0:
                        b_qrt.owr(tv)
                    else:
                        b_qrt.owp(tv)
                    b_p5[n].ord(tv)
            pr.add("v", qr_v)

            def qc_t(q, sl=sl):
                for n in range(2):
                    pairs = [(cin[sl][:, kt * 128:(kt + 1) * 128],
                              wQ[:, kt * 1024 + n * 512: kt * 1024 + n * 512 + 512])
                             for kt in range(8)]
                    t = q.mm_group(p5[n][:], pairs,
                                   deps=b_in[sl].w + b_p5[n].wd() + b_wq.w)
                    b_p5[n].owr(t)
                    b_in[sl].ord(t)
            pr.add("t", qc_t)

            def qc_v(q):
                for n in range(2):
                    src = p5[n][:].rearrange("p (h j) -> p h j", j=64)
                    dstv = qtb[:].rearrange("p (h j) -> p h j", j=128)[:, n * 8:(n + 1) * 8, 0:64]
                    tv = q.op(q.eng.tensor_copy, dstv, src,
                              deps=b_p5[n].rd() + (b_qt.wd() if n == 0 else []))
                    if n == 0:
                        b_qt.owr(tv)
                    else:
                        b_qt.owp(tv)
                    b_p5[n].ord(tv)
            pr.add("v", qc_v)

            qtv = qtb[:].rearrange("p (h j) -> p h j", j=128)[:, :, 64:128]
            rope_steps(qrt, b_qrt, qtv, b_qt, csb, b_cs, snb, b_ss, rp1, b_r1,
                       rp2, b_r2)

            def b2_m(q, gtt=gtt, sl=sl):
                t = None
                for h in range(NH):
                    t = q.mm_group(mps[:, h * 128:(h + 1) * 128],
                                   [(qtb[:, h * 128:(h + 1) * 128],
                                     vin[sl][:, h * 128:(h + 1) * 128])],
                                   deps=(b_qt.w + b_vin[sl].w +
                                         (b_mps.wd() if gtt == 0 else []))
                                   if h == 0 else (),
                                   start=(gtt == 0 and h % 4 == 0),
                                   stop=(gtt == NTTA - 1 and h % 4 == 3))
                b_mps.owp(t)
                b_qt.ord(t)
                b_vin[sl].ord(t)
            pr.add("t", b2_m)

        def b2_mev(q):
            t = q.op(q.eng.tensor_scalar, msb[:], mps[:], RSC, None, ALU.mult,
                     deps=b_mps.w + b_msb.wd())
            b_msb.owr(t)
            b_mps.ord(t)
        pr.add("v", b2_mev)
        pr.flush("b2")

    with contextlib.ExitStack() as st_:
        sbp = lambda n, s, d: st_.enter_context(nc.sbuf_tensor(n, s, d))
        psp = lambda n, s, d: st_.enter_context(nc.psum_tensor(n, s, d))
        wB3 = sbp("wB3", [128, 8 * 1024], BF16)
        cin = [sbp(f"b3in{i}", [128, CD], BF16) for i in range(2)]
        kch = sbp("b3kch", [128, NH * 128], BF16)
        p5 = [psp(f"b3p5{i}", [128, 512], F32) for i in range(2)]
        b_w, b_in, b_kch = B(), [B(), B()], B()
        b_p5 = [B(), B()]

        def b3_w(q):
            b_w.owr(q.dma("dw", wB3[:].rearrange("p (a n) -> p a n", n=1024),
                          wuk.rearrange("(a p) n -> p a n", p=128)))
        pr.add("g", b3_w)

        for gtt in range(NTT):
            sl = gtt % 2

            def iload(q, gtt=gtt, sl=sl):
                t = q.dma(f"db3{sl}", cin[sl][:], ckv_sp[gtt],
                          deps=b_in[sl].wd() + b_ckvsp.w)
                b_in[sl].owr(t)
            pr.add("sy", iload)

            for hg in range(4):
                def b3_t(q, sl=sl, hg=hg):
                    psl = hg % 2
                    tk = None
                    for j in range(4):
                        h = hg * 4 + j
                        pairs = [(wB3[:, kt * 1024 + h * 64: kt * 1024 + h * 64 + 64],
                                  cin[sl][:, kt * 128:(kt + 1) * 128])
                                 for kt in range(8)]
                        tk = q.mm_group(p5[psl][0:64, j * 128:(j + 1) * 128], pairs,
                                        deps=(b_in[sl].w + b_p5[psl].wd() + b_w.w)
                                        if j == 0 else ())
                    b_p5[psl].owr(tk)
                    b_in[sl].ord(tk)
                pr.add("t", b3_t)

                def b3_v(q, hg=hg):
                    psl = hg % 2
                    t = q.op(q.eng.tensor_copy, kch[0:64, hg * 512:(hg + 1) * 512],
                             p5[psl][0:64, :],
                             deps=b_p5[psl].rd() + (b_kch.wd() if hg == 0 else []))
                    if hg == 0:
                        b_kch.owr(t)
                    else:
                        b_kch.owp(t)
                    b_p5[psl].ord(t)
                pr.add("v", b3_v)

            def b3_spill(q, gtt=gtt):
                t = q.dma("skf2", kf_sp[gtt, 0:64, :], kch[0:64, :], deps=b_kch.w)
                b_kch.ord(t)
                b_kfsp.owp(t)
            pr.add("sy", b3_spill)
        pr.flush("b3")

    with contextlib.ExitStack() as st_:
        sbp = lambda n, s, d: st_.enter_context(nc.sbuf_tensor(n, s, d))
        psp = lambda n, s, d: st_.enter_context(nc.psum_tensor(n, s, d))
        wO = sbp("wO", [128, 16 * 2048], BF16)
        kin = [sbp(f"kin{i}", [128, 2048], BF16) for i in range(2)]
        outf = sbp("outf", [128, 2048], BF16)
        s_st = sbp("s_st", [128, 2048], BF16)
        p5 = [psp(f"p2p5{i}", [128, 512], F32) for i in range(2)]
        b_w, b_kin, b_of, b_sst = B(), [B(), B()], B(), B()
        b_p5 = [B(), B()]

        def p2_w(q):
            b_w.owr(q.dma("dw", wO[:].rearrange("p (a n) -> p a n", n=2048),
                          wout.rearrange("(a p) n -> p a n", p=128)))
        pr.add("g", p2_w)

        for gtt in range(NTT):
            sl = gtt % 2

            def iload(q, gtt=gtt, sl=sl):
                t = q.dma(f"dp2{sl}", kin[sl][:], kf_sp[gtt],
                          deps=b_kin[sl].wd() + b_kfsp.w)
                b_kin[sl].owr(t)
            pr.add("sy", iload)

            for hg in range(4):
                def of_t(q, sl=sl, hg=hg):
                    psl = hg % 2
                    tk = None
                    for j in range(4):
                        h = hg * 4 + j
                        tk = q.mm_group(p5[psl][:, j * 128:(j + 1) * 128],
                                        [(msb[:, h * 128:(h + 1) * 128],
                                          kin[sl][:, h * 128:(h + 1) * 128])],
                                        deps=(b_msb.w + b_kin[sl].w + b_p5[psl].wd())
                                        if j == 0 else ())
                    b_p5[psl].owr(tk)
                    b_kin[sl].ord(tk)
                pr.add("t", of_t)

                def of_v(q, hg=hg):
                    psl = hg % 2
                    t = q.op(q.eng.tensor_copy, outf[:, hg * 512:(hg + 1) * 512],
                             p5[psl][:],
                             deps=b_p5[psl].rd() + (b_of.wd() if hg == 0 else []))
                    if hg == 0:
                        b_of.owr(t)
                    else:
                        b_of.owp(t)
                    b_p5[psl].ord(t)
                pr.add("v", of_v)

            for half in range(0, 4, 2):
                def s_t(q, half=half):
                    for n in (half, half + 1):
                        psl = n % 2
                        pairs = [(outf[:, kt * 128:(kt + 1) * 128],
                                  wO[:, kt * 2048 + n * 512: kt * 2048 + n * 512 + 512])
                                 for kt in range(16)]
                        t = q.mm_group(p5[psl][:], pairs,
                                       deps=b_of.w + b_p5[psl].wd() + b_w.w)
                        b_p5[psl].owr(t)
                        b_of.ord(t)
                pr.add("t", s_t)

                def s_v(q, half=half):
                    for n in (half, half + 1):
                        psl = n % 2
                        tv = q.op(q.eng.tensor_copy, s_st[:, n * 512:(n + 1) * 512],
                                  p5[psl][:],
                                  deps=b_p5[psl].rd() + (b_sst.wd() if n == 0 else []))
                        if n == 0:
                            b_sst.owr(tv)
                        else:
                            b_sst.owp(tv)
                        b_p5[psl].ord(tv)
                pr.add("v", s_v)

            def sspill(q, gtt=gtt):
                t = q.dma("ssp", s_sp[gtt], s_st[:], deps=b_sst.w)
                b_sst.ord(t)
                b_ssp.owp(t)
            pr.add("sy", sspill)
        pr.flush("p2")

    with contextlib.ExitStack() as st_:
        sbp = lambda n, s, d: st_.enter_context(nc.sbuf_tensor(n, s, d))
        psp = lambda n, s, d: st_.enter_context(nc.psum_tensor(n, s, d))
        xn_f = sbp("xn_f", [128, 16 * Tc], BF16)
        act = sbp("act", [128, 12 * Tc], BF16)
        u16b = [sbp(f"u16b{i}", [128, 4 * 512], U16) for i in range(2)]
        wdq_ = [sbp(f"wdq{i}", [128, 16 * 512], BF16) for i in range(2)]
        sclb = sbp("sclb", [128, 4 * 512], BF16)
        gss = [sbp(f"gss{i}", [4, 4 * 512], BF16) for i in range(2)]
        t7b = sbp("t7b", [128, 512], U16)
        mbb = [sbp(f"mbb{i}", [128, 512], U16) for i in range(4)]
        sgb = [sbp(f"sgb{i}", [128, 512], U16) for i in range(4)]
        pcb = [sbp(f"pcb{i}", [128, 512], BF16) for i in range(4)]
        stg = [sbp(f"stg{i}", [128, G4 * 512], F32) for i in range(2)]
        p5 = [psp(f"p3p5{i}", [128, 512], F32) for i in range(2)]
        sclp = psp("sclp", [128, 2048], F32)
        b_xnf, b_act, b_scl = (B() for _ in range(3))
        b_u16 = [B(), B()]
        b_gss = [B(), B()]
        b_wd = [B(), B()]
        b_t7 = B()
        b_mb = [B() for _ in range(4)]
        b_sg = [B() for _ in range(4)]
        b_pc = [B() for _ in range(4)]
        b_stg = [B(), B()]
        b_sclp = B()
        b_p5 = [B(), B()]

        def ffn_xn(q):
            xv = xn_f[:].rearrange("p (a n) -> p a n", n=Tc)
            for tt in range(NTT):
                t = q.dma("dxnf", xv[:, :, tt * 128:(tt + 1) * 128],
                          xn_sp[tt].rearrange("p (a n) -> p a n", n=128),
                          deps=(b_xnf.wd() if tt == 0 else []) + b_xnsp.w)
                b_xnf.owp(t)
        pr.add("sy", ffn_xn)

        order_g = [0, 3, 1, 4, 2, 5]
        actv = act[:].rearrange("p (a n) -> p a n", n=Tc)
        xnv = xn_f[:].rearrange("p (a n) -> p a n", n=Tc)

        def deq_chunks(nu, slot):
            wd = wdq_[slot]
            bwd = b_wd[slot]
            ub_ = u16b[slot]
            bu = b_u16[slot]
            gs_ = gss[slot]
            bgs = b_gss[slot]

            def head():
                def f_sc(q):
                    tk = None
                    for u in range(nu):
                        tk = q.mm_group(sclp[:, u * 512:(u + 1) * 512],
                                        [(ind4[:], gs_[0:4, u * 512:(u + 1) * 512])],
                                        deps=(b_sclp.wd() + bgs.w + b_const.w)
                                        if u == 0 else ())
                    b_sclp.owr(tk)
                    bgs.ord(tk)
                pr.add("t", f_sc)

                def f_scv(q):
                    t = q.op(q.eng.tensor_copy, sclb[:, 0:nu * 512],
                             sclp[:, 0:nu * 512], deps=b_sclp.rd() + b_scl.wd())
                    b_scl.owr(t)
                    b_sclp.ord(t)
                pr.add("v", f_scv)

            def uchunk(u):
                def f_v1(q, u=u):
                    ub = ub_[:, u * 512:(u + 1) * 512]
                    for k in range(4):
                        t1 = q.op(q.eng.tensor_scalar, t7b[:], ub, 4 * k, 7,
                                  ALU.logical_shift_right, ALU.bitwise_and,
                                  deps=bu.w + b_t7.wd())
                        b_t7.owr(t1)
                        bu.ord(t1)
                        t2 = q.op(q.eng.tensor_scalar, mbb[k][:], t7b[:], 64, 16128,
                                  ALU.mult, ALU.add, deps=[t1] + b_mb[k].wd())
                        b_mb[k].owr(t2)
                        b_t7.ord(t2)
                        t3 = q.op(q.eng.tensor_scalar, sgb[k][:], ub, 12 - 4 * k,
                                  32768, ALU.logical_shift_left, ALU.bitwise_and,
                                  deps=b_sg[k].wd())
                        b_sg[k].owr(t3)
                        bu.ord(t3)
                pr.add("v", f_v1)

                def f_s(q, u=u):
                    for k in range(4):
                        t = q.op(q.eng.activation, pcb[k][:],
                                 mbb[k][:].bitcast(BF16), AF.Relu, bias=1.0,
                                 scale=-1.0, deps=b_mb[k].w + b_pc[k].wd())
                        b_pc[k].owr(t)
                        b_mb[k].ord(t)
                pr.add("s", f_s)

                def f_v2(q, u=u):
                    for k in range(4):
                        kt = 4 * u + k
                        wsl = wd[:, kt * 512:(kt + 1) * 512]
                        t1 = q.op(q.eng.tensor_tensor, wsl, mbb[k][:].bitcast(BF16),
                                  pcb[k][:], ALU.subtract,
                                  deps=b_pc[k].w + b_mb[k].w +
                                  (bwd.wd() if (u == 0 and k == 0) else []))
                        t2 = q.op(q.eng.tensor_tensor, wsl.bitcast(U16),
                                  wsl.bitcast(U16), sgb[k][:], ALU.bitwise_or,
                                  deps=[t1] + b_sg[k].w)
                        t3 = q.op(q.eng.tensor_tensor, wsl, wsl,
                                  sclb[:, u * 512:(u + 1) * 512], ALU.mult,
                                  deps=[t2] + b_scl.w)
                        if u == 0 and k == 0:
                            bwd.owr(t3)
                        else:
                            bwd.owp(t3)
                        b_pc[k].ord(t3)
                        b_sg[k].ord(t3)
                        b_mb[k].ord(t3)
                        b_scl.ord(t3)
                pr.add("v", f_v2)

            return [head] + [(lambda u=u: uchunk(u)) for u in range(nu)]

        allblocks = [(e, kind, idx) for e in range(17)
                     for (kind, idx) in ([("gu", nb) for nb in order_g]
                                         + [("dn", hb) for hb in range(4)])]
        NB = len(allblocks)

        def emit_load(bi):
            e, kind, idx = allblocks[bi]
            slot = bi % 2
            ub_, bu = u16b[slot], b_u16[slot]
            gs_, bgs = gss[slot], b_gss[slot]
            if kind == "gu":
                def gu_load(q, e=e, nb=idx):
                    t = q.dma("du", ub_[:], gu16[e, nb], deps=bu.wd())
                    bu.owr(t)

                def gss_load(q, e=e, nb=idx):
                    tg = q.dma("dgs", gs_[:], gsc[e, nb], deps=bgs.wd())
                    bgs.owr(tg)
            else:
                def gu_load(q, e=e, hb=idx):
                    t = q.dma("du", ub_[:, 0:3 * 512], d16[e, hb],
                              deps=bu.wd())
                    bu.owr(t)

                def gss_load(q, e=e, hb=idx):
                    tg = q.dma("dgs", gs_[0:4, 0:3 * 512], dsc[e, hb],
                               deps=bgs.wd())
                    bgs.owr(tg)
            pr.add("sy", gu_load)
            pr.add("g", gss_load)

        def emit_mm(bi, deq):
            e, kind, idx = allblocks[bi]
            slot = bi % 2
            wd = wdq_[slot]
            bwd = b_wd[slot]
            steps = []
            if kind == "gu":
                nb = idx
                is_up = nb >= 3
                jbase = (nb - 3) * 4 if is_up else nb * 4
                for ntl in range(4):
                    j = jbase + ntl
                    for tc0 in range(0, TNC, 2):
                        tcs = [tc for tc in (tc0, tc0 + 1) if tc < TNC]

                        def step(ntl=ntl, tcs=tcs, j=j, is_up=is_up):
                            def gu_t(q):
                                for tc in tcs:
                                    psl = tc % 2
                                    pairs = [(wd[:, kt * 512 + ntl * 128:
                                                 kt * 512 + ntl * 128 + 128],
                                              xnv[:, kt, tc * TW:(tc + 1) * TW])
                                             for kt in range(16)]
                                    t = q.mm_group(p5[psl][:, 0:TW], pairs,
                                                   deps=bwd.w + b_xnf.w + b_p5[psl].wd())
                                    b_p5[psl].owr(t)
                                    bwd.ord(t)
                                    b_xnf.ord(t)
                            pr.add("t", gu_t)

                            if not is_up:
                                def gu_s(q):
                                    for tc in tcs:
                                        psl = tc % 2
                                        t = q.op(q.eng.activation,
                                                 actv[:, j, tc * TW:(tc + 1) * TW],
                                                 p5[psl][:, 0:TW], AF.Silu,
                                                 deps=b_p5[psl].rd() + b_act.w + b_act.r)
                                        b_act.owp(t)
                                        b_p5[psl].ord(t)
                                pr.add("s", gu_s)
                            else:
                                def gu_v(q):
                                    for tc in tcs:
                                        psl = tc % 2
                                        asl = actv[:, j, tc * TW:(tc + 1) * TW]
                                        t = q.op(q.eng.tensor_tensor, asl, asl,
                                                 p5[psl][:, 0:TW], ALU.mult,
                                                 deps=b_p5[psl].rd() + b_act.w)
                                        b_act.owp(t)
                                        b_p5[psl].ord(t)
                                pr.add("v", gu_v)
                        steps.append(step)
            else:
                hb = idx
                for tg in range(NTT // G4):
                    def step(tg=tg, hb=hb, e=e):
                        tis = list(range(G4))

                        def dn_t(q):
                            for ti in tis:
                                tt = tg * G4 + ti
                                psl = ti % 2
                                pairs = [(actv[:, kt, tt * 128:(tt + 1) * 128],
                                          wd[:, kt * 512:(kt + 1) * 512])
                                         for kt in range(12)]
                                t = q.mm_group(p5[psl][:], pairs,
                                               deps=b_act.w + bwd.w + b_p5[psl].wd())
                                b_p5[psl].owr(t)
                                bwd.ord(t)
                                b_act.ord(t)
                        pr.add("t", dn_t)

                        def dn_v(q):
                            ssl = tg % 2
                            for ti in tis:
                                tt = tg * G4 + ti
                                psl = ti % 2
                                dst = stg[ssl][:, ti * 512:(ti + 1) * 512]
                                if e < 16:
                                    t = q.op(q.eng.tensor_scalar, dst, p5[psl][:],
                                             cw[:, tt * 16 + e:tt * 16 + e + 1], None,
                                             ALU.mult,
                                             deps=b_p5[psl].rd() + (b_stg[ssl].wd() if ti == 0 else []) + b_cw.w)
                                else:
                                    t = q.op(q.eng.tensor_copy, dst, p5[psl][:],
                                             deps=b_p5[psl].rd() + (b_stg[ssl].wd() if ti == 0 else []))
                                b_stg[ssl].owr(t) if ti == 0 else b_stg[ssl].owp(t)
                                b_p5[psl].ord(t)
                        pr.add("v", dn_v)

                        def dn_acc(q):
                            ssl = tg % 2
                            kw = {} if e == 0 else {"accum_op": ALU.add}
                            gate = [Tok("ga", 16 * NACC * e)] if e > 0 else []
                            t = q.dma("ga",
                                      gacc[hb, tg * G4:(tg + 1) * G4]
                                      .rearrange("a p n -> p a n"),
                                      stg[ssl][:].rearrange("p (a n) -> p a n", n=512),
                                      deps=b_stg[ssl].w + gate, **kw)
                            b_stg[ssl].ord(t)
                            b_gacc.owp(t)
                        pr.add("g", dn_acc)
                    steps.append(step)

            if deq:
                deq[0]()
                nch = len(deq) - 1
                done = 0
                for i, st in enumerate(steps):
                    st()
                    want = min(nch, ((i + 2) * nch) // len(steps))
                    while done < want:
                        deq[1 + done]()
                        done += 1
                while done < nch:
                    deq[1 + done]()
                    done += 1
            else:
                for st in steps:
                    st()

        emit_load(0)
        for c in deq_chunks(4 if allblocks[0][1] == "gu" else 3, 0):
            c()
        for bi in range(NB):
            deq = None
            if bi + 1 < NB:
                emit_load(bi + 1)
                deq = deq_chunks(4 if allblocks[bi + 1][1] == "gu" else 3,
                                 (bi + 1) % 2)
            emit_mm(bi, deq)
        pr.flush("p3")
        t_ga = Tok("ga", ctx.cnt.get("ga", 0))

    with contextlib.ExitStack() as st_:
        sbp = lambda n, s, d: st_.enter_context(nc.sbuf_tensor(n, s, d))
        gac = sbp("gac", [128, H], F32)
        gsl = sbp("gsl", [128, H], BF16)
        s_in = sbp("s_in", [128, H], BF16)
        x_in = sbp("x_in", [128, H], F32)
        sgf = sbp("sgf", [128, H], F32)
        b_gac, b_gsl, b_sin, b_xin, b_sgf = (B() for _ in range(5))

        for tt in range(NTT):
            def p4_l(q, tt=tt):
                gv = gac[:].rearrange("p (a n) -> p a n", n=512)
                t1 = q.dma("dp4g", gv, gacc[:, tt].rearrange("a p n -> p a n"),
                           deps=b_gac.wd() + [t_ga])
                b_gac.owr(t1)
                t2 = q.dma("dp4s", s_in[:], s_sp[tt], deps=b_sin.wd() + b_ssp.w)
                b_sin.owr(t2)
                t3 = q.dma("dp4x", x_in[:], x_loc[tt * 128:(tt + 1) * 128, :],
                           deps=b_xin.wd())
                b_xin.owr(t3)
            pr.add("sy", p4_l)

            def p4_s(q):
                t = q.op(q.eng.activation, gsl[:], gac[:], AF.Silu,
                         deps=b_gac.w + b_gsl.wd())
                b_gsl.owr(t)
                b_gac.ord(t)
            pr.add("s", p4_s)

            def p4_v(q):
                t1 = q.op(q.eng.tensor_tensor, sgf[:], s_in[:], gsl[:], ALU.mult,
                          deps=b_gsl.w + b_sin.w + b_sgf.wd())
                b_sin.ord(t1)
                b_gsl.ord(t1)
                t2 = q.op(q.eng.tensor_tensor, sgf[:], x_in[:], sgf[:], ALU.add,
                          deps=[t1] + b_xin.w)
                b_sgf.owr(t2)
                b_xin.ord(t2)
            pr.add("v", p4_v)

            def p4_o(q, tt=tt):
                t = q.dma("dout", out_h[tt * 128:(tt + 1) * 128, :], sgf[:],
                          deps=b_sgf.w)
                b_sgf.ord(t)
            pr.add("sy", p4_o)

        def fin(q):
            q.wait(Tok("dout", ctx.cnt["dout"]))
        pr.add("sy", fin)
        pr.flush("p4")

    return nc


def _mk_pi(n):
    r = np.arange(n)
    tile, p = r // 128, r % 128
    u, k = tile // 4, tile % 4
    return 512 * u + 4 * p + k


def _pack_u16(packed):
    k8, n = packed.shape
    return np.ascontiguousarray(
        packed.view(np.uint16).reshape(k8, n, 2).transpose(0, 2, 1).reshape(k8 * 2, n))


def _resc(s, nu):
    g, n = s.shape
    return np.ascontiguousarray(
        s.reshape(nu, 4, n).transpose(1, 0, 2).reshape(4, nu * n))


def prep_inputs(inputs):
    x = np.asarray(inputs["L0_attn_x"], dtype=np.float32)
    B_, S, H_ = x.shape
    PIH = _mk_pi(H)
    PII = _mk_pi(I2)
    colsI = np.concatenate([PII, I2 + PII])

    wdkv = np.ascontiguousarray(np.asarray(inputs["L0_attn_W_dkv"])[PIH])
    wdq = np.ascontiguousarray(np.asarray(inputs["L0_attn_W_dq"])[PIH])
    wkr = np.ascontiguousarray(np.asarray(inputs["L0_attn_W_kr"])[PIH])
    wuk = np.ascontiguousarray(np.asarray(inputs["L0_attn_W_uk"]))
    wuv = np.ascontiguousarray(np.asarray(inputs["L0_attn_W_uv"]))
    wuq = np.ascontiguousarray(np.asarray(inputs["L0_attn_W_uq"]))
    wqr = np.ascontiguousarray(np.asarray(inputs["L0_attn_W_qr"]))
    wout = np.ascontiguousarray(np.asarray(inputs["L0_attn_W_out"]))
    gwT = np.ascontiguousarray(np.asarray(inputs["L0_ffn_gate_w"]).T[PIH])

    gup = np.asarray(inputs["L0_ffn_gate_up_packed"])
    gus = np.asarray(inputs["L0_ffn_gate_up_scales"])
    dnp = np.asarray(inputs["L0_ffn_down_packed"])
    dns = np.asarray(inputs["L0_ffn_down_scales"])
    sgup = np.asarray(inputs["L0_ffn_shared_gate_up_packed"])
    sgus = np.asarray(inputs["L0_ffn_shared_gate_up_scales"])
    sdnp = np.asarray(inputs["L0_ffn_shared_down_packed"])
    sdns = np.asarray(inputs["L0_ffn_shared_down_scales"])

    gu16 = np.stack([_pack_u16(gup[e])[:, colsI] for e in range(16)]
                    + [_pack_u16(sgup)[:, colsI]])
    gu16 = np.ascontiguousarray(
        gu16.reshape(17, 4, 128, 6, 512).transpose(0, 3, 2, 1, 4)
        .reshape(17, 6, 128, 2048))
    gscv = np.stack([_resc(gus[e][:, colsI], 4) for e in range(16)]
                    + [_resc(sgus[:, colsI], 4)]).astype(np.float32)
    gscv = np.ascontiguousarray(
        gscv.reshape(17, 4, 4, 6, 512).transpose(0, 3, 1, 2, 4)
        .reshape(17, 6, 4, 2048))
    d16 = np.stack([_pack_u16(dnp[e]) for e in range(16)] + [_pack_u16(sdnp)])
    d16 = np.ascontiguousarray(
        d16.reshape(17, 3, 128, 4, 512).transpose(0, 3, 2, 1, 4)
        .reshape(17, 4, 128, 1536))
    dscv = np.stack([_resc(dns[e], 3) for e in range(16)]
                    + [_resc(sdns, 3)]).astype(np.float32)
    dscv = np.ascontiguousarray(
        dscv.reshape(17, 4, 3, 4, 512).transpose(0, 3, 1, 2, 4)
        .reshape(17, 4, 4, 1536))

    half_dim = DR // 2
    inv = (1.0 / 10000.0 ** (np.arange(0, half_dim, 2, dtype=np.float32)
                             / np.float32(half_dim))).astype(np.float32)
    tpos = (np.arange(S, dtype=np.float32) / np.float32(40.0))
    fr = np.outer(tpos, inv).astype(np.float32)
    rot = np.concatenate([fr, fr], axis=1)
    cosT = np.tile(np.cos(rot), (1, NH)).astype(np.float32)
    sinT = np.tile(np.sin(rot), (1, NH)).astype(np.float32)

    identd = np.eye(128, dtype=np.float32)
    ind4d = (np.arange(128)[None, :] // 32 == np.arange(4)[:, None]
             ).astype(np.float32)

    Tc = S // 2
    shared = dict(wdkv=wdkv, wdq=wdq, wkr=wkr, wuk=wuk, wuv=wuv, wuq=wuq,
                  wqr=wqr, wout=wout, gw=gwT, gu16=gu16, gsc=gscv, d16=d16,
                  dsc=dscv, identd=identd, ind4d=ind4d)
    in_maps = []
    for c in range(8):
        b, hf = c // 2, c % 2
        oth = 1 - hf
        m = dict(shared)
        m["x_loc"] = np.ascontiguousarray(x[b, hf * Tc:(hf + 1) * Tc])
        m["x_oth"] = np.ascontiguousarray(x[b, oth * Tc:(oth + 1) * Tc])
        m["cos_a"] = np.ascontiguousarray(np.concatenate(
            [cosT[hf * Tc:(hf + 1) * Tc], cosT[oth * Tc:(oth + 1) * Tc]]))
        m["sin_a"] = np.ascontiguousarray(np.concatenate(
            [sinT[hf * Tc:(hf + 1) * Tc], sinT[oth * Tc:(oth + 1) * Tc]]))
        in_maps.append(m)

    return in_maps, S, (B_, H_)


def kernel(**inputs):
    in_maps, S, (B_, H_) = prep_inputs(inputs)
    Tc = S // 2
    nc = build_nc(S)
    res = run_bass_kernel_spmd(nc, in_maps, core_ids=list(range(8)))
    out = np.empty((B_, S, H_), dtype=np.float32)
    for c in range(8):
        b, hf = c // 2, c % 2
        out[b, hf * Tc:(hf + 1) * Tc, :] = res.results[c]["out"]
    return out


# version 11
# speedup vs baseline: 1.2817x; 1.0288x over previous
import contextlib
import numpy as np
import concourse.bass as bass
import concourse.mybir as mybir
from concourse.bass_utils import run_bass_kernel_spmd

F32 = mybir.dt.float32
BF16 = mybir.dt.bfloat16
U16 = mybir.dt.uint16
U8 = mybir.dt.uint8
AF = mybir.ActivationFunctionType
ALU = mybir.AluOpType
AX = mybir.AxisListType

H, CD, UP = 2048, 1024, 1024
NH, DH, DR = 16, 128, 64
I2 = 1536
RSC = 0.08838834764831845
ENG = {"v": "vector", "s": "scalar", "t": "tensor", "g": "gpsimd", "sy": "sync"}


class Tok:
    __slots__ = ("key", "val")

    def __init__(self, key, val):
        self.key = key
        self.val = val


class Ctx:
    def __init__(self, nc):
        self.nc = nc
        self.sems = {}
        self.cnt = {}
        self.hwm = {}
        self.guards = []

    def sem(self, key):
        if key not in self.sems:
            g = self.nc.semaphore(f"S_{key}")
            self.sems[key] = g.__enter__()
            self.guards.append(g)
            self.cnt[key] = 0
        return self.sems[key]


class Q:
    def __init__(self, ctx, name, eng):
        self.ctx = ctx
        self.name = name
        self.eng = eng
        ctx.sem(name)

    def wait(self, *deps):
        best = {}
        for d in deps:
            if d is None:
                continue
            for dd in (d if isinstance(d, (list, tuple)) else [d]):
                if dd is not None and (dd.key not in best or best[dd.key] < dd.val):
                    best[dd.key] = dd.val
        for key, val in best.items():
            hk = (self.name, key)
            if self.ctx.hwm.get(hk, 0) >= val:
                continue
            self.eng.wait_ge(self.ctx.sem(key), val)
            self.ctx.hwm[hk] = val

    def op(self, fn, *args, deps=(), inc=1, **kwargs):
        self.wait(*deps)
        inst = fn(*args, **kwargs)
        if inc:
            self.ctx.cnt[self.name] += inc
            inst.then_inc(self.ctx.sem(self.name), inc)
        return Tok(self.name, self.ctx.cnt[self.name])

    def dma(self, key, out, in_, deps=(), **kwargs):
        self.wait(*deps)
        inst = self.eng.dma_start(out, in_, **kwargs)
        self.ctx.sem(key)
        self.ctx.cnt[key] += 16
        inst.then_inc(self.ctx.sem(key), 16)
        return Tok(key, self.ctx.cnt[key])

    def mm_group(self, out_ap, pairs, deps=(), tile_position=None, start=True,
                 stop=True):
        kw = {} if tile_position is None else {"tile_position": tile_position}
        n = len(pairs)
        inst = None
        for i, (l, r) in enumerate(pairs):
            if i == 0:
                self.wait(*deps)
            inst = self.eng.matmul(out_ap, l, r, start=(start and i == 0),
                                   stop=(stop and i == n - 1), **kw)
        self.ctx.cnt[self.name] += 1
        inst.then_inc(self.ctx.sem(self.name), 1)
        return Tok(self.name, self.ctx.cnt[self.name])


class Buf:
    __slots__ = ("w", "r")

    def __init__(self):
        self.w = []
        self.r = []

    def rd(self):
        return list(self.w)

    def ord(self, tok):
        self.r.append(tok)

    def wd(self):
        return self.w + self.r

    def owr(self, tok):
        self.w = [tok]
        self.r = []

    def owp(self, tok):
        self.w.append(tok)


class Prog:
    def __init__(self, nc, ctx):
        self.nc = nc
        self.ctx = ctx
        self.steps = []

    def add(self, eng, fn):
        self.steps.append((eng, fn))

    def drain_dma(self):
        def mk(q):
            q.wait(*[Tok(k, v) for k, v in list(self.ctx.cnt.items())
                     if k not in ("v", "s", "t", "g", "sy") and v > 0])
        self.add("sy", mk)
        self.add("g", mk)

    def flush(self, name):
        self.drain_dma()
        steps, self.steps = self.steps, None
        with self.nc.Block(name) as blk:
            i = 0
            while i < len(steps):
                eng = steps[i][0]
                fns = [steps[i][1]]
                while i + 1 < len(steps) and steps[i + 1][0] == eng:
                    i += 1
                    fns.append(steps[i][1])

                def sec(engine, fns=fns, eng=eng):
                    q = Q(self.ctx, eng, engine)
                    for f in fns:
                        f(q)
                getattr(blk, ENG[eng])(sec)
                i += 1
        self.steps = []


def build_nc(S_all):
    Tc = S_all // 2
    NTT = Tc // 128
    NTTA = S_all // 128
    TW = min(512, Tc)
    TNC = Tc // TW
    G4 = min(2, NTT)
    NACC = (NTT // G4) * 4

    nc = bass.Bass("TRN2", target_bir_lowering=False)
    dt = lambda n, s, d: nc.dram_tensor(n, s, d, kind="ExternalInput").ap()
    x_loc = dt("x_loc", [Tc, H], F32)
    x_oth = dt("x_oth", [Tc, H], F32)
    wdkv = dt("wdkv", [H, CD], F32)
    wdq = dt("wdq", [H, CD], F32)
    wkr = dt("wkr", [H, NH * DR], F32)
    wuk = dt("wuk", [CD, UP], F32)
    wuv = dt("wuv", [CD, NH * DH], F32)
    wuq = dt("wuq", [CD, UP], F32)
    wqr = dt("wqr", [UP, NH * DR], F32)
    wout = dt("wout", [NH * DH, H], F32)
    gw = dt("gw", [H, 16], F32)
    gu16 = dt("gu16", [17, 6, 128, 2048], U16)
    gsc = dt("gsc", [17, 6, 4, 2048], F32)
    d16 = dt("d16", [17, 4, 128, 1536], U16)
    dsc = dt("dsc", [17, 4, 4, 1536], F32)
    cos_a = dt("cos_a", [S_all, 512], F32)
    sin_a = dt("sin_a", [S_all, 512], F32)
    identd = dt("identd", [128, 128], F32)
    ind4d = dt("ind4d", [4, 128], F32)

    out_h = nc.dram_tensor("out", [Tc, H], F32, kind="ExternalOutput").ap()

    di = lambda n, s, d: nc.dram_tensor(n, s, d, kind="Internal").ap()
    xn_sp = di("xn_sp", [NTTA, 128, H], BF16)
    ckv_sp = di("ckv_sp", [NTTA, 128, CD], BF16)
    cq_sp = di("cq_sp", [NTTA, 128, CD], BF16)
    v_sp = di("v_sp", [NTTA, 128, NH * DH], BF16)
    kf_sp = di("kf_sp", [NTT, 128, NH * 128], BF16)
    s_sp = di("s_sp", [NTT, 128, H], BF16)
    gacc = di("gacc", [4, NTT, 128, 512], F32)

    ctx = Ctx(nc)
    pr = Prog(nc, ctx)
    B = Buf
    es = contextlib.ExitStack()

    def sb(name, shape, dtyp):
        return es.enter_context(nc.sbuf_tensor(name, shape, dtyp))

    def ps_(name, shape, dtyp):
        return es.enter_context(nc.psum_tensor(name, shape, dtyp))

    ident = nc.alloc_sbuf_tensor("ident", [128, 128], BF16)
    ind4 = nc.alloc_sbuf_tensor("ind4", [4, 128], BF16)
    gw_sb = nc.alloc_sbuf_tensor("gw_sb", [128, 16 * 16], BF16)
    logit = nc.alloc_sbuf_tensor("logit", [128, NTT * 16], F32)
    cw = nc.alloc_sbuf_tensor("cw", [128, NTT * 16], F32)
    tau = nc.alloc_sbuf_tensor("tau", [128, NTT], F32)
    msb = nc.alloc_sbuf_tensor("msb", [128, NH * DH], BF16)
    neg = nc.alloc_sbuf_tensor("neg", [128, 16], F32)
    lw = nc.alloc_sbuf_tensor("lw", [128, 16], F32)
    sel = nc.alloc_sbuf_tensor("sel", [128, 16], F32)
    ela = nc.alloc_sbuf_tensor("ela", [128, 16], F32)
    mku = nc.alloc_sbuf_tensor("mku", [128, 16], U8)
    st8 = nc.alloc_sbuf_tensor("st8", [128, 8], F32)
    cst = nc.alloc_sbuf_tensor("cst", [128, 2], F32)

    (b_const, b_logit, b_cw, b_tau, b_msb, b_lw, b_sel, b_ela, b_mku,
     b_st8) = (B() for _ in range(10))
    (b_xnsp, b_ckvsp, b_cqsp, b_vsp, b_kfsp, b_ssp, b_gacc) = (B() for _ in range(7))

    def p0(q):
        q.dma("dcst", ident[:], identd)
        q.dma("dcst", ind4[:], ind4d)
        t = q.dma("dcst", gw_sb[:].rearrange("p (a n) -> p a n", n=16),
                  gw.rearrange("(a p) n -> p a n", p=128))
        b_const.owr(t)
        b_const.owp(q.op(q.eng.memset, neg[:], -1e30))
        b_const.owp(q.op(q.eng.memset, cst[:, 0:1], -0.5))
        b_const.owp(q.op(q.eng.memset, cst[:, 1:2], 1e-5))
    pr.add("g", p0)
    pr.flush("p0")

    def x_tile(gtt):
        if gtt < NTT:
            return x_loc[gtt * 128:(gtt + 1) * 128, :]
        return x_oth[(gtt - NTT) * 128:(gtt - NTT + 1) * 128, :]

    def fm_proj_steps(wbuf, b_w, inb, b_in, ob, b_o, ktn, ntile, mm2, b_m2,
                      extra=None):
        for half in range(0, ntile, 2):
            def tstep(q, half=half):
                for n in (half, half + 1):
                    if n >= ntile:
                        break
                    psl = n % 2
                    pairs = [(wbuf[:, (kt * ntile + n) * 128:(kt * ntile + n) * 128 + 128],
                              inb[:, kt * 128:(kt + 1) * 128]) for kt in range(ktn)]
                    t = q.mm_group(mm2[psl][:, 0:128], pairs,
                                   deps=b_in.w + b_m2[psl].wd() + b_w.w)
                    b_m2[psl].owr(t)
                    b_in.ord(t)
            pr.add("t", tstep)

            def vstep(q, half=half):
                for n in (half, half + 1):
                    if n >= ntile:
                        break
                    psl = n % 2
                    tv = q.op(q.eng.tensor_copy, ob[:, n * 128:(n + 1) * 128],
                              mm2[psl][:, 0:128],
                              deps=b_m2[psl].rd() + (b_o.wd() if n == 0 else []))
                    if n == 0:
                        b_o.owr(tv)
                    else:
                        b_o.owp(tv)
                    b_m2[psl].ord(tv)
            pr.add("v", vstep)
        if extra is not None:
            extra()

    def rope_steps(src, b_src, dstv, b_dst, csb, b_cs, ssbt, b_ss, rp1, b_r1,
                   rp2, b_r2):
        def f(q):
            sv = src[:].rearrange("p (h j) -> p h j", j=64)
            cv = csb[:].rearrange("p (h j) -> p h j", j=32)
            sv2 = ssbt[:].rearrange("p (h j) -> p h j", j=32)
            r1v = rp1[:].rearrange("p (h j) -> p h j", j=32)
            r2v = rp2[:].rearrange("p (h j) -> p h j", j=16)
            t0 = q.op(q.eng.tensor_copy, dstv[:, :, 32:64], sv[:, :, 32:64],
                      deps=b_src.w + b_cs.w + b_ss.w + b_dst.w + b_dst.r)
            b_dst.owp(t0)
            t1 = q.op(q.eng.tensor_tensor, r1v, sv[:, :, 0:32], cv, ALU.mult,
                      deps=b_r1.wd())
            b_r1.owr(t1)
            t2 = q.op(q.eng.tensor_tensor, r2v, sv[:, :, 16:32], sv2[:, :, 0:16],
                      ALU.mult, deps=b_r2.wd())
            b_r2.owr(t2)
            t3 = q.op(q.eng.tensor_tensor, dstv[:, :, 0:16], r1v[:, :, 0:16], r2v,
                      ALU.subtract, deps=[t1, t2])
            b_dst.owp(t3)
            b_r2.ord(t3)
            t4 = q.op(q.eng.tensor_tensor, r2v, sv[:, :, 0:16], sv2[:, :, 16:32],
                      ALU.mult, deps=b_r2.wd())
            b_r2.owr(t4)
            t5 = q.op(q.eng.tensor_tensor, dstv[:, :, 16:32], r1v[:, :, 16:32], r2v,
                      ALU.add, deps=[t4])
            b_dst.owp(t5)
            b_src.ord(t5)
            b_r1.ord(t5)
            b_r2.ord(t5)
            b_cs.ord(t5)
            b_ss.ord(t5)
        pr.add("v", f)

    with contextlib.ExitStack() as st_:
        sbp = lambda n, s, d: st_.enter_context(nc.sbuf_tensor(n, s, d))
        psp = lambda n, s, d: st_.enter_context(nc.psum_tensor(n, s, d))
        wA = sbp("wA", [128, 16 * CD], BF16)
        xt = [sbp(f"xt{i}", [128, H], F32) for i in range(2)]
        xnt = [sbp(f"xnt{i}", [128, H], BF16) for i in range(2)]
        sqs = sbp("sqs", [128, H], BF16)
        xnp = [sbp(f"xnp{i}", [128, H], BF16) for i in range(2)]
        ckvb = sbp("ckvb", [128, CD], BF16)
        pips = psp("pips", [128, 512], BF16)
        mm2 = [psp(f"mm2_{i}", [128, 128], F32) for i in range(2)]
        lgps = psp("lgps", [128, 16], F32)
        b_wA = B()
        b_xt = [B(), B()]
        b_xnt = [B(), B()]
        b_sq = B()
        b_xnp = [B(), B()]
        b_ckv = B()
        b_pips = B()
        b_m2 = [B(), B()]
        b_lg = B()

        def a1_w(q):
            b_wA.owr(q.dma("dw", wA[:].rearrange("p (a n) -> p a n", n=CD),
                           wdkv.rearrange("(a p) n -> p a n", p=128)))
        pr.add("g", a1_w)

        def mk_xload(gtt, sl):
            def f(q):
                t = q.dma(f"dx{sl}", xt[sl][:], x_tile(gtt), deps=b_xt[sl].wd())
                b_xt[sl].owr(t)
            return f
        pr.add("sy", mk_xload(0, 0))
        if NTTA > 1:
            pr.add("sy", mk_xload(1, 1))

        for gtt in range(NTTA):
            sl = gtt % 2

            def ln_v1(q, sl=sl):
                t = q.op(q.eng.reduce_sum, st8[:, 0:1], xt[sl][:], axis=AX.X,
                         deps=b_xt[sl].rd() + b_st8.wd())
                b_st8.owr(t)
                b_xt[sl].ord(t)
            pr.add("v", ln_v1)

            def ln_s1(q, sl=sl):
                t = q.op(q.eng.activation, sqs[:], xt[sl][:], AF.Square,
                         accum_out=st8[:, 1:2],
                         deps=b_xt[sl].rd() + b_sq.wd() + b_st8.w + b_st8.r)
                b_sq.owr(t)
                b_st8.owp(t)
                b_xt[sl].ord(t)
            pr.add("s", ln_s1)

            def ln_v2(q):
                t1 = q.op(q.eng.tensor_scalar, st8[:, 2:3], st8[:, 0:1], 1.0 / H,
                          None, ALU.mult, deps=b_st8.w)
                t2 = q.op(q.eng.tensor_scalar, st8[:, 3:4], st8[:, 2:3],
                          st8[:, 2:3], None, ALU.mult, deps=[t1])
                t3 = q.op(q.eng.scalar_tensor_tensor, st8[:, 4:5], st8[:, 1:2],
                          1.0 / H, st8[:, 3:4], ALU.mult, ALU.subtract, deps=[t2])
                b_st8.owp(t3)
            pr.add("v", ln_v2)

            def ln_s2(q):
                t1 = q.op(q.eng.activation, st8[:, 5:6], st8[:, 4:5], AF.Ln,
                          bias=cst[:, 1:2], deps=b_st8.w + b_const.w)
                t2 = q.op(q.eng.activation, st8[:, 6:7], st8[:, 5:6], AF.Exp,
                          scale=cst[:, 0:1], deps=[t1])
                b_st8.owp(t2)
            pr.add("s", ln_s2)

            def ln_v3(q, sl=sl):
                t = q.op(q.eng.tensor_scalar, xnt[sl][:], xt[sl][:], st8[:, 2:3],
                         st8[:, 6:7], ALU.subtract, ALU.mult,
                         deps=b_st8.w + b_xnt[sl].wd() + b_xt[sl].rd())
                b_xnt[sl].owr(t)
                b_xt[sl].ord(t)
                b_st8.ord(t)
            pr.add("v", ln_v3)

            if gtt + 2 < NTTA:
                pr.add("sy", mk_xload(gtt + 2, sl))

            for grp in range(4):
                def pi_t(q, sl=sl, grp=grp):
                    tk = None
                    for j in range(4):
                        tu = grp * 4 + j
                        u, k = tu // 4, tu % 4
                        tk = q.op(q.eng.transpose, pips[:, j * 128:(j + 1) * 128],
                                  xnt[sl][:, (512 * u + k):(512 * u + k + 509):4],
                                  ident[:],
                                  deps=(b_xnt[sl].w + b_pips.wd() + b_const.w)
                                  if j == 0 else (), inc=(1 if j == 3 else 0))
                    b_pips.owr(tk)
                    b_xnt[sl].ord(tk)
                pr.add("t", pi_t)

                def pi_v(q, sl=sl, grp=grp):
                    t = q.op(q.eng.tensor_copy, xnp[sl][:, grp * 512:(grp + 1) * 512],
                             pips[:],
                             deps=b_pips.rd() + (b_xnp[sl].wd() if grp == 0 else []))
                    if grp == 0:
                        b_xnp[sl].owr(t)
                    else:
                        b_xnp[sl].owp(t)
                    b_pips.ord(t)
                pr.add("v", pi_v)

            def xn_spill(q, gtt=gtt, sl=sl):
                t = q.dma(f"sxn{sl}", xn_sp[gtt], xnp[sl][:], deps=b_xnp[sl].w)
                b_xnp[sl].ord(t)
                b_xnsp.owp(t)
            pr.add("sy", xn_spill)

            for half in range(0, 8, 2):
                def ckv_t(q, sl=sl, half=half):
                    for n in (half, half + 1):
                        psl = n % 2
                        pairs = [(wA[:, (kt * 8 + n) * 128:(kt * 8 + n) * 128 + 128],
                                  xnp[sl][:, kt * 128:(kt + 1) * 128])
                                 for kt in range(16)]
                        t = q.mm_group(mm2[psl][:], pairs,
                                       deps=b_xnp[sl].w + b_m2[psl].wd() + b_wA.w)
                        b_m2[psl].owr(t)
                        b_xnp[sl].ord(t)
                pr.add("t", ckv_t)

                def ckv_v(q, half=half):
                    for n in (half, half + 1):
                        psl = n % 2
                        tv = q.op(q.eng.tensor_copy, ckvb[:, n * 128:(n + 1) * 128],
                                  mm2[psl][:],
                                  deps=b_m2[psl].rd() + (b_ckv.wd() if n == 0 else []))
                        if n == 0:
                            b_ckv.owr(tv)
                        else:
                            b_ckv.owp(tv)
                        b_m2[psl].ord(tv)
                pr.add("v", ckv_v)

            if gtt < NTT:
                def lg_t(q, gtt=gtt, sl=sl):
                    pairs = [(xnp[sl][:, kt * 128:(kt + 1) * 128],
                              gw_sb[:, kt * 16:(kt + 1) * 16]) for kt in range(16)]
                    t = q.mm_group(lgps[:], pairs,
                                   deps=b_xnp[sl].w + b_lg.wd() + b_const.w)
                    b_lg.owr(t)
                    b_xnp[sl].ord(t)
                pr.add("t", lg_t)

                def lg_v(q, gtt=gtt):
                    tv = q.op(q.eng.tensor_copy, logit[:, gtt * 16:(gtt + 1) * 16],
                              lgps[:], deps=b_lg.rd())
                    b_logit.owp(tv)
                    b_lg.ord(tv)
                pr.add("v", lg_v)

            def ckv_spill(q, gtt=gtt):
                t = q.dma("scv", ckv_sp[gtt], ckvb[:], deps=b_ckv.w)
                b_ckv.ord(t)
                b_ckvsp.owp(t)
            pr.add("sy", ckv_spill)

        def top4_v1(q):
            for tt in range(NTT):
                t0 = q.op(q.eng.tensor_copy, lw[:], logit[:, tt * 16:(tt + 1) * 16],
                          deps=b_logit.w + b_lw.wd())
                b_lw.owr(t0)
                b_logit.ord(t0)
                for r in range(4):
                    dst = st8[:, 0:1] if r < 3 else tau[:, tt:tt + 1]
                    t1 = q.op(q.eng.reduce_max, dst, lw[:], axis=AX.X,
                              deps=[t0] + (b_st8.wd() if r < 3 else b_tau.wd()))
                    if r < 3:
                        b_st8.owr(t1)
                        t2 = q.op(q.eng.tensor_scalar, mku[:], lw[:], dst, None,
                                  ALU.is_ge, deps=[t1] + b_mku.wd())
                        b_mku.owr(t2)
                        t0 = q.op(q.eng.copy_predicated, lw[:], mku[:], neg[:],
                                  deps=[t2] + b_const.w)
                        b_lw.owr(t0)
                    else:
                        b_tau.owp(t1)
                        b_lw.ord(t1)
        pr.add("v", top4_v1)

        def top4_pair(tt):
            def fs(q, tt=tt):
                t = q.op(q.eng.activation, ela[:], logit[:, tt * 16:(tt + 1) * 16],
                         AF.Exp, deps=b_logit.w + b_ela.wd() + b_tau.w)
                b_ela.owr(t)
            def fv(q, tt=tt):
                t1 = q.op(q.eng.tensor_scalar, sel[:], logit[:, tt * 16:(tt + 1) * 16],
                          tau[:, tt:tt + 1], None, ALU.is_ge,
                          deps=b_tau.w + b_logit.w + b_sel.wd())
                b_sel.owr(t1)
                t2 = q.op(q.eng.tensor_tensor, sel[:], ela[:], sel[:], ALU.mult,
                          deps=[t1] + b_ela.w)
                b_sel.owr(t2)
                b_ela.ord(t2)
                t3 = q.op(q.eng.reduce_sum, st8[:, 0:1], sel[:], axis=AX.X,
                          deps=[t2] + b_st8.wd())
                t4 = q.op(q.eng.reciprocal, st8[:, 1:2], st8[:, 0:1], deps=[t3])
                b_st8.owr(t4)
                t5 = q.op(q.eng.tensor_scalar, cw[:, tt * 16:(tt + 1) * 16], sel[:],
                          st8[:, 1:2], None, ALU.mult, deps=[t4])
                b_cw.owp(t5)
                b_sel.ord(t5)
                b_st8.ord(t5)
            return fs, fv
        for tt in range(NTT):
            fs, fv = top4_pair(tt)
            pr.add("s", fs)
            pr.add("v", fv)
        pr.flush("a1")

    with contextlib.ExitStack() as st_:
        sbp = lambda n, s, d: st_.enter_context(nc.sbuf_tensor(n, s, d))
        psp = lambda n, s, d: st_.enter_context(nc.psum_tensor(n, s, d))
        wA2 = sbp("wA2", [128, 16 * CD], BF16)
        inb = [sbp(f"a2in{i}", [128, H], BF16) for i in range(2)]
        ob = sbp("a2ob", [128, CD], BF16)
        mm2 = [psp(f"a2mm{i}", [128, 128], F32) for i in range(2)]
        b_w, b_in, b_o = B(), [B(), B()], B()
        b_m2 = [B(), B()]

        def a2_w(q):
            b_w.owr(q.dma("dw", wA2[:].rearrange("p (a n) -> p a n", n=CD),
                          wdq.rearrange("(a p) n -> p a n", p=128)))
        pr.add("g", a2_w)

        for gtt in range(NTTA):
            sl = gtt % 2

            def iload(q, gtt=gtt, sl=sl):
                t = q.dma(f"da2{sl}", inb[sl][:], xn_sp[gtt],
                          deps=b_in[sl].wd() + b_xnsp.w)
                b_in[sl].owr(t)
            pr.add("sy", iload)
            fm_proj_steps(wA2, b_w, inb[sl], b_in[sl], ob, b_o, 16, 8, mm2, b_m2)

            def ospill(q, gtt=gtt):
                t = q.dma("scq", cq_sp[gtt], ob[:], deps=b_o.w)
                b_o.ord(t)
                b_cqsp.owp(t)
            pr.add("sy", ospill)
        pr.flush("a2")

    with contextlib.ExitStack() as st_:
        sbp = lambda n, s, d: st_.enter_context(nc.sbuf_tensor(n, s, d))
        psp = lambda n, s, d: st_.enter_context(nc.psum_tensor(n, s, d))
        wA3 = sbp("wA3", [128, 16 * 1024], BF16)
        inb = [sbp(f"a3in{i}", [128, H], BF16) for i in range(2)]
        krt = sbp("krt", [128, 1024], BF16)
        krot = sbp("krot", [128, 1024], BF16)
        csb = sbp("csb", [128, 512], BF16)
        snb = sbp("snb", [128, 512], BF16)
        rp1 = sbp("rp1", [128, 512], BF16)
        rp2 = sbp("rp2", [128, 256], BF16)
        kch = sbp("kch", [128, NH * 128], BF16)
        p5 = [psp(f"a3p5{i}", [128, 512], F32) for i in range(2)]
        tpsb = psp("a3tps", [128, 512], BF16)
        b_w, b_in = B(), [B(), B()]
        b_krt, b_krot, b_cs, b_ss, b_r1, b_r2, b_kch, b_tps = (B() for _ in range(8))
        b_p5 = [B(), B()]

        def a3_w(q):
            b_w.owr(q.dma("dw", wA3[:].rearrange("p (a n) -> p a n", n=1024),
                          wkr.rearrange("(a p) n -> p a n", p=128)))
        pr.add("g", a3_w)

        for gtt in range(NTT):
            sl = gtt % 2

            def iload(q, gtt=gtt, sl=sl):
                t = q.dma(f"da3{sl}", inb[sl][:], xn_sp[gtt],
                          deps=b_in[sl].wd() + b_xnsp.w)
                b_in[sl].owr(t)
            pr.add("sy", iload)

            def csload(q, gtt=gtt):
                q.dma("da3c", csb[:], cos_a[gtt * 128:(gtt + 1) * 128, :],
                      deps=b_cs.wd() + b_ss.wd())
                ts_ = q.dma("da3c", snb[:], sin_a[gtt * 128:(gtt + 1) * 128, :])
                b_cs.owr(ts_)
                b_ss.owr(ts_)
            pr.add("g", csload)

            def a3_t(q, sl=sl):
                for n in range(2):
                    pairs = [(inb[sl][:, kt * 128:(kt + 1) * 128],
                              wA3[:, kt * 1024 + n * 512: kt * 1024 + n * 512 + 512])
                             for kt in range(16)]
                    t = q.mm_group(p5[n][:], pairs,
                                   deps=b_in[sl].w + b_p5[n].wd() + b_w.w)
                    b_p5[n].owr(t)
                    b_in[sl].ord(t)
            pr.add("t", a3_t)

            def a3_v(q):
                for n in range(2):
                    tv = q.op(q.eng.tensor_copy, krt[:, n * 512:(n + 1) * 512],
                              p5[n][:],
                              deps=b_p5[n].rd() + (b_krt.wd() if n == 0 else []))
                    if n == 0:
                        b_krt.owr(tv)
                    else:
                        b_krt.owp(tv)
                    b_p5[n].ord(tv)
            pr.add("v", a3_v)

            krotv = krot[:].rearrange("p (h j) -> p h j", j=64)
            rope_steps(krt, b_krt, krotv, b_krot, csb, b_cs, snb, b_ss,
                       rp1, b_r1, rp2, b_r2)

            for hg in range(4):
                def a3_tr(q, hg=hg):
                    tk = None
                    for j in range(4):
                        h = hg * 4 + j
                        tk = q.op(q.eng.transpose,
                                  tpsb[64:128, j * 128:(j + 1) * 128],
                                  krot[:, h * 64:(h + 1) * 64], ident[:],
                                  tile_position=(0, 64),
                                  deps=(b_krot.w + b_tps.wd() + b_const.w)
                                  if j == 0 else (), inc=(1 if j == 3 else 0))
                    b_tps.owr(tk)
                    b_krot.ord(tk)
                pr.add("t", a3_tr)

                def a3_ev(q, hg=hg):
                    t = q.op(q.eng.tensor_copy,
                             kch[64:128, hg * 512:(hg + 1) * 512], tpsb[64:128, :],
                             deps=b_tps.rd() + (b_kch.wd() if hg == 0 else []))
                    if hg == 0:
                        b_kch.owr(t)
                    else:
                        b_kch.owp(t)
                    b_tps.ord(t)
                pr.add("v", a3_ev)

            def a3_spill(q, gtt=gtt):
                t = q.dma("skf", kf_sp[gtt, 64:128, :], kch[64:128, :],
                          deps=b_kch.w)
                b_kch.ord(t)
                b_kfsp.owp(t)
            pr.add("sy", a3_spill)
        pr.flush("a3")

    with contextlib.ExitStack() as st_:
        sbp = lambda n, s, d: st_.enter_context(nc.sbuf_tensor(n, s, d))
        psp = lambda n, s, d: st_.enter_context(nc.psum_tensor(n, s, d))
        wB1 = sbp("wB1", [128, 8 * 2048], BF16)
        cin = [sbp(f"b1in{i}", [128, CD], BF16) for i in range(2)]
        vtb = sbp("vtb", [128, 2048], BF16)
        p5 = [psp(f"b1p5{i}", [128, 512], F32) for i in range(2)]
        b_w, b_in, b_vt = B(), [B(), B()], B()
        b_p5 = [B(), B()]

        def b1_w(q):
            b_w.owr(q.dma("dw", wB1[:].rearrange("p (a n) -> p a n", n=2048),
                          wuv.rearrange("(a p) n -> p a n", p=128)))
        pr.add("g", b1_w)

        for gtt in range(NTTA):
            sl = gtt % 2

            def iload(q, gtt=gtt, sl=sl):
                t = q.dma(f"db1{sl}", cin[sl][:], ckv_sp[gtt],
                          deps=b_in[sl].wd() + b_ckvsp.w)
                b_in[sl].owr(t)
            pr.add("sy", iload)

            for half in range(0, 4, 2):
                def b1_t(q, sl=sl, half=half):
                    for n in (half, half + 1):
                        psl = n % 2
                        pairs = [(cin[sl][:, kt * 128:(kt + 1) * 128],
                                  wB1[:, kt * 2048 + n * 512: kt * 2048 + n * 512 + 512])
                                 for kt in range(8)]
                        t = q.mm_group(p5[psl][:], pairs,
                                       deps=b_in[sl].w + b_p5[psl].wd() + b_w.w)
                        b_p5[psl].owr(t)
                        b_in[sl].ord(t)
                pr.add("t", b1_t)

                def b1_v(q, half=half):
                    for n in (half, half + 1):
                        psl = n % 2
                        tv = q.op(q.eng.tensor_copy, vtb[:, n * 512:(n + 1) * 512],
                                  p5[psl][:],
                                  deps=b_p5[psl].rd() + (b_vt.wd() if n == 0 else []))
                        if n == 0:
                            b_vt.owr(tv)
                        else:
                            b_vt.owp(tv)
                        b_p5[psl].ord(tv)
                pr.add("v", b1_v)

            def vspill(q, gtt=gtt):
                t = q.dma("sv", v_sp[gtt], vtb[:], deps=b_vt.w)
                b_vt.ord(t)
                b_vsp.owp(t)
            pr.add("sy", vspill)
        pr.flush("b1")

    with contextlib.ExitStack() as st_:
        sbp = lambda n, s, d: st_.enter_context(nc.sbuf_tensor(n, s, d))
        psp = lambda n, s, d: st_.enter_context(nc.psum_tensor(n, s, d))
        wQ = sbp("wQ", [128, 8 * 1024], BF16)
        wR = sbp("wR", [128, 8 * 1024], BF16)
        cin = [sbp(f"b2in{i}", [128, CD], BF16) for i in range(2)]
        vin = [sbp(f"vin{i}", [128, 2048], BF16) for i in range(2)]
        qcf = sbp("qcf", [128, 1024], BF16)
        qrt = sbp("qrt", [128, 1024], BF16)
        qtb = sbp("qtb", [128, 2048], BF16)
        csb = sbp("b2cs", [128, 512], BF16)
        snb = sbp("b2sn", [128, 512], BF16)
        rp1 = sbp("b2r1", [128, 512], BF16)
        rp2 = sbp("b2r2", [128, 256], BF16)
        mm2 = [psp(f"b2mm{i}", [128, 128], F32) for i in range(2)]
        p5 = [psp(f"b2p5{i}", [128, 512], F32) for i in range(2)]
        mps = psp("mps", [128, 2048], F32)
        b_wq, b_wr = B(), B()
        b_in, b_vin = [B(), B()], [B(), B()]
        b_qcf, b_qrt, b_qt, b_cs, b_ss, b_r1, b_r2, b_mps = (B() for _ in range(8))
        b_m2 = [B(), B()]
        b_p5 = [B(), B()]

        def b2_w(q):
            q.dma("dw", wQ[:].rearrange("p (a n) -> p a n", n=1024),
                  wuq.rearrange("(a p) n -> p a n", p=128))
            t = q.dma("dw", wR[:].rearrange("p (a n) -> p a n", n=1024),
                      wqr.rearrange("(a p) n -> p a n", p=128))
            b_wq.owr(t)
            b_wr.owr(t)
        pr.add("g", b2_w)

        for gtt in range(NTTA):
            sl = gtt % 2

            def iload(q, gtt=gtt, sl=sl):
                t = q.dma(f"db2{sl}", cin[sl][:], cq_sp[gtt],
                          deps=b_in[sl].wd() + b_cqsp.w)
                b_in[sl].owr(t)
                tv = q.dma(f"db2v{sl}", vin[sl][:], v_sp[gtt],
                           deps=b_vin[sl].wd() + b_vsp.w)
                b_vin[sl].owr(tv)
            pr.add("sy", iload)

            def csload(q, gtt=gtt):
                q.dma("db2c", csb[:], cos_a[gtt * 128:(gtt + 1) * 128, :],
                      deps=b_cs.wd() + b_ss.wd())
                ts_ = q.dma("db2c", snb[:], sin_a[gtt * 128:(gtt + 1) * 128, :])
                b_cs.owr(ts_)
                b_ss.owr(ts_)
            pr.add("g", csload)

            fm_proj_steps(wQ, b_wq, cin[sl], b_in[sl], qcf, b_qcf, 8, 8, mm2, b_m2)

            def qr_t(q):
                for n in range(2):
                    pairs = [(qcf[:, kt * 128:(kt + 1) * 128],
                              wR[:, kt * 1024 + n * 512: kt * 1024 + n * 512 + 512])
                             for kt in range(8)]
                    t = q.mm_group(p5[n][:], pairs,
                                   deps=b_qcf.w + b_p5[n].wd() + b_wr.w)
                    b_p5[n].owr(t)
                    b_qcf.ord(t)
            pr.add("t", qr_t)

            def qr_v(q):
                for n in range(2):
                    tv = q.op(q.eng.tensor_copy, qrt[:, n * 512:(n + 1) * 512],
                              p5[n][:],
                              deps=b_p5[n].rd() + (b_qrt.wd() if n == 0 else []))
                    if n == 0:
                        b_qrt.owr(tv)
                    else:
                        b_qrt.owp(tv)
                    b_p5[n].ord(tv)
            pr.add("v", qr_v)

            def qc_t(q, sl=sl):
                for n in range(2):
                    pairs = [(cin[sl][:, kt * 128:(kt + 1) * 128],
                              wQ[:, kt * 1024 + n * 512: kt * 1024 + n * 512 + 512])
                             for kt in range(8)]
                    t = q.mm_group(p5[n][:], pairs,
                                   deps=b_in[sl].w + b_p5[n].wd() + b_wq.w)
                    b_p5[n].owr(t)
                    b_in[sl].ord(t)
            pr.add("t", qc_t)

            def qc_v(q):
                for n in range(2):
                    src = p5[n][:].rearrange("p (h j) -> p h j", j=64)
                    dstv = qtb[:].rearrange("p (h j) -> p h j", j=128)[:, n * 8:(n + 1) * 8, 0:64]
                    tv = q.op(q.eng.tensor_copy, dstv, src,
                              deps=b_p5[n].rd() + (b_qt.wd() if n == 0 else []))
                    if n == 0:
                        b_qt.owr(tv)
                    else:
                        b_qt.owp(tv)
                    b_p5[n].ord(tv)
            pr.add("v", qc_v)

            qtv = qtb[:].rearrange("p (h j) -> p h j", j=128)[:, :, 64:128]
            rope_steps(qrt, b_qrt, qtv, b_qt, csb, b_cs, snb, b_ss, rp1, b_r1,
                       rp2, b_r2)

            def b2_m(q, gtt=gtt, sl=sl):
                t = None
                for h in range(NH):
                    t = q.mm_group(mps[:, h * 128:(h + 1) * 128],
                                   [(qtb[:, h * 128:(h + 1) * 128],
                                     vin[sl][:, h * 128:(h + 1) * 128])],
                                   deps=(b_qt.w + b_vin[sl].w +
                                         (b_mps.wd() if gtt == 0 else []))
                                   if h == 0 else (),
                                   start=(gtt == 0 and h % 4 == 0),
                                   stop=(gtt == NTTA - 1 and h % 4 == 3))
                b_mps.owp(t)
                b_qt.ord(t)
                b_vin[sl].ord(t)
            pr.add("t", b2_m)

        def b2_mev(q):
            t = q.op(q.eng.tensor_scalar, msb[:], mps[:], RSC, None, ALU.mult,
                     deps=b_mps.w + b_msb.wd())
            b_msb.owr(t)
            b_mps.ord(t)
        pr.add("v", b2_mev)
        pr.flush("b2")

    with contextlib.ExitStack() as st_:
        sbp = lambda n, s, d: st_.enter_context(nc.sbuf_tensor(n, s, d))
        psp = lambda n, s, d: st_.enter_context(nc.psum_tensor(n, s, d))
        wB3 = sbp("wB3", [128, 8 * 1024], BF16)
        cin = [sbp(f"b3in{i}", [128, CD], BF16) for i in range(2)]
        kch = sbp("b3kch", [128, NH * 128], BF16)
        p5 = [psp(f"b3p5{i}", [128, 512], F32) for i in range(2)]
        b_w, b_in, b_kch = B(), [B(), B()], B()
        b_p5 = [B(), B()]

        def b3_w(q):
            b_w.owr(q.dma("dw", wB3[:].rearrange("p (a n) -> p a n", n=1024),
                          wuk.rearrange("(a p) n -> p a n", p=128)))
        pr.add("g", b3_w)

        for gtt in range(NTT):
            sl = gtt % 2

            def iload(q, gtt=gtt, sl=sl):
                t = q.dma(f"db3{sl}", cin[sl][:], ckv_sp[gtt],
                          deps=b_in[sl].wd() + b_ckvsp.w)
                b_in[sl].owr(t)
            pr.add("sy", iload)

            for hg in range(4):
                def b3_t(q, sl=sl, hg=hg):
                    psl = hg % 2
                    tk = None
                    for j in range(4):
                        h = hg * 4 + j
                        pairs = [(wB3[:, kt * 1024 + h * 64: kt * 1024 + h * 64 + 64],
                                  cin[sl][:, kt * 128:(kt + 1) * 128])
                                 for kt in range(8)]
                        tk = q.mm_group(p5[psl][0:64, j * 128:(j + 1) * 128], pairs,
                                        deps=(b_in[sl].w + b_p5[psl].wd() + b_w.w)
                                        if j == 0 else ())
                    b_p5[psl].owr(tk)
                    b_in[sl].ord(tk)
                pr.add("t", b3_t)

                def b3_v(q, hg=hg):
                    psl = hg % 2
                    t = q.op(q.eng.tensor_copy, kch[0:64, hg * 512:(hg + 1) * 512],
                             p5[psl][0:64, :],
                             deps=b_p5[psl].rd() + (b_kch.wd() if hg == 0 else []))
                    if hg == 0:
                        b_kch.owr(t)
                    else:
                        b_kch.owp(t)
                    b_p5[psl].ord(t)
                pr.add("v", b3_v)

            def b3_spill(q, gtt=gtt):
                t = q.dma("skf2", kf_sp[gtt, 0:64, :], kch[0:64, :], deps=b_kch.w)
                b_kch.ord(t)
                b_kfsp.owp(t)
            pr.add("sy", b3_spill)
        pr.flush("b3")

    with contextlib.ExitStack() as st_:
        sbp = lambda n, s, d: st_.enter_context(nc.sbuf_tensor(n, s, d))
        psp = lambda n, s, d: st_.enter_context(nc.psum_tensor(n, s, d))
        wO = sbp("wO", [128, 16 * 2048], BF16)
        kin = [sbp(f"kin{i}", [128, 2048], BF16) for i in range(2)]
        outf = sbp("outf", [128, 2048], BF16)
        s_st = sbp("s_st", [128, 2048], BF16)
        p5 = [psp(f"p2p5{i}", [128, 512], F32) for i in range(2)]
        b_w, b_kin, b_of, b_sst = B(), [B(), B()], B(), B()
        b_p5 = [B(), B()]

        def p2_w(q):
            b_w.owr(q.dma("dw", wO[:].rearrange("p (a n) -> p a n", n=2048),
                          wout.rearrange("(a p) n -> p a n", p=128)))
        pr.add("g", p2_w)

        for gtt in range(NTT):
            sl = gtt % 2

            def iload(q, gtt=gtt, sl=sl):
                t = q.dma(f"dp2{sl}", kin[sl][:], kf_sp[gtt],
                          deps=b_kin[sl].wd() + b_kfsp.w)
                b_kin[sl].owr(t)
            pr.add("sy", iload)

            for hg in range(4):
                def of_t(q, sl=sl, hg=hg):
                    psl = hg % 2
                    tk = None
                    for j in range(4):
                        h = hg * 4 + j
                        tk = q.mm_group(p5[psl][:, j * 128:(j + 1) * 128],
                                        [(msb[:, h * 128:(h + 1) * 128],
                                          kin[sl][:, h * 128:(h + 1) * 128])],
                                        deps=(b_msb.w + b_kin[sl].w + b_p5[psl].wd())
                                        if j == 0 else ())
                    b_p5[psl].owr(tk)
                    b_kin[sl].ord(tk)
                pr.add("t", of_t)

                def of_v(q, hg=hg):
                    psl = hg % 2
                    t = q.op(q.eng.tensor_copy, outf[:, hg * 512:(hg + 1) * 512],
                             p5[psl][:],
                             deps=b_p5[psl].rd() + (b_of.wd() if hg == 0 else []))
                    if hg == 0:
                        b_of.owr(t)
                    else:
                        b_of.owp(t)
                    b_p5[psl].ord(t)
                pr.add("v", of_v)

            for half in range(0, 4, 2):
                def s_t(q, half=half):
                    for n in (half, half + 1):
                        psl = n % 2
                        pairs = [(outf[:, kt * 128:(kt + 1) * 128],
                                  wO[:, kt * 2048 + n * 512: kt * 2048 + n * 512 + 512])
                                 for kt in range(16)]
                        t = q.mm_group(p5[psl][:], pairs,
                                       deps=b_of.w + b_p5[psl].wd() + b_w.w)
                        b_p5[psl].owr(t)
                        b_of.ord(t)
                pr.add("t", s_t)

                def s_v(q, half=half):
                    for n in (half, half + 1):
                        psl = n % 2
                        tv = q.op(q.eng.tensor_copy, s_st[:, n * 512:(n + 1) * 512],
                                  p5[psl][:],
                                  deps=b_p5[psl].rd() + (b_sst.wd() if n == 0 else []))
                        if n == 0:
                            b_sst.owr(tv)
                        else:
                            b_sst.owp(tv)
                        b_p5[psl].ord(tv)
                pr.add("v", s_v)

            def sspill(q, gtt=gtt):
                t = q.dma("ssp", s_sp[gtt], s_st[:], deps=b_sst.w)
                b_sst.ord(t)
                b_ssp.owp(t)
            pr.add("sy", sspill)
        pr.flush("p2")

    with contextlib.ExitStack() as st_:
        sbp = lambda n, s, d: st_.enter_context(nc.sbuf_tensor(n, s, d))
        psp = lambda n, s, d: st_.enter_context(nc.psum_tensor(n, s, d))
        xn_f = sbp("xn_f", [128, 16 * Tc], BF16)
        act = sbp("act", [128, 12 * Tc], BF16)
        u16b = [sbp(f"u16b{i}", [128, 4 * 512], U16) for i in range(2)]
        wdq_ = [sbp(f"wdq{i}", [128, 16 * 512], BF16) for i in range(2)]
        sclb = sbp("sclb", [128, 4 * 512], BF16)
        gss = [sbp(f"gss{i}", [4, 4 * 512], BF16) for i in range(2)]
        t7b = sbp("t7b", [128, 512], U16)
        mbb = [sbp(f"mbb{i}", [128, 512], U16) for i in range(4)]
        sgb = [sbp(f"sgb{i}", [128, 512], U16) for i in range(4)]
        stg = [sbp(f"stg{i}", [128, G4 * 512], F32) for i in range(2)]
        p5 = [psp(f"p3p5{i}", [128, 512], F32) for i in range(2)]
        sclp = psp("sclp", [128, 2048], F32)
        b_xnf, b_act, b_scl = (B() for _ in range(3))
        b_u16 = [B(), B()]
        b_gss = [B(), B()]
        b_wd = [B(), B()]
        b_t7 = B()
        b_mb = [B() for _ in range(4)]
        b_sg = [B() for _ in range(4)]
        b_stg = [B(), B()]
        b_sclp = B()
        b_p5 = [B(), B()]

        def ffn_xn(q):
            xv = xn_f[:].rearrange("p (a n) -> p a n", n=Tc)
            for tt in range(NTT):
                t = q.dma("dxnf", xv[:, :, tt * 128:(tt + 1) * 128],
                          xn_sp[tt].rearrange("p (a n) -> p a n", n=128),
                          deps=(b_xnf.wd() if tt == 0 else []) + b_xnsp.w)
                b_xnf.owp(t)
        pr.add("sy", ffn_xn)

        order_g = [0, 3, 1, 4, 2, 5]
        actv = act[:].rearrange("p (a n) -> p a n", n=Tc)
        xnv = xn_f[:].rearrange("p (a n) -> p a n", n=Tc)

        def deq_chunks(nu, slot):
            wd = wdq_[slot]
            bwd = b_wd[slot]
            ub_ = u16b[slot]
            bu = b_u16[slot]
            gs_ = gss[slot]
            bgs = b_gss[slot]

            def head():
                def f_sc(q):
                    tk = None
                    for u in range(nu):
                        tk = q.mm_group(sclp[:, u * 512:(u + 1) * 512],
                                        [(ind4[:], gs_[0:4, u * 512:(u + 1) * 512])],
                                        deps=(b_sclp.wd() + bgs.w + b_const.w)
                                        if u == 0 else ())
                    b_sclp.owr(tk)
                    bgs.ord(tk)
                pr.add("t", f_sc)

                def f_scv(q):
                    t = q.op(q.eng.tensor_copy, sclb[:, 0:nu * 512],
                             sclp[:, 0:nu * 512], deps=b_sclp.rd() + b_scl.wd())
                    b_scl.owr(t)
                    b_sclp.ord(t)
                pr.add("v", f_scv)

            def uchunk(u):
                tgk = [None] * 4

                def f_v1(q, u=u):
                    ub = ub_[:, u * 512:(u + 1) * 512]
                    for k in range(4):
                        t1 = q.op(q.eng.tensor_scalar, t7b[:], ub, 4 * k, 7,
                                  ALU.logical_shift_right, ALU.bitwise_and,
                                  deps=bu.w + b_t7.wd())
                        b_t7.owr(t1)
                        bu.ord(t1)
                        t2 = q.op(q.eng.tensor_scalar, mbb[k][:], t7b[:], 64, 16128,
                                  ALU.mult, ALU.add, deps=[t1] + b_mb[k].wd())
                        b_mb[k].owr(t2)
                        b_t7.ord(t2)
                        t3 = q.op(q.eng.tensor_scalar, sgb[k][:], ub, 12 - 4 * k,
                                  32768, ALU.logical_shift_left, ALU.bitwise_and,
                                  deps=b_sg[k].wd())
                        b_sg[k].owr(t3)
                        bu.ord(t3)
                pr.add("v", f_v1)

                def f_gp(q, u=u):
                    for k in range(4):
                        kt = 4 * u + k
                        wsl = wd[:, kt * 512:(kt + 1) * 512]
                        t1 = q.op(q.eng.tensor_scalar, wsl, mbb[k][:].bitcast(BF16),
                                  2.0, -1.0, ALU.mult, ALU.add,
                                  deps=b_mb[k].w +
                                  (bwd.wd() if (u == 0 and k == 0) else []))
                        t2 = q.op(q.eng.tensor_tensor, wsl, wsl,
                                  mbb[k][:].bitcast(BF16), ALU.min, deps=[t1])
                        tgk[k] = t2
                        if u == 0 and k == 0:
                            bwd.owr(t1)
                        bwd.owp(t2)
                        b_mb[k].ord(t2)
                pr.add("v", f_gp)

                def f_v2(q, u=u):
                    for k in range(4):
                        kt = 4 * u + k
                        wsl = wd[:, kt * 512:(kt + 1) * 512]
                        t3 = q.op(q.eng.tensor_tensor, wsl.bitcast(U16),
                                  wsl.bitcast(U16), sgb[k][:], ALU.bitwise_or,
                                  deps=[tgk[k]] + b_sg[k].w)
                        t4 = q.op(q.eng.tensor_tensor, wsl, wsl,
                                  sclb[:, u * 512:(u + 1) * 512], ALU.mult,
                                  deps=[t3] + b_scl.w)
                        bwd.owp(t4)
                        b_sg[k].ord(t4)
                        b_mb[k].ord(t4)
                        b_scl.ord(t4)
                pr.add("v", f_v2)

            return [head] + [(lambda u=u: uchunk(u)) for u in range(nu)]

        allblocks = [(e, kind, idx) for e in range(17)
                     for (kind, idx) in ([("gu", nb) for nb in order_g]
                                         + [("dn", hb) for hb in range(4)])]
        NB = len(allblocks)

        def emit_load(bi):
            e, kind, idx = allblocks[bi]
            slot = bi % 2
            ub_, bu = u16b[slot], b_u16[slot]
            gs_, bgs = gss[slot], b_gss[slot]
            if kind == "gu":
                def gu_load(q, e=e, nb=idx):
                    t = q.dma("du", ub_[:], gu16[e, nb], deps=bu.wd())
                    bu.owr(t)

                def gss_load(q, e=e, nb=idx):
                    tg = q.dma("dgs", gs_[:], gsc[e, nb], deps=bgs.wd())
                    bgs.owr(tg)
            else:
                def gu_load(q, e=e, hb=idx):
                    t = q.dma("du", ub_[:, 0:3 * 512], d16[e, hb],
                              deps=bu.wd())
                    bu.owr(t)

                def gss_load(q, e=e, hb=idx):
                    tg = q.dma("dgs", gs_[0:4, 0:3 * 512], dsc[e, hb],
                               deps=bgs.wd())
                    bgs.owr(tg)
            pr.add("sy", gu_load)
            pr.add("g", gss_load)

        def emit_mm(bi, deq):
            e, kind, idx = allblocks[bi]
            slot = bi % 2
            wd = wdq_[slot]
            bwd = b_wd[slot]
            steps = []
            if kind == "gu":
                nb = idx
                is_up = nb >= 3
                jbase = (nb - 3) * 4 if is_up else nb * 4
                for ntl in range(4):
                    j = jbase + ntl
                    for tc0 in range(0, TNC, 2):
                        tcs = [tc for tc in (tc0, tc0 + 1) if tc < TNC]

                        def step(ntl=ntl, tcs=tcs, j=j, is_up=is_up):
                            def gu_t(q):
                                for tc in tcs:
                                    psl = tc % 2
                                    pairs = [(wd[:, kt * 512 + ntl * 128:
                                                 kt * 512 + ntl * 128 + 128],
                                              xnv[:, kt, tc * TW:(tc + 1) * TW])
                                             for kt in range(16)]
                                    t = q.mm_group(p5[psl][:, 0:TW], pairs,
                                                   deps=bwd.w + b_xnf.w + b_p5[psl].wd())
                                    b_p5[psl].owr(t)
                                    bwd.ord(t)
                                    b_xnf.ord(t)
                            pr.add("t", gu_t)

                            if not is_up:
                                def gu_s(q):
                                    for tc in tcs:
                                        psl = tc % 2
                                        t = q.op(q.eng.activation,
                                                 actv[:, j, tc * TW:(tc + 1) * TW],
                                                 p5[psl][:, 0:TW], AF.Silu,
                                                 deps=b_p5[psl].rd() + b_act.w + b_act.r)
                                        b_act.owp(t)
                                        b_p5[psl].ord(t)
                                pr.add("s", gu_s)
                            else:
                                def gu_v(q):
                                    for tc in tcs:
                                        psl = tc % 2
                                        asl = actv[:, j, tc * TW:(tc + 1) * TW]
                                        t = q.op(q.eng.tensor_tensor, asl, asl,
                                                 p5[psl][:, 0:TW], ALU.mult,
                                                 deps=b_p5[psl].rd() + b_act.w)
                                        b_act.owp(t)
                                        b_p5[psl].ord(t)
                                pr.add("v", gu_v)
                        steps.append(step)
            else:
                hb = idx
                for tg in range(NTT // G4):
                    def step(tg=tg, hb=hb, e=e):
                        tis = list(range(G4))

                        def dn_t(q):
                            for ti in tis:
                                tt = tg * G4 + ti
                                psl = ti % 2
                                pairs = [(actv[:, kt, tt * 128:(tt + 1) * 128],
                                          wd[:, kt * 512:(kt + 1) * 512])
                                         for kt in range(12)]
                                t = q.mm_group(p5[psl][:], pairs,
                                               deps=b_act.w + bwd.w + b_p5[psl].wd())
                                b_p5[psl].owr(t)
                                bwd.ord(t)
                                b_act.ord(t)
                        pr.add("t", dn_t)

                        def dn_v(q):
                            ssl = tg % 2
                            for ti in tis:
                                tt = tg * G4 + ti
                                psl = ti % 2
                                dst = stg[ssl][:, ti * 512:(ti + 1) * 512]
                                if e < 16:
                                    t = q.op(q.eng.tensor_scalar, dst, p5[psl][:],
                                             cw[:, tt * 16 + e:tt * 16 + e + 1], None,
                                             ALU.mult,
                                             deps=b_p5[psl].rd() + (b_stg[ssl].wd() if ti == 0 else []) + b_cw.w)
                                else:
                                    t = q.op(q.eng.tensor_copy, dst, p5[psl][:],
                                             deps=b_p5[psl].rd() + (b_stg[ssl].wd() if ti == 0 else []))
                                b_stg[ssl].owr(t) if ti == 0 else b_stg[ssl].owp(t)
                                b_p5[psl].ord(t)
                        pr.add("v", dn_v)

                        def dn_acc(q):
                            ssl = tg % 2
                            kw = {} if e == 0 else {"accum_op": ALU.add}
                            gate = [Tok("ga", 16 * NACC * e)] if e > 0 else []
                            t = q.dma("ga",
                                      gacc[hb, tg * G4:(tg + 1) * G4]
                                      .rearrange("a p n -> p a n"),
                                      stg[ssl][:].rearrange("p (a n) -> p a n", n=512),
                                      deps=b_stg[ssl].w + gate, **kw)
                            b_stg[ssl].ord(t)
                            b_gacc.owp(t)
                        pr.add("g", dn_acc)
                    steps.append(step)

            if deq:
                deq[0]()
                nch = len(deq) - 1
                done = 0
                for i, st in enumerate(steps):
                    st()
                    want = min(nch, ((i + 2) * nch) // len(steps))
                    while done < want:
                        deq[1 + done]()
                        done += 1
                while done < nch:
                    deq[1 + done]()
                    done += 1
            else:
                for st in steps:
                    st()

        emit_load(0)
        emit_load(1)
        for c in deq_chunks(4 if allblocks[0][1] == "gu" else 3, 0):
            c()
        for bi in range(NB):
            if bi + 2 < NB:
                emit_load(bi + 2)
            deq = None
            if bi + 1 < NB:
                deq = deq_chunks(4 if allblocks[bi + 1][1] == "gu" else 3,
                                 (bi + 1) % 2)
            emit_mm(bi, deq)
        pr.flush("p3")
        t_ga = Tok("ga", ctx.cnt.get("ga", 0))

    with contextlib.ExitStack() as st_:
        sbp = lambda n, s, d: st_.enter_context(nc.sbuf_tensor(n, s, d))
        gac = sbp("gac", [128, H], F32)
        gsl = sbp("gsl", [128, H], BF16)
        s_in = sbp("s_in", [128, H], BF16)
        x_in = sbp("x_in", [128, H], F32)
        sgf = sbp("sgf", [128, H], F32)
        b_gac, b_gsl, b_sin, b_xin, b_sgf = (B() for _ in range(5))

        for tt in range(NTT):
            def p4_l(q, tt=tt):
                gv = gac[:].rearrange("p (a n) -> p a n", n=512)
                t1 = q.dma("dp4g", gv, gacc[:, tt].rearrange("a p n -> p a n"),
                           deps=b_gac.wd() + [t_ga])
                b_gac.owr(t1)
                t2 = q.dma("dp4s", s_in[:], s_sp[tt], deps=b_sin.wd() + b_ssp.w)
                b_sin.owr(t2)
                t3 = q.dma("dp4x", x_in[:], x_loc[tt * 128:(tt + 1) * 128, :],
                           deps=b_xin.wd())
                b_xin.owr(t3)
            pr.add("sy", p4_l)

            def p4_s(q):
                t = q.op(q.eng.activation, gsl[:], gac[:], AF.Silu,
                         deps=b_gac.w + b_gsl.wd())
                b_gsl.owr(t)
                b_gac.ord(t)
            pr.add("s", p4_s)

            def p4_v(q):
                t1 = q.op(q.eng.tensor_tensor, sgf[:], s_in[:], gsl[:], ALU.mult,
                          deps=b_gsl.w + b_sin.w + b_sgf.wd())
                b_sin.ord(t1)
                b_gsl.ord(t1)
                t2 = q.op(q.eng.tensor_tensor, sgf[:], x_in[:], sgf[:], ALU.add,
                          deps=[t1] + b_xin.w)
                b_sgf.owr(t2)
                b_xin.ord(t2)
            pr.add("v", p4_v)

            def p4_o(q, tt=tt):
                t = q.dma("dout", out_h[tt * 128:(tt + 1) * 128, :], sgf[:],
                          deps=b_sgf.w)
                b_sgf.ord(t)
            pr.add("sy", p4_o)

        def fin(q):
            q.wait(Tok("dout", ctx.cnt["dout"]))
        pr.add("sy", fin)
        pr.flush("p4")

    return nc


def _mk_pi(n):
    r = np.arange(n)
    tile, p = r // 128, r % 128
    u, k = tile // 4, tile % 4
    return 512 * u + 4 * p + k


def _pack_u16(packed):
    k8, n = packed.shape
    return np.ascontiguousarray(
        packed.view(np.uint16).reshape(k8, n, 2).transpose(0, 2, 1).reshape(k8 * 2, n))


def _resc(s, nu):
    g, n = s.shape
    return np.ascontiguousarray(
        s.reshape(nu, 4, n).transpose(1, 0, 2).reshape(4, nu * n))


def prep_inputs(inputs):
    x = np.asarray(inputs["L0_attn_x"], dtype=np.float32)
    B_, S, H_ = x.shape
    PIH = _mk_pi(H)
    PII = _mk_pi(I2)
    colsI = np.concatenate([PII, I2 + PII])

    wdkv = np.ascontiguousarray(np.asarray(inputs["L0_attn_W_dkv"])[PIH])
    wdq = np.ascontiguousarray(np.asarray(inputs["L0_attn_W_dq"])[PIH])
    wkr = np.ascontiguousarray(np.asarray(inputs["L0_attn_W_kr"])[PIH])
    wuk = np.ascontiguousarray(np.asarray(inputs["L0_attn_W_uk"]))
    wuv = np.ascontiguousarray(np.asarray(inputs["L0_attn_W_uv"]))
    wuq = np.ascontiguousarray(np.asarray(inputs["L0_attn_W_uq"]))
    wqr = np.ascontiguousarray(np.asarray(inputs["L0_attn_W_qr"]))
    wout = np.ascontiguousarray(np.asarray(inputs["L0_attn_W_out"]))
    gwT = np.ascontiguousarray(np.asarray(inputs["L0_ffn_gate_w"]).T[PIH])

    gup = np.asarray(inputs["L0_ffn_gate_up_packed"])
    gus = np.asarray(inputs["L0_ffn_gate_up_scales"])
    dnp = np.asarray(inputs["L0_ffn_down_packed"])
    dns = np.asarray(inputs["L0_ffn_down_scales"])
    sgup = np.asarray(inputs["L0_ffn_shared_gate_up_packed"])
    sgus = np.asarray(inputs["L0_ffn_shared_gate_up_scales"])
    sdnp = np.asarray(inputs["L0_ffn_shared_down_packed"])
    sdns = np.asarray(inputs["L0_ffn_shared_down_scales"])

    gu16 = np.stack([_pack_u16(gup[e])[:, colsI] for e in range(16)]
                    + [_pack_u16(sgup)[:, colsI]])
    gu16 = np.ascontiguousarray(
        gu16.reshape(17, 4, 128, 6, 512).transpose(0, 3, 2, 1, 4)
        .reshape(17, 6, 128, 2048))
    gscv = np.stack([_resc(gus[e][:, colsI], 4) for e in range(16)]
                    + [_resc(sgus[:, colsI], 4)]).astype(np.float32)
    gscv = np.ascontiguousarray(
        gscv.reshape(17, 4, 4, 6, 512).transpose(0, 3, 1, 2, 4)
        .reshape(17, 6, 4, 2048))
    d16 = np.stack([_pack_u16(dnp[e]) for e in range(16)] + [_pack_u16(sdnp)])
    d16 = np.ascontiguousarray(
        d16.reshape(17, 3, 128, 4, 512).transpose(0, 3, 2, 1, 4)
        .reshape(17, 4, 128, 1536))
    dscv = np.stack([_resc(dns[e], 3) for e in range(16)]
                    + [_resc(sdns, 3)]).astype(np.float32)
    dscv = np.ascontiguousarray(
        dscv.reshape(17, 4, 3, 4, 512).transpose(0, 3, 1, 2, 4)
        .reshape(17, 4, 4, 1536))

    half_dim = DR // 2
    inv = (1.0 / 10000.0 ** (np.arange(0, half_dim, 2, dtype=np.float32)
                             / np.float32(half_dim))).astype(np.float32)
    tpos = (np.arange(S, dtype=np.float32) / np.float32(40.0))
    fr = np.outer(tpos, inv).astype(np.float32)
    rot = np.concatenate([fr, fr], axis=1)
    cosT = np.tile(np.cos(rot), (1, NH)).astype(np.float32)
    sinT = np.tile(np.sin(rot), (1, NH)).astype(np.float32)

    identd = np.eye(128, dtype=np.float32)
    ind4d = (np.arange(128)[None, :] // 32 == np.arange(4)[:, None]
             ).astype(np.float32)

    Tc = S // 2
    shared = dict(wdkv=wdkv, wdq=wdq, wkr=wkr, wuk=wuk, wuv=wuv, wuq=wuq,
                  wqr=wqr, wout=wout, gw=gwT, gu16=gu16, gsc=gscv, d16=d16,
                  dsc=dscv, identd=identd, ind4d=ind4d)
    in_maps = []
    for c in range(8):
        b, hf = c // 2, c % 2
        oth = 1 - hf
        m = dict(shared)
        m["x_loc"] = np.ascontiguousarray(x[b, hf * Tc:(hf + 1) * Tc])
        m["x_oth"] = np.ascontiguousarray(x[b, oth * Tc:(oth + 1) * Tc])
        m["cos_a"] = np.ascontiguousarray(np.concatenate(
            [cosT[hf * Tc:(hf + 1) * Tc], cosT[oth * Tc:(oth + 1) * Tc]]))
        m["sin_a"] = np.ascontiguousarray(np.concatenate(
            [sinT[hf * Tc:(hf + 1) * Tc], sinT[oth * Tc:(oth + 1) * Tc]]))
        in_maps.append(m)

    return in_maps, S, (B_, H_)


def kernel(**inputs):
    in_maps, S, (B_, H_) = prep_inputs(inputs)
    Tc = S // 2
    nc = build_nc(S)
    res = run_bass_kernel_spmd(nc, in_maps, core_ids=list(range(8)))
    out = np.empty((B_, S, H_), dtype=np.float32)
    for c in range(8):
        b, hf = c // 2, c % 2
        out[b, hf * Tc:(hf + 1) * Tc, :] = res.results[c]["out"]
    return out


# version 12
# speedup vs baseline: 1.2960x; 1.0112x over previous
import contextlib
import numpy as np
import concourse.bass as bass
import concourse.mybir as mybir
from concourse.bass_utils import run_bass_kernel_spmd

F32 = mybir.dt.float32
BF16 = mybir.dt.bfloat16
U16 = mybir.dt.uint16
U8 = mybir.dt.uint8
AF = mybir.ActivationFunctionType
ALU = mybir.AluOpType
AX = mybir.AxisListType

H, CD, UP = 2048, 1024, 1024
NH, DH, DR = 16, 128, 64
I2 = 1536
RSC = 0.08838834764831845
ENG = {"v": "vector", "s": "scalar", "t": "tensor", "g": "gpsimd", "sy": "sync"}


class Tok:
    __slots__ = ("key", "val")

    def __init__(self, key, val):
        self.key = key
        self.val = val


class Ctx:
    def __init__(self, nc):
        self.nc = nc
        self.sems = {}
        self.cnt = {}
        self.hwm = {}
        self.guards = []

    def sem(self, key):
        if key not in self.sems:
            g = self.nc.semaphore(f"S_{key}")
            self.sems[key] = g.__enter__()
            self.guards.append(g)
            self.cnt[key] = 0
        return self.sems[key]


class Q:
    def __init__(self, ctx, name, eng):
        self.ctx = ctx
        self.name = name
        self.eng = eng
        ctx.sem(name)

    def wait(self, *deps):
        best = {}
        for d in deps:
            if d is None:
                continue
            for dd in (d if isinstance(d, (list, tuple)) else [d]):
                if dd is not None and (dd.key not in best or best[dd.key] < dd.val):
                    best[dd.key] = dd.val
        for key, val in best.items():
            hk = (self.name, key)
            if self.ctx.hwm.get(hk, 0) >= val:
                continue
            self.eng.wait_ge(self.ctx.sem(key), val)
            self.ctx.hwm[hk] = val

    def op(self, fn, *args, deps=(), inc=1, **kwargs):
        self.wait(*deps)
        inst = fn(*args, **kwargs)
        if inc:
            self.ctx.cnt[self.name] += inc
            inst.then_inc(self.ctx.sem(self.name), inc)
        return Tok(self.name, self.ctx.cnt[self.name])

    def dma(self, key, out, in_, deps=(), **kwargs):
        self.wait(*deps)
        inst = self.eng.dma_start(out, in_, **kwargs)
        self.ctx.sem(key)
        self.ctx.cnt[key] += 16
        inst.then_inc(self.ctx.sem(key), 16)
        return Tok(key, self.ctx.cnt[key])

    def mm_group(self, out_ap, pairs, deps=(), tile_position=None, start=True,
                 stop=True):
        kw = {} if tile_position is None else {"tile_position": tile_position}
        n = len(pairs)
        inst = None
        for i, (l, r) in enumerate(pairs):
            if i == 0:
                self.wait(*deps)
            inst = self.eng.matmul(out_ap, l, r, start=(start and i == 0),
                                   stop=(stop and i == n - 1), **kw)
        self.ctx.cnt[self.name] += 1
        inst.then_inc(self.ctx.sem(self.name), 1)
        return Tok(self.name, self.ctx.cnt[self.name])


class Buf:
    __slots__ = ("w", "r")

    def __init__(self):
        self.w = []
        self.r = []

    def rd(self):
        return list(self.w)

    def ord(self, tok):
        self.r.append(tok)

    def wd(self):
        return self.w + self.r

    def owr(self, tok):
        self.w = [tok]
        self.r = []

    def owp(self, tok):
        self.w.append(tok)


class Prog:
    def __init__(self, nc, ctx):
        self.nc = nc
        self.ctx = ctx
        self.steps = []

    def add(self, eng, fn):
        self.steps.append((eng, fn))

    def drain_dma(self):
        def mk(q):
            q.wait(*[Tok(k, v) for k, v in list(self.ctx.cnt.items())
                     if k not in ("v", "s", "t", "g", "sy") and v > 0])
        self.add("sy", mk)
        self.add("g", mk)

    def flush(self, name):
        self.drain_dma()
        steps, self.steps = self.steps, None
        with self.nc.Block(name) as blk:
            i = 0
            while i < len(steps):
                eng = steps[i][0]
                fns = [steps[i][1]]
                while i + 1 < len(steps) and steps[i + 1][0] == eng:
                    i += 1
                    fns.append(steps[i][1])

                def sec(engine, fns=fns, eng=eng):
                    q = Q(self.ctx, eng, engine)
                    for f in fns:
                        f(q)
                getattr(blk, ENG[eng])(sec)
                i += 1
        self.steps = []


def build_nc(S_all):
    Tc = S_all // 2
    NTT = Tc // 128
    NTTA = S_all // 128
    TW = min(512, Tc)
    TNC = Tc // TW
    G4 = min(2, NTT)
    NACC = (NTT // G4) * 4

    nc = bass.Bass("TRN2", target_bir_lowering=False)
    dt = lambda n, s, d: nc.dram_tensor(n, s, d, kind="ExternalInput").ap()
    x_loc = dt("x_loc", [Tc, H], F32)
    x_oth = dt("x_oth", [Tc, H], F32)
    wdkv = dt("wdkv", [H, CD], F32)
    wdq = dt("wdq", [H, CD], F32)
    wkr = dt("wkr", [H, NH * DR], F32)
    wuk = dt("wuk", [CD, UP], F32)
    wuv = dt("wuv", [CD, NH * DH], F32)
    wuq = dt("wuq", [CD, UP], F32)
    wqr = dt("wqr", [UP, NH * DR], F32)
    wout = dt("wout", [NH * DH, H], F32)
    gw = dt("gw", [H, 16], F32)
    gu16 = dt("gu16", [17, 6, 128, 2048], U16)
    gsc = dt("gsc", [17, 6, 4, 2048], F32)
    d16 = dt("d16", [17, 4, 128, 1536], U16)
    dsc = dt("dsc", [17, 4, 4, 1536], F32)
    cos_a = dt("cos_a", [S_all, 512], F32)
    sin_a = dt("sin_a", [S_all, 512], F32)
    identd = dt("identd", [128, 128], F32)
    ind4d = dt("ind4d", [4, 128], F32)

    out_h = nc.dram_tensor("out", [Tc, H], F32, kind="ExternalOutput").ap()

    di = lambda n, s, d: nc.dram_tensor(n, s, d, kind="Internal").ap()
    xn_sp = di("xn_sp", [NTTA, 128, H], BF16)
    ckv_sp = di("ckv_sp", [NTTA, 128, CD], BF16)
    cq_sp = di("cq_sp", [NTTA, 128, CD], BF16)
    v_sp = di("v_sp", [NTTA, 128, NH * DH], BF16)
    kf_sp = di("kf_sp", [NTT, 128, NH * 128], BF16)
    s_sp = di("s_sp", [NTT, 128, H], BF16)
    gacc = di("gacc", [4, NTT, 128, 512], F32)

    ctx = Ctx(nc)
    pr = Prog(nc, ctx)
    B = Buf
    es = contextlib.ExitStack()

    def sb(name, shape, dtyp):
        return es.enter_context(nc.sbuf_tensor(name, shape, dtyp))

    def ps_(name, shape, dtyp):
        return es.enter_context(nc.psum_tensor(name, shape, dtyp))

    ident = nc.alloc_sbuf_tensor("ident", [128, 128], BF16)
    ind4 = nc.alloc_sbuf_tensor("ind4", [4, 128], BF16)
    gw_sb = nc.alloc_sbuf_tensor("gw_sb", [128, 16 * 16], BF16)
    logit = nc.alloc_sbuf_tensor("logit", [128, NTT * 16], F32)
    cw = nc.alloc_sbuf_tensor("cw", [128, NTT * 16], F32)
    tau = nc.alloc_sbuf_tensor("tau", [128, NTT], F32)
    msb = nc.alloc_sbuf_tensor("msb", [128, NH * DH], BF16)
    neg = nc.alloc_sbuf_tensor("neg", [128, 16], F32)
    lw = nc.alloc_sbuf_tensor("lw", [128, 16], F32)
    sel = nc.alloc_sbuf_tensor("sel", [128, 16], F32)
    ela = nc.alloc_sbuf_tensor("ela", [128, 16], F32)
    mku = nc.alloc_sbuf_tensor("mku", [128, 16], U8)
    st8 = nc.alloc_sbuf_tensor("st8", [128, 8], F32)
    cst = nc.alloc_sbuf_tensor("cst", [128, 2], F32)

    (b_const, b_logit, b_cw, b_tau, b_msb, b_lw, b_sel, b_ela, b_mku,
     b_st8) = (B() for _ in range(10))
    (b_xnsp, b_ckvsp, b_cqsp, b_vsp, b_kfsp, b_ssp, b_gacc) = (B() for _ in range(7))

    def p0(q):
        q.dma("dcst", ident[:], identd)
        q.dma("dcst", ind4[:], ind4d)
        t = q.dma("dcst", gw_sb[:].rearrange("p (a n) -> p a n", n=16),
                  gw.rearrange("(a p) n -> p a n", p=128))
        b_const.owr(t)
        b_const.owp(q.op(q.eng.memset, neg[:], -1e30))
        b_const.owp(q.op(q.eng.memset, cst[:, 0:1], -0.5))
        b_const.owp(q.op(q.eng.memset, cst[:, 1:2], 1e-5))
    pr.add("g", p0)
    pr.flush("p0")

    def x_tile(gtt):
        if gtt < NTT:
            return x_loc[gtt * 128:(gtt + 1) * 128, :]
        return x_oth[(gtt - NTT) * 128:(gtt - NTT + 1) * 128, :]

    def fm_proj_steps(wbuf, b_w, inb, b_in, ob, b_o, ktn, ntile, mm2, b_m2,
                      extra=None):
        for half in range(0, ntile, 2):
            def tstep(q, half=half):
                for n in (half, half + 1):
                    if n >= ntile:
                        break
                    psl = n % 2
                    pairs = [(wbuf[:, (kt * ntile + n) * 128:(kt * ntile + n) * 128 + 128],
                              inb[:, kt * 128:(kt + 1) * 128]) for kt in range(ktn)]
                    t = q.mm_group(mm2[psl][:, 0:128], pairs,
                                   deps=b_in.w + b_m2[psl].wd() + b_w.w)
                    b_m2[psl].owr(t)
                    b_in.ord(t)
            pr.add("t", tstep)

            def vstep(q, half=half):
                for n in (half, half + 1):
                    if n >= ntile:
                        break
                    psl = n % 2
                    tv = q.op(q.eng.tensor_copy, ob[:, n * 128:(n + 1) * 128],
                              mm2[psl][:, 0:128],
                              deps=b_m2[psl].rd() + (b_o.wd() if n == 0 else []))
                    if n == 0:
                        b_o.owr(tv)
                    else:
                        b_o.owp(tv)
                    b_m2[psl].ord(tv)
            pr.add("v", vstep)
        if extra is not None:
            extra()

    def rope_steps(src, b_src, dstv, b_dst, csb, b_cs, ssbt, b_ss, rp1, b_r1,
                   rp2, b_r2):
        def f(q):
            sv = src[:].rearrange("p (h j) -> p h j", j=64)
            cv = csb[:].rearrange("p (h j) -> p h j", j=32)
            sv2 = ssbt[:].rearrange("p (h j) -> p h j", j=32)
            r1v = rp1[:].rearrange("p (h j) -> p h j", j=32)
            r2v = rp2[:].rearrange("p (h j) -> p h j", j=16)
            t0 = q.op(q.eng.tensor_copy, dstv[:, :, 32:64], sv[:, :, 32:64],
                      deps=b_src.w + b_cs.w + b_ss.w + b_dst.w + b_dst.r)
            b_dst.owp(t0)
            t1 = q.op(q.eng.tensor_tensor, r1v, sv[:, :, 0:32], cv, ALU.mult,
                      deps=b_r1.wd())
            b_r1.owr(t1)
            t2 = q.op(q.eng.tensor_tensor, r2v, sv[:, :, 16:32], sv2[:, :, 0:16],
                      ALU.mult, deps=b_r2.wd())
            b_r2.owr(t2)
            t3 = q.op(q.eng.tensor_tensor, dstv[:, :, 0:16], r1v[:, :, 0:16], r2v,
                      ALU.subtract, deps=[t1, t2])
            b_dst.owp(t3)
            b_r2.ord(t3)
            t4 = q.op(q.eng.tensor_tensor, r2v, sv[:, :, 0:16], sv2[:, :, 16:32],
                      ALU.mult, deps=b_r2.wd())
            b_r2.owr(t4)
            t5 = q.op(q.eng.tensor_tensor, dstv[:, :, 16:32], r1v[:, :, 16:32], r2v,
                      ALU.add, deps=[t4])
            b_dst.owp(t5)
            b_src.ord(t5)
            b_r1.ord(t5)
            b_r2.ord(t5)
            b_cs.ord(t5)
            b_ss.ord(t5)
        pr.add("v", f)

    with contextlib.ExitStack() as st_:
        sbp = lambda n, s, d: st_.enter_context(nc.sbuf_tensor(n, s, d))
        psp = lambda n, s, d: st_.enter_context(nc.psum_tensor(n, s, d))
        wA = sbp("wA", [128, 16 * CD], BF16)
        xt = [sbp(f"xt{i}", [128, H], F32) for i in range(2)]
        xnt = [sbp(f"xnt{i}", [128, H], BF16) for i in range(2)]
        sqs = sbp("sqs", [128, H], BF16)
        xnp = [sbp(f"xnp{i}", [128, H], BF16) for i in range(2)]
        ckvb = sbp("ckvb", [128, CD], BF16)
        pips = psp("pips", [128, 512], BF16)
        mm2 = [psp(f"mm2_{i}", [128, 128], F32) for i in range(2)]
        lgps = psp("lgps", [128, 16], F32)
        b_wA = B()
        b_xt = [B(), B()]
        b_xnt = [B(), B()]
        b_sq = B()
        b_xnp = [B(), B()]
        b_ckv = B()
        b_pips = B()
        b_m2 = [B(), B()]
        b_lg = B()

        def a1_w(q):
            b_wA.owr(q.dma("dw", wA[:].rearrange("p (a n) -> p a n", n=CD),
                           wdkv.rearrange("(a p) n -> p a n", p=128)))
        pr.add("g", a1_w)

        def mk_xload(gtt, sl):
            def f(q):
                t = q.dma(f"dx{sl}", xt[sl][:], x_tile(gtt), deps=b_xt[sl].wd())
                b_xt[sl].owr(t)
            return f
        pr.add("sy", mk_xload(0, 0))
        if NTTA > 1:
            pr.add("sy", mk_xload(1, 1))

        for gtt in range(NTTA):
            sl = gtt % 2

            def ln_v1(q, sl=sl):
                t = q.op(q.eng.reduce_sum, st8[:, 0:1], xt[sl][:], axis=AX.X,
                         deps=b_xt[sl].rd() + b_st8.wd())
                b_st8.owr(t)
                b_xt[sl].ord(t)
            pr.add("v", ln_v1)

            def ln_s1(q, sl=sl):
                t = q.op(q.eng.activation, sqs[:], xt[sl][:], AF.Square,
                         accum_out=st8[:, 1:2],
                         deps=b_xt[sl].rd() + b_sq.wd() + b_st8.w + b_st8.r)
                b_sq.owr(t)
                b_st8.owp(t)
                b_xt[sl].ord(t)
            pr.add("s", ln_s1)

            def ln_v2(q):
                t1 = q.op(q.eng.tensor_scalar, st8[:, 2:3], st8[:, 0:1], 1.0 / H,
                          None, ALU.mult, deps=b_st8.w)
                t2 = q.op(q.eng.tensor_scalar, st8[:, 3:4], st8[:, 2:3],
                          st8[:, 2:3], None, ALU.mult, deps=[t1])
                t3 = q.op(q.eng.scalar_tensor_tensor, st8[:, 4:5], st8[:, 1:2],
                          1.0 / H, st8[:, 3:4], ALU.mult, ALU.subtract, deps=[t2])
                b_st8.owp(t3)
            pr.add("v", ln_v2)

            def ln_s2(q):
                t1 = q.op(q.eng.activation, st8[:, 5:6], st8[:, 4:5], AF.Ln,
                          bias=cst[:, 1:2], deps=b_st8.w + b_const.w)
                t2 = q.op(q.eng.activation, st8[:, 6:7], st8[:, 5:6], AF.Exp,
                          scale=cst[:, 0:1], deps=[t1])
                b_st8.owp(t2)
            pr.add("s", ln_s2)

            def ln_v3(q, sl=sl):
                t = q.op(q.eng.tensor_scalar, xnt[sl][:], xt[sl][:], st8[:, 2:3],
                         st8[:, 6:7], ALU.subtract, ALU.mult,
                         deps=b_st8.w + b_xnt[sl].wd() + b_xt[sl].rd())
                b_xnt[sl].owr(t)
                b_xt[sl].ord(t)
                b_st8.ord(t)
            pr.add("v", ln_v3)

            if gtt + 2 < NTTA:
                pr.add("sy", mk_xload(gtt + 2, sl))

            for grp in range(4):
                def pi_t(q, sl=sl, grp=grp):
                    tk = None
                    for j in range(4):
                        tu = grp * 4 + j
                        u, k = tu // 4, tu % 4
                        tk = q.op(q.eng.transpose, pips[:, j * 128:(j + 1) * 128],
                                  xnt[sl][:, (512 * u + k):(512 * u + k + 509):4],
                                  ident[:],
                                  deps=(b_xnt[sl].w + b_pips.wd() + b_const.w)
                                  if j == 0 else (), inc=(1 if j == 3 else 0))
                    b_pips.owr(tk)
                    b_xnt[sl].ord(tk)
                pr.add("t", pi_t)

                def pi_v(q, sl=sl, grp=grp):
                    t = q.op(q.eng.tensor_copy, xnp[sl][:, grp * 512:(grp + 1) * 512],
                             pips[:],
                             deps=b_pips.rd() + (b_xnp[sl].wd() if grp == 0 else []))
                    if grp == 0:
                        b_xnp[sl].owr(t)
                    else:
                        b_xnp[sl].owp(t)
                    b_pips.ord(t)
                pr.add("v", pi_v)

            def xn_spill(q, gtt=gtt, sl=sl):
                t = q.dma(f"sxn{sl}", xn_sp[gtt], xnp[sl][:], deps=b_xnp[sl].w)
                b_xnp[sl].ord(t)
                b_xnsp.owp(t)
            pr.add("sy", xn_spill)

            for half in range(0, 8, 2):
                def ckv_t(q, sl=sl, half=half):
                    for n in (half, half + 1):
                        psl = n % 2
                        pairs = [(wA[:, (kt * 8 + n) * 128:(kt * 8 + n) * 128 + 128],
                                  xnp[sl][:, kt * 128:(kt + 1) * 128])
                                 for kt in range(16)]
                        t = q.mm_group(mm2[psl][:], pairs,
                                       deps=b_xnp[sl].w + b_m2[psl].wd() + b_wA.w)
                        b_m2[psl].owr(t)
                        b_xnp[sl].ord(t)
                pr.add("t", ckv_t)

                def ckv_v(q, half=half):
                    for n in (half, half + 1):
                        psl = n % 2
                        tv = q.op(q.eng.tensor_copy, ckvb[:, n * 128:(n + 1) * 128],
                                  mm2[psl][:],
                                  deps=b_m2[psl].rd() + (b_ckv.wd() if n == 0 else []))
                        if n == 0:
                            b_ckv.owr(tv)
                        else:
                            b_ckv.owp(tv)
                        b_m2[psl].ord(tv)
                pr.add("v", ckv_v)

            if gtt < NTT:
                def lg_t(q, gtt=gtt, sl=sl):
                    pairs = [(xnp[sl][:, kt * 128:(kt + 1) * 128],
                              gw_sb[:, kt * 16:(kt + 1) * 16]) for kt in range(16)]
                    t = q.mm_group(lgps[:], pairs,
                                   deps=b_xnp[sl].w + b_lg.wd() + b_const.w)
                    b_lg.owr(t)
                    b_xnp[sl].ord(t)
                pr.add("t", lg_t)

                def lg_v(q, gtt=gtt):
                    tv = q.op(q.eng.tensor_copy, logit[:, gtt * 16:(gtt + 1) * 16],
                              lgps[:], deps=b_lg.rd())
                    b_logit.owp(tv)
                    b_lg.ord(tv)
                pr.add("v", lg_v)

            def ckv_spill(q, gtt=gtt):
                t = q.dma("scv", ckv_sp[gtt], ckvb[:], deps=b_ckv.w)
                b_ckv.ord(t)
                b_ckvsp.owp(t)
            pr.add("sy", ckv_spill)

        def top4_v1(q):
            for tt in range(NTT):
                t0 = q.op(q.eng.tensor_copy, lw[:], logit[:, tt * 16:(tt + 1) * 16],
                          deps=b_logit.w + b_lw.wd())
                b_lw.owr(t0)
                b_logit.ord(t0)
                for r in range(4):
                    dst = st8[:, 0:1] if r < 3 else tau[:, tt:tt + 1]
                    t1 = q.op(q.eng.reduce_max, dst, lw[:], axis=AX.X,
                              deps=[t0] + (b_st8.wd() if r < 3 else b_tau.wd()))
                    if r < 3:
                        b_st8.owr(t1)
                        t2 = q.op(q.eng.tensor_scalar, mku[:], lw[:], dst, None,
                                  ALU.is_ge, deps=[t1] + b_mku.wd())
                        b_mku.owr(t2)
                        t0 = q.op(q.eng.copy_predicated, lw[:], mku[:], neg[:],
                                  deps=[t2] + b_const.w)
                        b_lw.owr(t0)
                    else:
                        b_tau.owp(t1)
                        b_lw.ord(t1)
        pr.add("v", top4_v1)

        def top4_pair(tt):
            def fs(q, tt=tt):
                t = q.op(q.eng.activation, ela[:], logit[:, tt * 16:(tt + 1) * 16],
                         AF.Exp, deps=b_logit.w + b_ela.wd() + b_tau.w)
                b_ela.owr(t)
            def fv(q, tt=tt):
                t1 = q.op(q.eng.tensor_scalar, sel[:], logit[:, tt * 16:(tt + 1) * 16],
                          tau[:, tt:tt + 1], None, ALU.is_ge,
                          deps=b_tau.w + b_logit.w + b_sel.wd())
                b_sel.owr(t1)
                t2 = q.op(q.eng.tensor_tensor, sel[:], ela[:], sel[:], ALU.mult,
                          deps=[t1] + b_ela.w)
                b_sel.owr(t2)
                b_ela.ord(t2)
                t3 = q.op(q.eng.reduce_sum, st8[:, 0:1], sel[:], axis=AX.X,
                          deps=[t2] + b_st8.wd())
                t4 = q.op(q.eng.reciprocal, st8[:, 1:2], st8[:, 0:1], deps=[t3])
                b_st8.owr(t4)
                t5 = q.op(q.eng.tensor_scalar, cw[:, tt * 16:(tt + 1) * 16], sel[:],
                          st8[:, 1:2], None, ALU.mult, deps=[t4])
                b_cw.owp(t5)
                b_sel.ord(t5)
                b_st8.ord(t5)
            return fs, fv
        for tt in range(NTT):
            fs, fv = top4_pair(tt)
            pr.add("s", fs)
            pr.add("v", fv)
        pr.flush("a1")

    with contextlib.ExitStack() as st_:
        sbp = lambda n, s, d: st_.enter_context(nc.sbuf_tensor(n, s, d))
        psp = lambda n, s, d: st_.enter_context(nc.psum_tensor(n, s, d))
        wA2 = sbp("wA2", [128, 16 * CD], BF16)
        inb = [sbp(f"a2in{i}", [128, H], BF16) for i in range(2)]
        ob = sbp("a2ob", [128, CD], BF16)
        mm2 = [psp(f"a2mm{i}", [128, 128], F32) for i in range(2)]
        b_w, b_in, b_o = B(), [B(), B()], B()
        b_m2 = [B(), B()]

        def a2_w(q):
            b_w.owr(q.dma("dw", wA2[:].rearrange("p (a n) -> p a n", n=CD),
                          wdq.rearrange("(a p) n -> p a n", p=128)))
        pr.add("g", a2_w)

        for gtt in range(NTTA):
            sl = gtt % 2

            def iload(q, gtt=gtt, sl=sl):
                t = q.dma(f"da2{sl}", inb[sl][:], xn_sp[gtt],
                          deps=b_in[sl].wd() + b_xnsp.w)
                b_in[sl].owr(t)
            pr.add("sy", iload)
            fm_proj_steps(wA2, b_w, inb[sl], b_in[sl], ob, b_o, 16, 8, mm2, b_m2)

            def ospill(q, gtt=gtt):
                t = q.dma("scq", cq_sp[gtt], ob[:], deps=b_o.w)
                b_o.ord(t)
                b_cqsp.owp(t)
            pr.add("sy", ospill)
        pr.flush("a2")

    with contextlib.ExitStack() as st_:
        sbp = lambda n, s, d: st_.enter_context(nc.sbuf_tensor(n, s, d))
        psp = lambda n, s, d: st_.enter_context(nc.psum_tensor(n, s, d))
        wA3 = sbp("wA3", [128, 16 * 1024], BF16)
        inb = [sbp(f"a3in{i}", [128, H], BF16) for i in range(2)]
        krt = sbp("krt", [128, 1024], BF16)
        krot = sbp("krot", [128, 1024], BF16)
        csb = sbp("csb", [128, 512], BF16)
        snb = sbp("snb", [128, 512], BF16)
        rp1 = sbp("rp1", [128, 512], BF16)
        rp2 = sbp("rp2", [128, 256], BF16)
        kch = sbp("kch", [128, NH * 128], BF16)
        p5 = [psp(f"a3p5{i}", [128, 512], F32) for i in range(2)]
        tpsb = psp("a3tps", [128, 512], BF16)
        b_w, b_in = B(), [B(), B()]
        b_krt, b_krot, b_cs, b_ss, b_r1, b_r2, b_kch, b_tps = (B() for _ in range(8))
        b_p5 = [B(), B()]

        def a3_w(q):
            b_w.owr(q.dma("dw", wA3[:].rearrange("p (a n) -> p a n", n=1024),
                          wkr.rearrange("(a p) n -> p a n", p=128)))
        pr.add("g", a3_w)

        for gtt in range(NTT):
            sl = gtt % 2

            def iload(q, gtt=gtt, sl=sl):
                t = q.dma(f"da3{sl}", inb[sl][:], xn_sp[gtt],
                          deps=b_in[sl].wd() + b_xnsp.w)
                b_in[sl].owr(t)
            pr.add("sy", iload)

            def csload(q, gtt=gtt):
                q.dma("da3c", csb[:], cos_a[gtt * 128:(gtt + 1) * 128, :],
                      deps=b_cs.wd() + b_ss.wd())
                ts_ = q.dma("da3c", snb[:], sin_a[gtt * 128:(gtt + 1) * 128, :])
                b_cs.owr(ts_)
                b_ss.owr(ts_)
            pr.add("g", csload)

            def a3_t(q, sl=sl):
                for n in range(2):
                    pairs = [(inb[sl][:, kt * 128:(kt + 1) * 128],
                              wA3[:, kt * 1024 + n * 512: kt * 1024 + n * 512 + 512])
                             for kt in range(16)]
                    t = q.mm_group(p5[n][:], pairs,
                                   deps=b_in[sl].w + b_p5[n].wd() + b_w.w)
                    b_p5[n].owr(t)
                    b_in[sl].ord(t)
            pr.add("t", a3_t)

            def a3_v(q):
                for n in range(2):
                    tv = q.op(q.eng.tensor_copy, krt[:, n * 512:(n + 1) * 512],
                              p5[n][:],
                              deps=b_p5[n].rd() + (b_krt.wd() if n == 0 else []))
                    if n == 0:
                        b_krt.owr(tv)
                    else:
                        b_krt.owp(tv)
                    b_p5[n].ord(tv)
            pr.add("v", a3_v)

            krotv = krot[:].rearrange("p (h j) -> p h j", j=64)
            rope_steps(krt, b_krt, krotv, b_krot, csb, b_cs, snb, b_ss,
                       rp1, b_r1, rp2, b_r2)

            for hg in range(4):
                def a3_tr(q, hg=hg):
                    tk = None
                    for j in range(4):
                        h = hg * 4 + j
                        tk = q.op(q.eng.transpose,
                                  tpsb[64:128, j * 128:(j + 1) * 128],
                                  krot[:, h * 64:(h + 1) * 64], ident[:],
                                  tile_position=(0, 64),
                                  deps=(b_krot.w + b_tps.wd() + b_const.w)
                                  if j == 0 else (), inc=(1 if j == 3 else 0))
                    b_tps.owr(tk)
                    b_krot.ord(tk)
                pr.add("t", a3_tr)

                def a3_ev(q, hg=hg):
                    t = q.op(q.eng.tensor_copy,
                             kch[64:128, hg * 512:(hg + 1) * 512], tpsb[64:128, :],
                             deps=b_tps.rd() + (b_kch.wd() if hg == 0 else []))
                    if hg == 0:
                        b_kch.owr(t)
                    else:
                        b_kch.owp(t)
                    b_tps.ord(t)
                pr.add("v", a3_ev)

            def a3_spill(q, gtt=gtt):
                t = q.dma("skf", kf_sp[gtt, 64:128, :], kch[64:128, :],
                          deps=b_kch.w)
                b_kch.ord(t)
                b_kfsp.owp(t)
            pr.add("sy", a3_spill)
        pr.flush("a3")

    with contextlib.ExitStack() as st_:
        sbp = lambda n, s, d: st_.enter_context(nc.sbuf_tensor(n, s, d))
        psp = lambda n, s, d: st_.enter_context(nc.psum_tensor(n, s, d))
        wB1 = sbp("wB1", [128, 8 * 2048], BF16)
        cin = [sbp(f"b1in{i}", [128, CD], BF16) for i in range(2)]
        vtb = sbp("vtb", [128, 2048], BF16)
        p5 = [psp(f"b1p5{i}", [128, 512], F32) for i in range(2)]
        b_w, b_in, b_vt = B(), [B(), B()], B()
        b_p5 = [B(), B()]

        def b1_w(q):
            b_w.owr(q.dma("dw", wB1[:].rearrange("p (a n) -> p a n", n=2048),
                          wuv.rearrange("(a p) n -> p a n", p=128)))
        pr.add("g", b1_w)

        for gtt in range(NTTA):
            sl = gtt % 2

            def iload(q, gtt=gtt, sl=sl):
                t = q.dma(f"db1{sl}", cin[sl][:], ckv_sp[gtt],
                          deps=b_in[sl].wd() + b_ckvsp.w)
                b_in[sl].owr(t)
            pr.add("sy", iload)

            for half in range(0, 4, 2):
                def b1_t(q, sl=sl, half=half):
                    for n in (half, half + 1):
                        psl = n % 2
                        pairs = [(cin[sl][:, kt * 128:(kt + 1) * 128],
                                  wB1[:, kt * 2048 + n * 512: kt * 2048 + n * 512 + 512])
                                 for kt in range(8)]
                        t = q.mm_group(p5[psl][:], pairs,
                                       deps=b_in[sl].w + b_p5[psl].wd() + b_w.w)
                        b_p5[psl].owr(t)
                        b_in[sl].ord(t)
                pr.add("t", b1_t)

                def b1_v(q, half=half):
                    for n in (half, half + 1):
                        psl = n % 2
                        tv = q.op(q.eng.tensor_copy, vtb[:, n * 512:(n + 1) * 512],
                                  p5[psl][:],
                                  deps=b_p5[psl].rd() + (b_vt.wd() if n == 0 else []))
                        if n == 0:
                            b_vt.owr(tv)
                        else:
                            b_vt.owp(tv)
                        b_p5[psl].ord(tv)
                pr.add("v", b1_v)

            def vspill(q, gtt=gtt):
                t = q.dma("sv", v_sp[gtt], vtb[:], deps=b_vt.w)
                b_vt.ord(t)
                b_vsp.owp(t)
            pr.add("sy", vspill)
        pr.flush("b1")

    with contextlib.ExitStack() as st_:
        sbp = lambda n, s, d: st_.enter_context(nc.sbuf_tensor(n, s, d))
        psp = lambda n, s, d: st_.enter_context(nc.psum_tensor(n, s, d))
        wQ = sbp("wQ", [128, 8 * 1024], BF16)
        wR = sbp("wR", [128, 8 * 1024], BF16)
        cin = [sbp(f"b2in{i}", [128, CD], BF16) for i in range(2)]
        vin = [sbp(f"vin{i}", [128, 2048], BF16) for i in range(2)]
        qcf = sbp("qcf", [128, 1024], BF16)
        qrt = sbp("qrt", [128, 1024], BF16)
        qtb = sbp("qtb", [128, 2048], BF16)
        csb = sbp("b2cs", [128, 512], BF16)
        snb = sbp("b2sn", [128, 512], BF16)
        rp1 = sbp("b2r1", [128, 512], BF16)
        rp2 = sbp("b2r2", [128, 256], BF16)
        mm2 = [psp(f"b2mm{i}", [128, 128], F32) for i in range(2)]
        p5 = [psp(f"b2p5{i}", [128, 512], F32) for i in range(2)]
        mps = psp("mps", [128, 2048], F32)
        b_wq, b_wr = B(), B()
        b_in, b_vin = [B(), B()], [B(), B()]
        b_qcf, b_qrt, b_qt, b_cs, b_ss, b_r1, b_r2, b_mps = (B() for _ in range(8))
        b_m2 = [B(), B()]
        b_p5 = [B(), B()]

        def b2_w(q):
            q.dma("dw", wQ[:].rearrange("p (a n) -> p a n", n=1024),
                  wuq.rearrange("(a p) n -> p a n", p=128))
            t = q.dma("dw", wR[:].rearrange("p (a n) -> p a n", n=1024),
                      wqr.rearrange("(a p) n -> p a n", p=128))
            b_wq.owr(t)
            b_wr.owr(t)
        pr.add("g", b2_w)

        for gtt in range(NTTA):
            sl = gtt % 2

            def iload(q, gtt=gtt, sl=sl):
                t = q.dma(f"db2{sl}", cin[sl][:], cq_sp[gtt],
                          deps=b_in[sl].wd() + b_cqsp.w)
                b_in[sl].owr(t)
                tv = q.dma(f"db2v{sl}", vin[sl][:], v_sp[gtt],
                           deps=b_vin[sl].wd() + b_vsp.w)
                b_vin[sl].owr(tv)
            pr.add("sy", iload)

            def csload(q, gtt=gtt):
                q.dma("db2c", csb[:], cos_a[gtt * 128:(gtt + 1) * 128, :],
                      deps=b_cs.wd() + b_ss.wd())
                ts_ = q.dma("db2c", snb[:], sin_a[gtt * 128:(gtt + 1) * 128, :])
                b_cs.owr(ts_)
                b_ss.owr(ts_)
            pr.add("g", csload)

            fm_proj_steps(wQ, b_wq, cin[sl], b_in[sl], qcf, b_qcf, 8, 8, mm2, b_m2)

            def qr_t(q):
                for n in range(2):
                    pairs = [(qcf[:, kt * 128:(kt + 1) * 128],
                              wR[:, kt * 1024 + n * 512: kt * 1024 + n * 512 + 512])
                             for kt in range(8)]
                    t = q.mm_group(p5[n][:], pairs,
                                   deps=b_qcf.w + b_p5[n].wd() + b_wr.w)
                    b_p5[n].owr(t)
                    b_qcf.ord(t)
            pr.add("t", qr_t)

            def qr_v(q):
                for n in range(2):
                    tv = q.op(q.eng.tensor_copy, qrt[:, n * 512:(n + 1) * 512],
                              p5[n][:],
                              deps=b_p5[n].rd() + (b_qrt.wd() if n == 0 else []))
                    if n == 0:
                        b_qrt.owr(tv)
                    else:
                        b_qrt.owp(tv)
                    b_p5[n].ord(tv)
            pr.add("v", qr_v)

            def qc_t(q, sl=sl):
                for n in range(2):
                    pairs = [(cin[sl][:, kt * 128:(kt + 1) * 128],
                              wQ[:, kt * 1024 + n * 512: kt * 1024 + n * 512 + 512])
                             for kt in range(8)]
                    t = q.mm_group(p5[n][:], pairs,
                                   deps=b_in[sl].w + b_p5[n].wd() + b_wq.w)
                    b_p5[n].owr(t)
                    b_in[sl].ord(t)
            pr.add("t", qc_t)

            def qc_v(q):
                for n in range(2):
                    src = p5[n][:].rearrange("p (h j) -> p h j", j=64)
                    dstv = qtb[:].rearrange("p (h j) -> p h j", j=128)[:, n * 8:(n + 1) * 8, 0:64]
                    tv = q.op(q.eng.tensor_copy, dstv, src,
                              deps=b_p5[n].rd() + (b_qt.wd() if n == 0 else []))
                    if n == 0:
                        b_qt.owr(tv)
                    else:
                        b_qt.owp(tv)
                    b_p5[n].ord(tv)
            pr.add("v", qc_v)

            qtv = qtb[:].rearrange("p (h j) -> p h j", j=128)[:, :, 64:128]
            rope_steps(qrt, b_qrt, qtv, b_qt, csb, b_cs, snb, b_ss, rp1, b_r1,
                       rp2, b_r2)

            def b2_m(q, gtt=gtt, sl=sl):
                t = None
                for h in range(NH):
                    t = q.mm_group(mps[:, h * 128:(h + 1) * 128],
                                   [(qtb[:, h * 128:(h + 1) * 128],
                                     vin[sl][:, h * 128:(h + 1) * 128])],
                                   deps=(b_qt.w + b_vin[sl].w +
                                         (b_mps.wd() if gtt == 0 else []))
                                   if h == 0 else (),
                                   start=(gtt == 0 and h % 4 == 0),
                                   stop=(gtt == NTTA - 1 and h % 4 == 3))
                b_mps.owp(t)
                b_qt.ord(t)
                b_vin[sl].ord(t)
            pr.add("t", b2_m)

        def b2_mev(q):
            t = q.op(q.eng.tensor_scalar, msb[:], mps[:], RSC, None, ALU.mult,
                     deps=b_mps.w + b_msb.wd())
            b_msb.owr(t)
            b_mps.ord(t)
        pr.add("v", b2_mev)
        pr.flush("b2")

    with contextlib.ExitStack() as st_:
        sbp = lambda n, s, d: st_.enter_context(nc.sbuf_tensor(n, s, d))
        psp = lambda n, s, d: st_.enter_context(nc.psum_tensor(n, s, d))
        wB3 = sbp("wB3", [128, 8 * 1024], BF16)
        cin = [sbp(f"b3in{i}", [128, CD], BF16) for i in range(2)]
        kch = sbp("b3kch", [128, NH * 128], BF16)
        p5 = [psp(f"b3p5{i}", [128, 512], F32) for i in range(2)]
        b_w, b_in, b_kch = B(), [B(), B()], B()
        b_p5 = [B(), B()]

        def b3_w(q):
            b_w.owr(q.dma("dw", wB3[:].rearrange("p (a n) -> p a n", n=1024),
                          wuk.rearrange("(a p) n -> p a n", p=128)))
        pr.add("g", b3_w)

        for gtt in range(NTT):
            sl = gtt % 2

            def iload(q, gtt=gtt, sl=sl):
                t = q.dma(f"db3{sl}", cin[sl][:], ckv_sp[gtt],
                          deps=b_in[sl].wd() + b_ckvsp.w)
                b_in[sl].owr(t)
            pr.add("sy", iload)

            for hg in range(4):
                def b3_t(q, sl=sl, hg=hg):
                    psl = hg % 2
                    tk = None
                    for j in range(4):
                        h = hg * 4 + j
                        pairs = [(wB3[:, kt * 1024 + h * 64: kt * 1024 + h * 64 + 64],
                                  cin[sl][:, kt * 128:(kt + 1) * 128])
                                 for kt in range(8)]
                        tk = q.mm_group(p5[psl][0:64, j * 128:(j + 1) * 128], pairs,
                                        deps=(b_in[sl].w + b_p5[psl].wd() + b_w.w)
                                        if j == 0 else ())
                    b_p5[psl].owr(tk)
                    b_in[sl].ord(tk)
                pr.add("t", b3_t)

                def b3_v(q, hg=hg):
                    psl = hg % 2
                    t = q.op(q.eng.tensor_copy, kch[0:64, hg * 512:(hg + 1) * 512],
                             p5[psl][0:64, :],
                             deps=b_p5[psl].rd() + (b_kch.wd() if hg == 0 else []))
                    if hg == 0:
                        b_kch.owr(t)
                    else:
                        b_kch.owp(t)
                    b_p5[psl].ord(t)
                pr.add("v", b3_v)

            def b3_spill(q, gtt=gtt):
                t = q.dma("skf2", kf_sp[gtt, 0:64, :], kch[0:64, :], deps=b_kch.w)
                b_kch.ord(t)
                b_kfsp.owp(t)
            pr.add("sy", b3_spill)
        pr.flush("b3")

    with contextlib.ExitStack() as st_:
        sbp = lambda n, s, d: st_.enter_context(nc.sbuf_tensor(n, s, d))
        psp = lambda n, s, d: st_.enter_context(nc.psum_tensor(n, s, d))
        wO = sbp("wO", [128, 16 * 2048], BF16)
        kin = [sbp(f"kin{i}", [128, 2048], BF16) for i in range(2)]
        outf = sbp("outf", [128, 2048], BF16)
        s_st = sbp("s_st", [128, 2048], BF16)
        p5 = [psp(f"p2p5{i}", [128, 512], F32) for i in range(2)]
        b_w, b_kin, b_of, b_sst = B(), [B(), B()], B(), B()
        b_p5 = [B(), B()]

        def p2_w(q):
            b_w.owr(q.dma("dw", wO[:].rearrange("p (a n) -> p a n", n=2048),
                          wout.rearrange("(a p) n -> p a n", p=128)))
        pr.add("g", p2_w)

        for gtt in range(NTT):
            sl = gtt % 2

            def iload(q, gtt=gtt, sl=sl):
                t = q.dma(f"dp2{sl}", kin[sl][:], kf_sp[gtt],
                          deps=b_kin[sl].wd() + b_kfsp.w)
                b_kin[sl].owr(t)
            pr.add("sy", iload)

            for hg in range(4):
                def of_t(q, sl=sl, hg=hg):
                    psl = hg % 2
                    tk = None
                    for j in range(4):
                        h = hg * 4 + j
                        tk = q.mm_group(p5[psl][:, j * 128:(j + 1) * 128],
                                        [(msb[:, h * 128:(h + 1) * 128],
                                          kin[sl][:, h * 128:(h + 1) * 128])],
                                        deps=(b_msb.w + b_kin[sl].w + b_p5[psl].wd())
                                        if j == 0 else ())
                    b_p5[psl].owr(tk)
                    b_kin[sl].ord(tk)
                pr.add("t", of_t)

                def of_v(q, hg=hg):
                    psl = hg % 2
                    t = q.op(q.eng.tensor_copy, outf[:, hg * 512:(hg + 1) * 512],
                             p5[psl][:],
                             deps=b_p5[psl].rd() + (b_of.wd() if hg == 0 else []))
                    if hg == 0:
                        b_of.owr(t)
                    else:
                        b_of.owp(t)
                    b_p5[psl].ord(t)
                pr.add("v", of_v)

            for half in range(0, 4, 2):
                def s_t(q, half=half):
                    for n in (half, half + 1):
                        psl = n % 2
                        pairs = [(outf[:, kt * 128:(kt + 1) * 128],
                                  wO[:, kt * 2048 + n * 512: kt * 2048 + n * 512 + 512])
                                 for kt in range(16)]
                        t = q.mm_group(p5[psl][:], pairs,
                                       deps=b_of.w + b_p5[psl].wd() + b_w.w)
                        b_p5[psl].owr(t)
                        b_of.ord(t)
                pr.add("t", s_t)

                def s_v(q, half=half):
                    for n in (half, half + 1):
                        psl = n % 2
                        tv = q.op(q.eng.tensor_copy, s_st[:, n * 512:(n + 1) * 512],
                                  p5[psl][:],
                                  deps=b_p5[psl].rd() + (b_sst.wd() if n == 0 else []))
                        if n == 0:
                            b_sst.owr(tv)
                        else:
                            b_sst.owp(tv)
                        b_p5[psl].ord(tv)
                pr.add("v", s_v)

            def sspill(q, gtt=gtt):
                t = q.dma("ssp", s_sp[gtt], s_st[:], deps=b_sst.w)
                b_sst.ord(t)
                b_ssp.owp(t)
            pr.add("sy", sspill)
        pr.flush("p2")

    with contextlib.ExitStack() as st_:
        sbp = lambda n, s, d: st_.enter_context(nc.sbuf_tensor(n, s, d))
        psp = lambda n, s, d: st_.enter_context(nc.psum_tensor(n, s, d))
        xn_f = sbp("xn_f", [128, 16 * Tc], BF16)
        act = sbp("act", [128, 12 * Tc], BF16)
        u16b = [sbp(f"u16b{i}", [128, 4 * 512], U16) for i in range(2)]
        wdq_ = [sbp(f"wdq{i}", [128, 16 * 512], BF16) for i in range(2)]
        sclb = sbp("sclb", [128, 4 * 512], BF16)
        gss = [sbp(f"gss{i}", [4, 4 * 512], BF16) for i in range(2)]
        t7b = sbp("t7b", [128, 512], U16)
        mbb = [sbp(f"mbb{i}", [128, 512], U16) for i in range(4)]
        sgb = [sbp(f"sgb{i}", [128, 512], U16) for i in range(4)]
        pcb = [sbp(f"pcb{i}", [128, 512], BF16) for i in range(4)]
        stg = [sbp(f"stg{i}", [128, G4 * 512], F32) for i in range(2)]
        p5 = [psp(f"p3p5{i}", [128, 512], F32) for i in range(2)]
        sclp = psp("sclp", [128, 2048], F32)
        b_xnf, b_act, b_scl = (B() for _ in range(3))
        b_u16 = [B(), B()]
        b_gss = [B(), B()]
        b_wd = [B(), B()]
        b_t7 = B()
        b_mb = [B() for _ in range(4)]
        b_sg = [B() for _ in range(4)]
        b_pc = [B() for _ in range(4)]
        b_stg = [B(), B()]
        b_sclp = B()
        b_p5 = [B(), B()]

        def ffn_xn(q):
            xv = xn_f[:].rearrange("p (a n) -> p a n", n=Tc)
            for tt in range(NTT):
                t = q.dma("dxnf", xv[:, :, tt * 128:(tt + 1) * 128],
                          xn_sp[tt].rearrange("p (a n) -> p a n", n=128),
                          deps=(b_xnf.wd() if tt == 0 else []) + b_xnsp.w)
                b_xnf.owp(t)
        pr.add("sy", ffn_xn)

        order_g = [0, 3, 1, 4, 2, 5]
        actv = act[:].rearrange("p (a n) -> p a n", n=Tc)
        xnv = xn_f[:].rearrange("p (a n) -> p a n", n=Tc)

        def deq_chunks(nu, slot):
            wd = wdq_[slot]
            bwd = b_wd[slot]
            ub_ = u16b[slot]
            bu = b_u16[slot]
            gs_ = gss[slot]
            bgs = b_gss[slot]

            def head():
                def f_sc(q):
                    tk = None
                    for u in range(nu):
                        tk = q.mm_group(sclp[:, u * 512:(u + 1) * 512],
                                        [(ind4[:], gs_[0:4, u * 512:(u + 1) * 512])],
                                        deps=(b_sclp.wd() + bgs.w + b_const.w)
                                        if u == 0 else ())
                    b_sclp.owr(tk)
                    bgs.ord(tk)
                pr.add("t", f_sc)

                def f_scv(q):
                    t = q.op(q.eng.tensor_copy, sclb[:, 0:nu * 512],
                             sclp[:, 0:nu * 512], deps=b_sclp.rd() + b_scl.wd())
                    b_scl.owr(t)
                    b_sclp.ord(t)
                pr.add("v", f_scv)

            def uchunk(u):
                tgk = [None] * 4

                def f_v1(q, u=u):
                    ub = ub_[:, u * 512:(u + 1) * 512]
                    for k in range(4):
                        t1 = q.op(q.eng.tensor_scalar, t7b[:], ub, 4 * k, 7,
                                  ALU.logical_shift_right, ALU.bitwise_and,
                                  deps=bu.w + b_t7.wd())
                        b_t7.owr(t1)
                        bu.ord(t1)
                        t2 = q.op(q.eng.tensor_scalar, mbb[k][:], t7b[:], 64, 16128,
                                  ALU.mult, ALU.add, deps=[t1] + b_mb[k].wd())
                        b_mb[k].owr(t2)
                        b_t7.ord(t2)
                        t3 = q.op(q.eng.tensor_scalar, sgb[k][:], ub, 12 - 4 * k,
                                  32768, ALU.logical_shift_left, ALU.bitwise_and,
                                  deps=b_sg[k].wd())
                        b_sg[k].owr(t3)
                        bu.ord(t3)
                pr.add("v", f_v1)

                def f_s(q, u=u):
                    for k in range(4):
                        t = q.op(q.eng.activation, pcb[k][:],
                                 mbb[k][:].bitcast(BF16), AF.Relu, bias=1.0,
                                 scale=-1.0, deps=b_mb[k].w + b_pc[k].wd())
                        b_pc[k].owr(t)
                        b_mb[k].ord(t)
                pr.add("s", f_s)

                def f_v2(q, u=u):
                    for k in range(4):
                        kt = 4 * u + k
                        wsl = wd[:, kt * 512:(kt + 1) * 512]
                        t1 = q.op(q.eng.tensor_tensor, wsl, mbb[k][:].bitcast(BF16),
                                  pcb[k][:], ALU.subtract,
                                  deps=b_pc[k].w + b_mb[k].w +
                                  (bwd.wd() if (u == 0 and k == 0) else []))
                        t2 = q.op(q.eng.tensor_tensor, wsl.bitcast(U16),
                                  wsl.bitcast(U16), sgb[k][:], ALU.bitwise_or,
                                  deps=[t1] + b_sg[k].w)
                        t3 = q.op(q.eng.tensor_tensor, wsl, wsl,
                                  sclb[:, u * 512:(u + 1) * 512], ALU.mult,
                                  deps=[t2] + b_scl.w)
                        if u == 0 and k == 0:
                            bwd.owr(t1)
                        bwd.owp(t3)
                        b_pc[k].ord(t3)
                        b_sg[k].ord(t3)
                        b_mb[k].ord(t3)
                        b_scl.ord(t3)
                pr.add("v", f_v2)

            return [head] + [(lambda u=u: uchunk(u)) for u in range(nu)]

        allblocks = [(e, kind, idx) for e in range(17)
                     for (kind, idx) in ([("gu", nb) for nb in order_g]
                                         + [("dn", hb) for hb in range(4)])]
        NB = len(allblocks)

        def emit_load(bi):
            e, kind, idx = allblocks[bi]
            slot = bi % 2
            ub_, bu = u16b[slot], b_u16[slot]
            gs_, bgs = gss[slot], b_gss[slot]
            if kind == "gu":
                def gu_load(q, e=e, nb=idx):
                    t = q.dma("du", ub_[:], gu16[e, nb], deps=bu.wd())
                    bu.owr(t)

                def gss_load(q, e=e, nb=idx):
                    tg = q.dma("dgs", gs_[:], gsc[e, nb], deps=bgs.wd())
                    bgs.owr(tg)
            else:
                def gu_load(q, e=e, hb=idx):
                    t = q.dma("du", ub_[:, 0:3 * 512], d16[e, hb],
                              deps=bu.wd())
                    bu.owr(t)

                def gss_load(q, e=e, hb=idx):
                    tg = q.dma("dgs", gs_[0:4, 0:3 * 512], dsc[e, hb],
                               deps=bgs.wd())
                    bgs.owr(tg)
            pr.add("sy", gu_load)
            pr.add("g", gss_load)

        def emit_mm(bi, deq):
            e, kind, idx = allblocks[bi]
            slot = bi % 2
            wd = wdq_[slot]
            bwd = b_wd[slot]
            steps = []
            if kind == "gu":
                nb = idx
                is_up = nb >= 3
                jbase = (nb - 3) * 4 if is_up else nb * 4
                for ntl in range(4):
                    j = jbase + ntl
                    for tc0 in range(0, TNC, 2):
                        tcs = [tc for tc in (tc0, tc0 + 1) if tc < TNC]

                        def step(ntl=ntl, tcs=tcs, j=j, is_up=is_up):
                            def gu_t(q):
                                for tc in tcs:
                                    psl = tc % 2
                                    pairs = [(wd[:, kt * 512 + ntl * 128:
                                                 kt * 512 + ntl * 128 + 128],
                                              xnv[:, kt, tc * TW:(tc + 1) * TW])
                                             for kt in range(16)]
                                    t = q.mm_group(p5[psl][:, 0:TW], pairs,
                                                   deps=bwd.w + b_xnf.w + b_p5[psl].wd())
                                    b_p5[psl].owr(t)
                                    bwd.ord(t)
                                    b_xnf.ord(t)
                            pr.add("t", gu_t)

                            if not is_up:
                                def gu_s(q):
                                    for tc in tcs:
                                        psl = tc % 2
                                        t = q.op(q.eng.activation,
                                                 actv[:, j, tc * TW:(tc + 1) * TW],
                                                 p5[psl][:, 0:TW], AF.Silu,
                                                 deps=b_p5[psl].rd() + b_act.w + b_act.r)
                                        b_act.owp(t)
                                        b_p5[psl].ord(t)
                                pr.add("s", gu_s)
                            else:
                                def gu_v(q):
                                    for tc in tcs:
                                        psl = tc % 2
                                        asl = actv[:, j, tc * TW:(tc + 1) * TW]
                                        t = q.op(q.eng.tensor_tensor, asl, asl,
                                                 p5[psl][:, 0:TW], ALU.mult,
                                                 deps=b_p5[psl].rd() + b_act.w)
                                        b_act.owp(t)
                                        b_p5[psl].ord(t)
                                pr.add("v", gu_v)
                        steps.append(step)
            else:
                hb = idx
                for tg in range(NTT // G4):
                    def step(tg=tg, hb=hb, e=e):
                        tis = list(range(G4))

                        def dn_t(q):
                            for ti in tis:
                                tt = tg * G4 + ti
                                psl = ti % 2
                                pairs = [(actv[:, kt, tt * 128:(tt + 1) * 128],
                                          wd[:, kt * 512:(kt + 1) * 512])
                                         for kt in range(12)]
                                t = q.mm_group(p5[psl][:], pairs,
                                               deps=b_act.w + bwd.w + b_p5[psl].wd())
                                b_p5[psl].owr(t)
                                bwd.ord(t)
                                b_act.ord(t)
                        pr.add("t", dn_t)

                        def dn_v(q):
                            ssl = tg % 2
                            for ti in tis:
                                tt = tg * G4 + ti
                                psl = ti % 2
                                dst = stg[ssl][:, ti * 512:(ti + 1) * 512]
                                if e < 16:
                                    t = q.op(q.eng.tensor_scalar, dst, p5[psl][:],
                                             cw[:, tt * 16 + e:tt * 16 + e + 1], None,
                                             ALU.mult,
                                             deps=b_p5[psl].rd() + (b_stg[ssl].wd() if ti == 0 else []) + b_cw.w)
                                else:
                                    t = q.op(q.eng.tensor_copy, dst, p5[psl][:],
                                             deps=b_p5[psl].rd() + (b_stg[ssl].wd() if ti == 0 else []))
                                b_stg[ssl].owr(t) if ti == 0 else b_stg[ssl].owp(t)
                                b_p5[psl].ord(t)
                        pr.add("v", dn_v)

                        def dn_acc(q):
                            ssl = tg % 2
                            kw = {} if e == 0 else {"accum_op": ALU.add}
                            gate = [Tok("ga", 16 * NACC * e)] if e > 0 else []
                            t = q.dma("ga",
                                      gacc[hb, tg * G4:(tg + 1) * G4]
                                      .rearrange("a p n -> p a n"),
                                      stg[ssl][:].rearrange("p (a n) -> p a n", n=512),
                                      deps=b_stg[ssl].w + gate, **kw)
                            b_stg[ssl].ord(t)
                            b_gacc.owp(t)
                        pr.add("g", dn_acc)
                    steps.append(step)

            if deq:
                deq[0]()
                nch = len(deq) - 1
                done = 0
                for i, st in enumerate(steps):
                    st()
                    want = min(nch, ((i + 2) * nch) // len(steps))
                    while done < want:
                        deq[1 + done]()
                        done += 1
                while done < nch:
                    deq[1 + done]()
                    done += 1
            else:
                for st in steps:
                    st()

        emit_load(0)
        emit_load(1)
        for c in deq_chunks(4 if allblocks[0][1] == "gu" else 3, 0):
            c()
        for bi in range(NB):
            if bi + 2 < NB:
                emit_load(bi + 2)
            deq = None
            if bi + 1 < NB:
                deq = deq_chunks(4 if allblocks[bi + 1][1] == "gu" else 3,
                                 (bi + 1) % 2)
            emit_mm(bi, deq)
        pr.flush("p3")
        t_ga = Tok("ga", ctx.cnt.get("ga", 0))

    with contextlib.ExitStack() as st_:
        sbp = lambda n, s, d: st_.enter_context(nc.sbuf_tensor(n, s, d))
        gac = sbp("gac", [128, H], F32)
        gsl = sbp("gsl", [128, H], BF16)
        s_in = sbp("s_in", [128, H], BF16)
        x_in = sbp("x_in", [128, H], F32)
        sgf = sbp("sgf", [128, H], F32)
        b_gac, b_gsl, b_sin, b_xin, b_sgf = (B() for _ in range(5))

        for tt in range(NTT):
            def p4_l(q, tt=tt):
                gv = gac[:].rearrange("p (a n) -> p a n", n=512)
                t1 = q.dma("dp4g", gv, gacc[:, tt].rearrange("a p n -> p a n"),
                           deps=b_gac.wd() + [t_ga])
                b_gac.owr(t1)
                t2 = q.dma("dp4s", s_in[:], s_sp[tt], deps=b_sin.wd() + b_ssp.w)
                b_sin.owr(t2)
                t3 = q.dma("dp4x", x_in[:], x_loc[tt * 128:(tt + 1) * 128, :],
                           deps=b_xin.wd())
                b_xin.owr(t3)
            pr.add("sy", p4_l)

            def p4_s(q):
                t = q.op(q.eng.activation, gsl[:], gac[:], AF.Silu,
                         deps=b_gac.w + b_gsl.wd())
                b_gsl.owr(t)
                b_gac.ord(t)
            pr.add("s", p4_s)

            def p4_v(q):
                t1 = q.op(q.eng.tensor_tensor, sgf[:], s_in[:], gsl[:], ALU.mult,
                          deps=b_gsl.w + b_sin.w + b_sgf.wd())
                b_sin.ord(t1)
                b_gsl.ord(t1)
                t2 = q.op(q.eng.tensor_tensor, sgf[:], x_in[:], sgf[:], ALU.add,
                          deps=[t1] + b_xin.w)
                b_sgf.owr(t2)
                b_xin.ord(t2)
            pr.add("v", p4_v)

            def p4_o(q, tt=tt):
                t = q.dma("dout", out_h[tt * 128:(tt + 1) * 128, :], sgf[:],
                          deps=b_sgf.w)
                b_sgf.ord(t)
            pr.add("sy", p4_o)

        def fin(q):
            q.wait(Tok("dout", ctx.cnt["dout"]))
        pr.add("sy", fin)
        pr.flush("p4")

    return nc


def _mk_pi(n):
    r = np.arange(n)
    tile, p = r // 128, r % 128
    u, k = tile // 4, tile % 4
    return 512 * u + 4 * p + k


def _pack_u16(packed):
    k8, n = packed.shape
    return np.ascontiguousarray(
        packed.view(np.uint16).reshape(k8, n, 2).transpose(0, 2, 1).reshape(k8 * 2, n))


def _resc(s, nu):
    g, n = s.shape
    return np.ascontiguousarray(
        s.reshape(nu, 4, n).transpose(1, 0, 2).reshape(4, nu * n))


def prep_inputs(inputs):
    x = np.asarray(inputs["L0_attn_x"], dtype=np.float32)
    B_, S, H_ = x.shape
    PIH = _mk_pi(H)
    PII = _mk_pi(I2)
    colsI = np.concatenate([PII, I2 + PII])

    wdkv = np.ascontiguousarray(np.asarray(inputs["L0_attn_W_dkv"])[PIH])
    wdq = np.ascontiguousarray(np.asarray(inputs["L0_attn_W_dq"])[PIH])
    wkr = np.ascontiguousarray(np.asarray(inputs["L0_attn_W_kr"])[PIH])
    wuk = np.ascontiguousarray(np.asarray(inputs["L0_attn_W_uk"]))
    wuv = np.ascontiguousarray(np.asarray(inputs["L0_attn_W_uv"]))
    wuq = np.ascontiguousarray(np.asarray(inputs["L0_attn_W_uq"]))
    wqr = np.ascontiguousarray(np.asarray(inputs["L0_attn_W_qr"]))
    wout = np.ascontiguousarray(np.asarray(inputs["L0_attn_W_out"]))
    gwT = np.ascontiguousarray(np.asarray(inputs["L0_ffn_gate_w"]).T[PIH])

    gup = np.asarray(inputs["L0_ffn_gate_up_packed"])
    gus = np.asarray(inputs["L0_ffn_gate_up_scales"])
    dnp = np.asarray(inputs["L0_ffn_down_packed"])
    dns = np.asarray(inputs["L0_ffn_down_scales"])
    sgup = np.asarray(inputs["L0_ffn_shared_gate_up_packed"])
    sgus = np.asarray(inputs["L0_ffn_shared_gate_up_scales"])
    sdnp = np.asarray(inputs["L0_ffn_shared_down_packed"])
    sdns = np.asarray(inputs["L0_ffn_shared_down_scales"])

    gu16 = np.stack([_pack_u16(gup[e])[:, colsI] for e in range(16)]
                    + [_pack_u16(sgup)[:, colsI]])
    gu16 = np.ascontiguousarray(
        gu16.reshape(17, 4, 128, 6, 512).transpose(0, 3, 2, 1, 4)
        .reshape(17, 6, 128, 2048))
    gscv = np.stack([_resc(gus[e][:, colsI], 4) for e in range(16)]
                    + [_resc(sgus[:, colsI], 4)]).astype(np.float32)
    gscv = np.ascontiguousarray(
        gscv.reshape(17, 4, 4, 6, 512).transpose(0, 3, 1, 2, 4)
        .reshape(17, 6, 4, 2048))
    d16 = np.stack([_pack_u16(dnp[e]) for e in range(16)] + [_pack_u16(sdnp)])
    d16 = np.ascontiguousarray(
        d16.reshape(17, 3, 128, 4, 512).transpose(0, 3, 2, 1, 4)
        .reshape(17, 4, 128, 1536))
    dscv = np.stack([_resc(dns[e], 3) for e in range(16)]
                    + [_resc(sdns, 3)]).astype(np.float32)
    dscv = np.ascontiguousarray(
        dscv.reshape(17, 4, 3, 4, 512).transpose(0, 3, 1, 2, 4)
        .reshape(17, 4, 4, 1536))

    half_dim = DR // 2
    inv = (1.0 / 10000.0 ** (np.arange(0, half_dim, 2, dtype=np.float32)
                             / np.float32(half_dim))).astype(np.float32)
    tpos = (np.arange(S, dtype=np.float32) / np.float32(40.0))
    fr = np.outer(tpos, inv).astype(np.float32)
    rot = np.concatenate([fr, fr], axis=1)
    cosT = np.tile(np.cos(rot), (1, NH)).astype(np.float32)
    sinT = np.tile(np.sin(rot), (1, NH)).astype(np.float32)

    identd = np.eye(128, dtype=np.float32)
    ind4d = (np.arange(128)[None, :] // 32 == np.arange(4)[:, None]
             ).astype(np.float32)

    Tc = S // 2
    shared = dict(wdkv=wdkv, wdq=wdq, wkr=wkr, wuk=wuk, wuv=wuv, wuq=wuq,
                  wqr=wqr, wout=wout, gw=gwT, gu16=gu16, gsc=gscv, d16=d16,
                  dsc=dscv, identd=identd, ind4d=ind4d)
    in_maps = []
    for c in range(8):
        b, hf = c // 2, c % 2
        oth = 1 - hf
        m = dict(shared)
        m["x_loc"] = np.ascontiguousarray(x[b, hf * Tc:(hf + 1) * Tc])
        m["x_oth"] = np.ascontiguousarray(x[b, oth * Tc:(oth + 1) * Tc])
        m["cos_a"] = np.ascontiguousarray(np.concatenate(
            [cosT[hf * Tc:(hf + 1) * Tc], cosT[oth * Tc:(oth + 1) * Tc]]))
        m["sin_a"] = np.ascontiguousarray(np.concatenate(
            [sinT[hf * Tc:(hf + 1) * Tc], sinT[oth * Tc:(oth + 1) * Tc]]))
        in_maps.append(m)

    return in_maps, S, (B_, H_)


def kernel(**inputs):
    in_maps, S, (B_, H_) = prep_inputs(inputs)
    Tc = S // 2
    nc = build_nc(S)
    res = run_bass_kernel_spmd(nc, in_maps, core_ids=list(range(8)))
    out = np.empty((B_, S, H_), dtype=np.float32)
    for c in range(8):
        b, hf = c // 2, c % 2
        out[b, hf * Tc:(hf + 1) * Tc, :] = res.results[c]["out"]
    return out


# version 13
# speedup vs baseline: 1.3086x; 1.0098x over previous
import contextlib
import numpy as np
import concourse.bass as bass
import concourse.mybir as mybir
from concourse.bass_utils import run_bass_kernel_spmd

F32 = mybir.dt.float32
BF16 = mybir.dt.bfloat16
U16 = mybir.dt.uint16
U8 = mybir.dt.uint8
AF = mybir.ActivationFunctionType
ALU = mybir.AluOpType
AX = mybir.AxisListType

H, CD, UP = 2048, 1024, 1024
NH, DH, DR = 16, 128, 64
I2 = 1536
RSC = 0.08838834764831845
ENG = {"v": "vector", "s": "scalar", "t": "tensor", "g": "gpsimd", "sy": "sync"}


class Tok:
    __slots__ = ("key", "val")

    def __init__(self, key, val):
        self.key = key
        self.val = val


class Ctx:
    def __init__(self, nc):
        self.nc = nc
        self.sems = {}
        self.cnt = {}
        self.hwm = {}
        self.guards = []

    def sem(self, key):
        if key not in self.sems:
            g = self.nc.semaphore(f"S_{key}")
            self.sems[key] = g.__enter__()
            self.guards.append(g)
            self.cnt[key] = 0
        return self.sems[key]


class Q:
    def __init__(self, ctx, name, eng):
        self.ctx = ctx
        self.name = name
        self.eng = eng
        ctx.sem(name)

    def wait(self, *deps):
        best = {}
        for d in deps:
            if d is None:
                continue
            for dd in (d if isinstance(d, (list, tuple)) else [d]):
                if dd is not None and (dd.key not in best or best[dd.key] < dd.val):
                    best[dd.key] = dd.val
        for key, val in best.items():
            hk = (self.name, key)
            if self.ctx.hwm.get(hk, 0) >= val:
                continue
            self.eng.wait_ge(self.ctx.sem(key), val)
            self.ctx.hwm[hk] = val

    def op(self, fn, *args, deps=(), inc=1, **kwargs):
        self.wait(*deps)
        inst = fn(*args, **kwargs)
        if inc:
            self.ctx.cnt[self.name] += inc
            inst.then_inc(self.ctx.sem(self.name), inc)
        return Tok(self.name, self.ctx.cnt[self.name])

    def dma(self, key, out, in_, deps=(), **kwargs):
        self.wait(*deps)
        inst = self.eng.dma_start(out, in_, **kwargs)
        self.ctx.sem(key)
        self.ctx.cnt[key] += 16
        inst.then_inc(self.ctx.sem(key), 16)
        return Tok(key, self.ctx.cnt[key])

    def mm_group(self, out_ap, pairs, deps=(), tile_position=None, start=True,
                 stop=True):
        kw = {} if tile_position is None else {"tile_position": tile_position}
        n = len(pairs)
        inst = None
        for i, (l, r) in enumerate(pairs):
            if i == 0:
                self.wait(*deps)
            inst = self.eng.matmul(out_ap, l, r, start=(start and i == 0),
                                   stop=(stop and i == n - 1), **kw)
        self.ctx.cnt[self.name] += 1
        inst.then_inc(self.ctx.sem(self.name), 1)
        return Tok(self.name, self.ctx.cnt[self.name])


class Buf:
    __slots__ = ("w", "r")

    def __init__(self):
        self.w = []
        self.r = []

    def rd(self):
        return list(self.w)

    def ord(self, tok):
        self.r.append(tok)

    def wd(self):
        return self.w + self.r

    def owr(self, tok):
        self.w = [tok]
        self.r = []

    def owp(self, tok):
        self.w.append(tok)


class Prog:
    def __init__(self, nc, ctx):
        self.nc = nc
        self.ctx = ctx
        self.steps = []

    def add(self, eng, fn):
        self.steps.append((eng, fn))

    def drain_dma(self):
        def mk(q):
            q.wait(*[Tok(k, v) for k, v in list(self.ctx.cnt.items())
                     if k not in ("v", "s", "t", "g", "sy") and v > 0])
        self.add("sy", mk)
        self.add("g", mk)

    def flush(self, name):
        self.drain_dma()
        steps, self.steps = self.steps, None
        with self.nc.Block(name) as blk:
            i = 0
            while i < len(steps):
                eng = steps[i][0]
                fns = [steps[i][1]]
                while i + 1 < len(steps) and steps[i + 1][0] == eng:
                    i += 1
                    fns.append(steps[i][1])

                def sec(engine, fns=fns, eng=eng):
                    q = Q(self.ctx, eng, engine)
                    for f in fns:
                        f(q)
                getattr(blk, ENG[eng])(sec)
                i += 1
        self.steps = []


def build_nc(S_all):
    Tc = S_all // 2
    NTT = Tc // 128
    NTTA = S_all // 128
    TW = min(512, Tc)
    TNC = Tc // TW
    G4 = min(2, NTT)
    NACC = (NTT // G4) * 4

    nc = bass.Bass("TRN2", target_bir_lowering=False)
    dt = lambda n, s, d: nc.dram_tensor(n, s, d, kind="ExternalInput").ap()
    x_loc = dt("x_loc", [Tc, H], F32)
    x_oth = dt("x_oth", [Tc, H], F32)
    wdkv = dt("wdkv", [H, CD], F32)
    wdq = dt("wdq", [H, CD], F32)
    wkr = dt("wkr", [H, NH * DR], F32)
    wuk = dt("wuk", [CD, UP], F32)
    wuv = dt("wuv", [CD, NH * DH], F32)
    wuq = dt("wuq", [CD, UP], F32)
    wqr = dt("wqr", [UP, NH * DR], F32)
    wout = dt("wout", [NH * DH, H], F32)
    gw = dt("gw", [H, 16], F32)
    gu16 = dt("gu16", [17, 6, 128, 2048], U16)
    gsc = dt("gsc", [17, 6, 4, 2048], F32)
    d16 = dt("d16", [17, 4, 128, 1536], U16)
    dsc = dt("dsc", [17, 4, 4, 1536], F32)
    cos_a = dt("cos_a", [S_all, 512], F32)
    sin_a = dt("sin_a", [S_all, 512], F32)
    identd = dt("identd", [128, 128], F32)
    ind4d = dt("ind4d", [4, 128], F32)

    out_h = nc.dram_tensor("out", [Tc, H], F32, kind="ExternalOutput").ap()

    di = lambda n, s, d: nc.dram_tensor(n, s, d, kind="Internal").ap()
    xn_sp = di("xn_sp", [NTTA, 128, H], BF16)
    ckv_sp = di("ckv_sp", [NTTA, 128, CD], BF16)
    cq_sp = di("cq_sp", [NTTA, 128, CD], BF16)
    v_sp = di("v_sp", [NTTA, 128, NH * DH], BF16)
    kf_sp = di("kf_sp", [NTT, 128, NH * 128], BF16)
    s_sp = di("s_sp", [NTT, 128, H], BF16)
    gacc = di("gacc", [4, NTT, 128, 512], F32)

    ctx = Ctx(nc)
    pr = Prog(nc, ctx)
    B = Buf
    es = contextlib.ExitStack()

    def sb(name, shape, dtyp):
        return es.enter_context(nc.sbuf_tensor(name, shape, dtyp))

    def ps_(name, shape, dtyp):
        return es.enter_context(nc.psum_tensor(name, shape, dtyp))

    ident = nc.alloc_sbuf_tensor("ident", [128, 128], BF16)
    ind4 = nc.alloc_sbuf_tensor("ind4", [4, 128], BF16)
    gw_sb = nc.alloc_sbuf_tensor("gw_sb", [128, 16 * 16], BF16)
    logit = nc.alloc_sbuf_tensor("logit", [128, NTT * 16], F32)
    cw = nc.alloc_sbuf_tensor("cw", [128, NTT * 16], F32)
    tau = nc.alloc_sbuf_tensor("tau", [128, NTT], F32)
    msb = nc.alloc_sbuf_tensor("msb", [128, NH * DH], BF16)
    neg = nc.alloc_sbuf_tensor("neg", [128, 16], F32)
    lw = nc.alloc_sbuf_tensor("lw", [128, 16], F32)
    sel = nc.alloc_sbuf_tensor("sel", [128, 16], F32)
    ela = nc.alloc_sbuf_tensor("ela", [128, 16], F32)
    mku = nc.alloc_sbuf_tensor("mku", [128, 16], U8)
    st8 = nc.alloc_sbuf_tensor("st8", [128, 8], F32)
    cst = nc.alloc_sbuf_tensor("cst", [128, 2], F32)

    (b_const, b_logit, b_cw, b_tau, b_msb, b_lw, b_sel, b_ela, b_mku,
     b_st8) = (B() for _ in range(10))
    (b_xnsp, b_ckvsp, b_cqsp, b_vsp, b_kfsp, b_ssp, b_gacc) = (B() for _ in range(7))

    def p0(q):
        q.dma("dcst", ident[:], identd)
        q.dma("dcst", ind4[:], ind4d)
        t = q.dma("dcst", gw_sb[:].rearrange("p (a n) -> p a n", n=16),
                  gw.rearrange("(a p) n -> p a n", p=128))
        b_const.owr(t)
        b_const.owp(q.op(q.eng.memset, neg[:], -1e30))
        b_const.owp(q.op(q.eng.memset, cst[:, 0:1], -0.5))
        b_const.owp(q.op(q.eng.memset, cst[:, 1:2], 1e-5))
    pr.add("g", p0)
    pr.flush("p0")

    def x_tile(gtt):
        if gtt < NTT:
            return x_loc[gtt * 128:(gtt + 1) * 128, :]
        return x_oth[(gtt - NTT) * 128:(gtt - NTT + 1) * 128, :]

    def fm_proj_steps(wbuf, b_w, inb, b_in, ob, b_o, ktn, ntile, mm2, b_m2,
                      extra=None):
        for half in range(0, ntile, 2):
            def tstep(q, half=half):
                for n in (half, half + 1):
                    if n >= ntile:
                        break
                    psl = n % 2
                    pairs = [(wbuf[:, (kt * ntile + n) * 128:(kt * ntile + n) * 128 + 128],
                              inb[:, kt * 128:(kt + 1) * 128]) for kt in range(ktn)]
                    t = q.mm_group(mm2[psl][:, 0:128], pairs,
                                   deps=b_in.w + b_m2[psl].wd() + b_w.w)
                    b_m2[psl].owr(t)
                    b_in.ord(t)
            pr.add("t", tstep)

            def vstep(q, half=half):
                for n in (half, half + 1):
                    if n >= ntile:
                        break
                    psl = n % 2
                    tv = q.op(q.eng.tensor_copy, ob[:, n * 128:(n + 1) * 128],
                              mm2[psl][:, 0:128],
                              deps=b_m2[psl].rd() + (b_o.wd() if n == 0 else []))
                    if n == 0:
                        b_o.owr(tv)
                    else:
                        b_o.owp(tv)
                    b_m2[psl].ord(tv)
            pr.add("v", vstep)
        if extra is not None:
            extra()

    def rope_steps(src, b_src, dstv, b_dst, csb, b_cs, ssbt, b_ss, rp1, b_r1,
                   rp2, b_r2):
        def f(q):
            sv = src[:].rearrange("p (h j) -> p h j", j=64)
            cv = csb[:].rearrange("p (h j) -> p h j", j=32)
            sv2 = ssbt[:].rearrange("p (h j) -> p h j", j=32)
            r1v = rp1[:].rearrange("p (h j) -> p h j", j=32)
            r2v = rp2[:].rearrange("p (h j) -> p h j", j=16)
            t0 = q.op(q.eng.tensor_copy, dstv[:, :, 32:64], sv[:, :, 32:64],
                      deps=b_src.w + b_cs.w + b_ss.w + b_dst.w + b_dst.r)
            b_dst.owp(t0)
            t1 = q.op(q.eng.tensor_tensor, r1v, sv[:, :, 0:32], cv, ALU.mult,
                      deps=b_r1.wd())
            b_r1.owr(t1)
            t2 = q.op(q.eng.tensor_tensor, r2v, sv[:, :, 16:32], sv2[:, :, 0:16],
                      ALU.mult, deps=b_r2.wd())
            b_r2.owr(t2)
            t3 = q.op(q.eng.tensor_tensor, dstv[:, :, 0:16], r1v[:, :, 0:16], r2v,
                      ALU.subtract, deps=[t1, t2])
            b_dst.owp(t3)
            b_r2.ord(t3)
            t4 = q.op(q.eng.tensor_tensor, r2v, sv[:, :, 0:16], sv2[:, :, 16:32],
                      ALU.mult, deps=b_r2.wd())
            b_r2.owr(t4)
            t5 = q.op(q.eng.tensor_tensor, dstv[:, :, 16:32], r1v[:, :, 16:32], r2v,
                      ALU.add, deps=[t4])
            b_dst.owp(t5)
            b_src.ord(t5)
            b_r1.ord(t5)
            b_r2.ord(t5)
            b_cs.ord(t5)
            b_ss.ord(t5)
        pr.add("v", f)

    with contextlib.ExitStack() as st_:
        sbp = lambda n, s, d: st_.enter_context(nc.sbuf_tensor(n, s, d))
        psp = lambda n, s, d: st_.enter_context(nc.psum_tensor(n, s, d))
        wA = sbp("wA", [128, 16 * CD], BF16)
        wA2 = sbp("wA2", [128, 16 * CD], BF16)
        xt = [sbp(f"xt{i}", [128, H], F32) for i in range(2)]
        xnt = [sbp(f"xnt{i}", [128, H], BF16) for i in range(2)]
        sqs = sbp("sqs", [128, H], BF16)
        xnp = [sbp(f"xnp{i}", [128, H], BF16) for i in range(2)]
        ckvb = sbp("ckvb", [128, CD], BF16)
        cqob = sbp("cqob", [128, CD], BF16)
        pips = psp("pips", [128, 512], BF16)
        mm2 = [psp(f"mm2_{i}", [128, 128], F32) for i in range(2)]
        lgps = psp("lgps", [128, 16], F32)
        b_wA = B()
        b_wA2 = B()
        b_xt = [B(), B()]
        b_xnt = [B(), B()]
        b_sq = B()
        b_xnp = [B(), B()]
        b_ckv = B()
        b_cqo = B()
        b_pips = B()
        b_m2 = [B(), B()]
        b_lg = B()

        def a1_w(q):
            b_wA.owr(q.dma("dw", wA[:].rearrange("p (a n) -> p a n", n=CD),
                           wdkv.rearrange("(a p) n -> p a n", p=128)))
            b_wA2.owr(q.dma("dw", wA2[:].rearrange("p (a n) -> p a n", n=CD),
                            wdq.rearrange("(a p) n -> p a n", p=128)))
        pr.add("g", a1_w)

        def mk_xload(gtt, sl):
            def f(q):
                t = q.dma(f"dx{sl}", xt[sl][:], x_tile(gtt), deps=b_xt[sl].wd())
                b_xt[sl].owr(t)
            return f
        pr.add("sy", mk_xload(0, 0))
        if NTTA > 1:
            pr.add("sy", mk_xload(1, 1))

        for gtt in range(NTTA):
            sl = gtt % 2

            def ln_v1(q, sl=sl):
                t = q.op(q.eng.reduce_sum, st8[:, 0:1], xt[sl][:], axis=AX.X,
                         deps=b_xt[sl].rd() + b_st8.wd())
                b_st8.owr(t)
                b_xt[sl].ord(t)
            pr.add("v", ln_v1)

            def ln_s1(q, sl=sl):
                t = q.op(q.eng.activation, sqs[:], xt[sl][:], AF.Square,
                         accum_out=st8[:, 1:2],
                         deps=b_xt[sl].rd() + b_sq.wd() + b_st8.w + b_st8.r)
                b_sq.owr(t)
                b_st8.owp(t)
                b_xt[sl].ord(t)
            pr.add("s", ln_s1)

            def ln_v2(q):
                t1 = q.op(q.eng.tensor_scalar, st8[:, 2:3], st8[:, 0:1], 1.0 / H,
                          None, ALU.mult, deps=b_st8.w)
                t2 = q.op(q.eng.tensor_scalar, st8[:, 3:4], st8[:, 2:3],
                          st8[:, 2:3], None, ALU.mult, deps=[t1])
                t3 = q.op(q.eng.scalar_tensor_tensor, st8[:, 4:5], st8[:, 1:2],
                          1.0 / H, st8[:, 3:4], ALU.mult, ALU.subtract, deps=[t2])
                b_st8.owp(t3)
            pr.add("v", ln_v2)

            def ln_s2(q):
                t1 = q.op(q.eng.activation, st8[:, 5:6], st8[:, 4:5], AF.Ln,
                          bias=cst[:, 1:2], deps=b_st8.w + b_const.w)
                t2 = q.op(q.eng.activation, st8[:, 6:7], st8[:, 5:6], AF.Exp,
                          scale=cst[:, 0:1], deps=[t1])
                b_st8.owp(t2)
            pr.add("s", ln_s2)

            def ln_v3(q, sl=sl):
                t = q.op(q.eng.tensor_scalar, xnt[sl][:], xt[sl][:], st8[:, 2:3],
                         st8[:, 6:7], ALU.subtract, ALU.mult,
                         deps=b_st8.w + b_xnt[sl].wd() + b_xt[sl].rd())
                b_xnt[sl].owr(t)
                b_xt[sl].ord(t)
                b_st8.ord(t)
            pr.add("v", ln_v3)

            if gtt + 2 < NTTA:
                pr.add("sy", mk_xload(gtt + 2, sl))

            for grp in range(4):
                def pi_t(q, sl=sl, grp=grp):
                    tk = None
                    for j in range(4):
                        tu = grp * 4 + j
                        u, k = tu // 4, tu % 4
                        tk = q.op(q.eng.transpose, pips[:, j * 128:(j + 1) * 128],
                                  xnt[sl][:, (512 * u + k):(512 * u + k + 509):4],
                                  ident[:],
                                  deps=(b_xnt[sl].w + b_pips.wd() + b_const.w)
                                  if j == 0 else (), inc=(1 if j == 3 else 0))
                    b_pips.owr(tk)
                    b_xnt[sl].ord(tk)
                pr.add("t", pi_t)

                def pi_v(q, sl=sl, grp=grp):
                    t = q.op(q.eng.tensor_copy, xnp[sl][:, grp * 512:(grp + 1) * 512],
                             pips[:],
                             deps=b_pips.rd() + (b_xnp[sl].wd() if grp == 0 else []))
                    if grp == 0:
                        b_xnp[sl].owr(t)
                    else:
                        b_xnp[sl].owp(t)
                    b_pips.ord(t)
                pr.add("v", pi_v)

            def xn_spill(q, gtt=gtt, sl=sl):
                t = q.dma(f"sxn{sl}", xn_sp[gtt], xnp[sl][:], deps=b_xnp[sl].w)
                b_xnp[sl].ord(t)
                b_xnsp.owp(t)
            pr.add("sy", xn_spill)

            for half in range(0, 8, 2):
                def ckv_t(q, sl=sl, half=half):
                    for n in (half, half + 1):
                        psl = n % 2
                        pairs = [(wA[:, (kt * 8 + n) * 128:(kt * 8 + n) * 128 + 128],
                                  xnp[sl][:, kt * 128:(kt + 1) * 128])
                                 for kt in range(16)]
                        t = q.mm_group(mm2[psl][:], pairs,
                                       deps=b_xnp[sl].w + b_m2[psl].wd() + b_wA.w)
                        b_m2[psl].owr(t)
                        b_xnp[sl].ord(t)
                pr.add("t", ckv_t)

                def ckv_v(q, half=half):
                    for n in (half, half + 1):
                        psl = n % 2
                        tv = q.op(q.eng.tensor_copy, ckvb[:, n * 128:(n + 1) * 128],
                                  mm2[psl][:],
                                  deps=b_m2[psl].rd() + (b_ckv.wd() if n == 0 else []))
                        if n == 0:
                            b_ckv.owr(tv)
                        else:
                            b_ckv.owp(tv)
                        b_m2[psl].ord(tv)
                pr.add("v", ckv_v)

            if gtt < NTT:
                def lg_t(q, gtt=gtt, sl=sl):
                    pairs = [(xnp[sl][:, kt * 128:(kt + 1) * 128],
                              gw_sb[:, kt * 16:(kt + 1) * 16]) for kt in range(16)]
                    t = q.mm_group(lgps[:], pairs,
                                   deps=b_xnp[sl].w + b_lg.wd() + b_const.w)
                    b_lg.owr(t)
                    b_xnp[sl].ord(t)
                pr.add("t", lg_t)

                def lg_v(q, gtt=gtt):
                    tv = q.op(q.eng.tensor_copy, logit[:, gtt * 16:(gtt + 1) * 16],
                              lgps[:], deps=b_lg.rd())
                    b_logit.owp(tv)
                    b_lg.ord(tv)
                pr.add("v", lg_v)

            def ckv_spill(q, gtt=gtt):
                t = q.dma("scv", ckv_sp[gtt], ckvb[:], deps=b_ckv.w)
                b_ckv.ord(t)
                b_ckvsp.owp(t)
            pr.add("sy", ckv_spill)

            fm_proj_steps(wA2, b_wA2, xnp[sl], b_xnp[sl], cqob, b_cqo, 16, 8,
                          mm2, b_m2)

            def cq_spill(q, gtt=gtt):
                t = q.dma("scq", cq_sp[gtt], cqob[:], deps=b_cqo.w)
                b_cqo.ord(t)
                b_cqsp.owp(t)
            pr.add("sy", cq_spill)

        def top4_v1(q):
            for tt in range(NTT):
                t0 = q.op(q.eng.tensor_copy, lw[:], logit[:, tt * 16:(tt + 1) * 16],
                          deps=b_logit.w + b_lw.wd())
                b_lw.owr(t0)
                b_logit.ord(t0)
                for r in range(4):
                    dst = st8[:, 0:1] if r < 3 else tau[:, tt:tt + 1]
                    t1 = q.op(q.eng.reduce_max, dst, lw[:], axis=AX.X,
                              deps=[t0] + (b_st8.wd() if r < 3 else b_tau.wd()))
                    if r < 3:
                        b_st8.owr(t1)
                        t2 = q.op(q.eng.tensor_scalar, mku[:], lw[:], dst, None,
                                  ALU.is_ge, deps=[t1] + b_mku.wd())
                        b_mku.owr(t2)
                        t0 = q.op(q.eng.copy_predicated, lw[:], mku[:], neg[:],
                                  deps=[t2] + b_const.w)
                        b_lw.owr(t0)
                    else:
                        b_tau.owp(t1)
                        b_lw.ord(t1)
        pr.add("v", top4_v1)

        def top4_pair(tt):
            def fs(q, tt=tt):
                t = q.op(q.eng.activation, ela[:], logit[:, tt * 16:(tt + 1) * 16],
                         AF.Exp, deps=b_logit.w + b_ela.wd() + b_tau.w)
                b_ela.owr(t)
            def fv(q, tt=tt):
                t1 = q.op(q.eng.tensor_scalar, sel[:], logit[:, tt * 16:(tt + 1) * 16],
                          tau[:, tt:tt + 1], None, ALU.is_ge,
                          deps=b_tau.w + b_logit.w + b_sel.wd())
                b_sel.owr(t1)
                t2 = q.op(q.eng.tensor_tensor, sel[:], ela[:], sel[:], ALU.mult,
                          deps=[t1] + b_ela.w)
                b_sel.owr(t2)
                b_ela.ord(t2)
                t3 = q.op(q.eng.reduce_sum, st8[:, 0:1], sel[:], axis=AX.X,
                          deps=[t2] + b_st8.wd())
                t4 = q.op(q.eng.reciprocal, st8[:, 1:2], st8[:, 0:1], deps=[t3])
                b_st8.owr(t4)
                t5 = q.op(q.eng.tensor_scalar, cw[:, tt * 16:(tt + 1) * 16], sel[:],
                          st8[:, 1:2], None, ALU.mult, deps=[t4])
                b_cw.owp(t5)
                b_sel.ord(t5)
                b_st8.ord(t5)
            return fs, fv
        for tt in range(NTT):
            fs, fv = top4_pair(tt)
            pr.add("s", fs)
            pr.add("v", fv)
        pr.flush("a1")

    with contextlib.ExitStack() as st_:
        sbp = lambda n, s, d: st_.enter_context(nc.sbuf_tensor(n, s, d))
        psp = lambda n, s, d: st_.enter_context(nc.psum_tensor(n, s, d))
        wA3 = sbp("wA3", [128, 16 * 1024], BF16)
        inb = [sbp(f"a3in{i}", [128, H], BF16) for i in range(2)]
        krt = sbp("krt", [128, 1024], BF16)
        krot = sbp("krot", [128, 1024], BF16)
        csb = sbp("csb", [128, 512], BF16)
        snb = sbp("snb", [128, 512], BF16)
        rp1 = sbp("rp1", [128, 512], BF16)
        rp2 = sbp("rp2", [128, 256], BF16)
        kch = sbp("kch", [128, NH * 128], BF16)
        p5 = [psp(f"a3p5{i}", [128, 512], F32) for i in range(2)]
        tpsb = psp("a3tps", [128, 512], BF16)
        b_w, b_in = B(), [B(), B()]
        b_krt, b_krot, b_cs, b_ss, b_r1, b_r2, b_kch, b_tps = (B() for _ in range(8))
        b_p5 = [B(), B()]

        def a3_w(q):
            b_w.owr(q.dma("dw", wA3[:].rearrange("p (a n) -> p a n", n=1024),
                          wkr.rearrange("(a p) n -> p a n", p=128)))
        pr.add("g", a3_w)

        for gtt in range(NTT):
            sl = gtt % 2

            def iload(q, gtt=gtt, sl=sl):
                t = q.dma(f"da3{sl}", inb[sl][:], xn_sp[gtt],
                          deps=b_in[sl].wd() + b_xnsp.w)
                b_in[sl].owr(t)
            pr.add("sy", iload)

            def csload(q, gtt=gtt):
                q.dma("da3c", csb[:], cos_a[gtt * 128:(gtt + 1) * 128, :],
                      deps=b_cs.wd() + b_ss.wd())
                ts_ = q.dma("da3c", snb[:], sin_a[gtt * 128:(gtt + 1) * 128, :])
                b_cs.owr(ts_)
                b_ss.owr(ts_)
            pr.add("g", csload)

            def a3_t(q, sl=sl):
                for n in range(2):
                    pairs = [(inb[sl][:, kt * 128:(kt + 1) * 128],
                              wA3[:, kt * 1024 + n * 512: kt * 1024 + n * 512 + 512])
                             for kt in range(16)]
                    t = q.mm_group(p5[n][:], pairs,
                                   deps=b_in[sl].w + b_p5[n].wd() + b_w.w)
                    b_p5[n].owr(t)
                    b_in[sl].ord(t)
            pr.add("t", a3_t)

            def a3_v(q):
                for n in range(2):
                    tv = q.op(q.eng.tensor_copy, krt[:, n * 512:(n + 1) * 512],
                              p5[n][:],
                              deps=b_p5[n].rd() + (b_krt.wd() if n == 0 else []))
                    if n == 0:
                        b_krt.owr(tv)
                    else:
                        b_krt.owp(tv)
                    b_p5[n].ord(tv)
            pr.add("v", a3_v)

            krotv = krot[:].rearrange("p (h j) -> p h j", j=64)
            rope_steps(krt, b_krt, krotv, b_krot, csb, b_cs, snb, b_ss,
                       rp1, b_r1, rp2, b_r2)

            for hg in range(4):
                def a3_tr(q, hg=hg):
                    tk = None
                    for j in range(4):
                        h = hg * 4 + j
                        tk = q.op(q.eng.transpose,
                                  tpsb[64:128, j * 128:(j + 1) * 128],
                                  krot[:, h * 64:(h + 1) * 64], ident[:],
                                  tile_position=(0, 64),
                                  deps=(b_krot.w + b_tps.wd() + b_const.w)
                                  if j == 0 else (), inc=(1 if j == 3 else 0))
                    b_tps.owr(tk)
                    b_krot.ord(tk)
                pr.add("t", a3_tr)

                def a3_ev(q, hg=hg):
                    t = q.op(q.eng.tensor_copy,
                             kch[64:128, hg * 512:(hg + 1) * 512], tpsb[64:128, :],
                             deps=b_tps.rd() + (b_kch.wd() if hg == 0 else []))
                    if hg == 0:
                        b_kch.owr(t)
                    else:
                        b_kch.owp(t)
                    b_tps.ord(t)
                pr.add("v", a3_ev)

            def a3_spill(q, gtt=gtt):
                t = q.dma("skf", kf_sp[gtt, 64:128, :], kch[64:128, :],
                          deps=b_kch.w)
                b_kch.ord(t)
                b_kfsp.owp(t)
            pr.add("sy", a3_spill)
        pr.flush("a3")

    with contextlib.ExitStack() as st_:
        sbp = lambda n, s, d: st_.enter_context(nc.sbuf_tensor(n, s, d))
        psp = lambda n, s, d: st_.enter_context(nc.psum_tensor(n, s, d))
        wB1 = sbp("wB1", [128, 8 * 2048], BF16)
        wB3 = sbp("wB3", [128, 8 * 1024], BF16)
        cin = [sbp(f"b1in{i}", [128, CD], BF16) for i in range(2)]
        vtb = sbp("vtb", [128, 2048], BF16)
        kch = sbp("b3kch", [128, NH * 128], BF16)
        p5 = [psp(f"b1p5{i}", [128, 512], F32) for i in range(2)]
        b_w, b_in, b_vt = B(), [B(), B()], B()
        b_w3, b_kch = B(), B()
        b_p5 = [B(), B()]

        def b1_w(q):
            b_w.owr(q.dma("dw", wB1[:].rearrange("p (a n) -> p a n", n=2048),
                          wuv.rearrange("(a p) n -> p a n", p=128)))
            b_w3.owr(q.dma("dw", wB3[:].rearrange("p (a n) -> p a n", n=1024),
                           wuk.rearrange("(a p) n -> p a n", p=128)))
        pr.add("g", b1_w)

        for gtt in range(NTTA):
            sl = gtt % 2

            def iload(q, gtt=gtt, sl=sl):
                t = q.dma(f"db1{sl}", cin[sl][:], ckv_sp[gtt],
                          deps=b_in[sl].wd() + b_ckvsp.w)
                b_in[sl].owr(t)
            pr.add("sy", iload)

            for half in range(0, 4, 2):
                def b1_t(q, sl=sl, half=half):
                    for n in (half, half + 1):
                        psl = n % 2
                        pairs = [(cin[sl][:, kt * 128:(kt + 1) * 128],
                                  wB1[:, kt * 2048 + n * 512: kt * 2048 + n * 512 + 512])
                                 for kt in range(8)]
                        t = q.mm_group(p5[psl][:], pairs,
                                       deps=b_in[sl].w + b_p5[psl].wd() + b_w.w)
                        b_p5[psl].owr(t)
                        b_in[sl].ord(t)
                pr.add("t", b1_t)

                def b1_v(q, half=half):
                    for n in (half, half + 1):
                        psl = n % 2
                        tv = q.op(q.eng.tensor_copy, vtb[:, n * 512:(n + 1) * 512],
                                  p5[psl][:],
                                  deps=b_p5[psl].rd() + (b_vt.wd() if n == 0 else []))
                        if n == 0:
                            b_vt.owr(tv)
                        else:
                            b_vt.owp(tv)
                        b_p5[psl].ord(tv)
                pr.add("v", b1_v)

            def vspill(q, gtt=gtt):
                t = q.dma("sv", v_sp[gtt], vtb[:], deps=b_vt.w)
                b_vt.ord(t)
                b_vsp.owp(t)
            pr.add("sy", vspill)

            if gtt < NTT:
                for hg in range(4):
                    def b3_t(q, sl=sl, hg=hg):
                        psl = hg % 2
                        tk = None
                        for j in range(4):
                            h = hg * 4 + j
                            pairs = [(wB3[:, kt * 1024 + h * 64: kt * 1024 + h * 64 + 64],
                                      cin[sl][:, kt * 128:(kt + 1) * 128])
                                     for kt in range(8)]
                            tk = q.mm_group(p5[psl][0:64, j * 128:(j + 1) * 128], pairs,
                                            deps=(b_in[sl].w + b_p5[psl].wd() + b_w3.w)
                                            if j == 0 else ())
                        b_p5[psl].owr(tk)
                        b_in[sl].ord(tk)
                    pr.add("t", b3_t)

                    def b3_v(q, hg=hg):
                        psl = hg % 2
                        t = q.op(q.eng.tensor_copy, kch[0:64, hg * 512:(hg + 1) * 512],
                                 p5[psl][0:64, :],
                                 deps=b_p5[psl].rd() + (b_kch.wd() if hg == 0 else []))
                        if hg == 0:
                            b_kch.owr(t)
                        else:
                            b_kch.owp(t)
                        b_p5[psl].ord(t)
                    pr.add("v", b3_v)

                def b3_spill(q, gtt=gtt):
                    t = q.dma("skf2", kf_sp[gtt, 0:64, :], kch[0:64, :], deps=b_kch.w)
                    b_kch.ord(t)
                    b_kfsp.owp(t)
                pr.add("sy", b3_spill)
        pr.flush("b1")

    with contextlib.ExitStack() as st_:
        sbp = lambda n, s, d: st_.enter_context(nc.sbuf_tensor(n, s, d))
        psp = lambda n, s, d: st_.enter_context(nc.psum_tensor(n, s, d))
        wQ = sbp("wQ", [128, 8 * 1024], BF16)
        wR = sbp("wR", [128, 8 * 1024], BF16)
        cin = [sbp(f"b2in{i}", [128, CD], BF16) for i in range(2)]
        vin = [sbp(f"vin{i}", [128, 2048], BF16) for i in range(2)]
        qcf = sbp("qcf", [128, 1024], BF16)
        qrt = sbp("qrt", [128, 1024], BF16)
        qtb = sbp("qtb", [128, 2048], BF16)
        csb = sbp("b2cs", [128, 512], BF16)
        snb = sbp("b2sn", [128, 512], BF16)
        rp1 = sbp("b2r1", [128, 512], BF16)
        rp2 = sbp("b2r2", [128, 256], BF16)
        mm2 = [psp(f"b2mm{i}", [128, 128], F32) for i in range(2)]
        p5 = [psp(f"b2p5{i}", [128, 512], F32) for i in range(2)]
        mps = psp("mps", [128, 2048], F32)
        b_wq, b_wr = B(), B()
        b_in, b_vin = [B(), B()], [B(), B()]
        b_qcf, b_qrt, b_qt, b_cs, b_ss, b_r1, b_r2, b_mps = (B() for _ in range(8))
        b_m2 = [B(), B()]
        b_p5 = [B(), B()]

        def b2_w(q):
            q.dma("dw", wQ[:].rearrange("p (a n) -> p a n", n=1024),
                  wuq.rearrange("(a p) n -> p a n", p=128))
            t = q.dma("dw", wR[:].rearrange("p (a n) -> p a n", n=1024),
                      wqr.rearrange("(a p) n -> p a n", p=128))
            b_wq.owr(t)
            b_wr.owr(t)
        pr.add("g", b2_w)

        for gtt in range(NTTA):
            sl = gtt % 2

            def iload(q, gtt=gtt, sl=sl):
                t = q.dma(f"db2{sl}", cin[sl][:], cq_sp[gtt],
                          deps=b_in[sl].wd() + b_cqsp.w)
                b_in[sl].owr(t)
                tv = q.dma(f"db2v{sl}", vin[sl][:], v_sp[gtt],
                           deps=b_vin[sl].wd() + b_vsp.w)
                b_vin[sl].owr(tv)
            pr.add("sy", iload)

            def csload(q, gtt=gtt):
                q.dma("db2c", csb[:], cos_a[gtt * 128:(gtt + 1) * 128, :],
                      deps=b_cs.wd() + b_ss.wd())
                ts_ = q.dma("db2c", snb[:], sin_a[gtt * 128:(gtt + 1) * 128, :])
                b_cs.owr(ts_)
                b_ss.owr(ts_)
            pr.add("g", csload)

            fm_proj_steps(wQ, b_wq, cin[sl], b_in[sl], qcf, b_qcf, 8, 8, mm2, b_m2)

            def qr_t(q):
                for n in range(2):
                    pairs = [(qcf[:, kt * 128:(kt + 1) * 128],
                              wR[:, kt * 1024 + n * 512: kt * 1024 + n * 512 + 512])
                             for kt in range(8)]
                    t = q.mm_group(p5[n][:], pairs,
                                   deps=b_qcf.w + b_p5[n].wd() + b_wr.w)
                    b_p5[n].owr(t)
                    b_qcf.ord(t)
            pr.add("t", qr_t)

            def qr_v(q):
                for n in range(2):
                    tv = q.op(q.eng.tensor_copy, qrt[:, n * 512:(n + 1) * 512],
                              p5[n][:],
                              deps=b_p5[n].rd() + (b_qrt.wd() if n == 0 else []))
                    if n == 0:
                        b_qrt.owr(tv)
                    else:
                        b_qrt.owp(tv)
                    b_p5[n].ord(tv)
            pr.add("v", qr_v)

            def qc_t(q, sl=sl):
                for n in range(2):
                    pairs = [(cin[sl][:, kt * 128:(kt + 1) * 128],
                              wQ[:, kt * 1024 + n * 512: kt * 1024 + n * 512 + 512])
                             for kt in range(8)]
                    t = q.mm_group(p5[n][:], pairs,
                                   deps=b_in[sl].w + b_p5[n].wd() + b_wq.w)
                    b_p5[n].owr(t)
                    b_in[sl].ord(t)
            pr.add("t", qc_t)

            def qc_v(q):
                for n in range(2):
                    src = p5[n][:].rearrange("p (h j) -> p h j", j=64)
                    dstv = qtb[:].rearrange("p (h j) -> p h j", j=128)[:, n * 8:(n + 1) * 8, 0:64]
                    tv = q.op(q.eng.tensor_copy, dstv, src,
                              deps=b_p5[n].rd() + (b_qt.wd() if n == 0 else []))
                    if n == 0:
                        b_qt.owr(tv)
                    else:
                        b_qt.owp(tv)
                    b_p5[n].ord(tv)
            pr.add("v", qc_v)

            qtv = qtb[:].rearrange("p (h j) -> p h j", j=128)[:, :, 64:128]
            rope_steps(qrt, b_qrt, qtv, b_qt, csb, b_cs, snb, b_ss, rp1, b_r1,
                       rp2, b_r2)

            def b2_m(q, gtt=gtt, sl=sl):
                t = None
                for h in range(NH):
                    t = q.mm_group(mps[:, h * 128:(h + 1) * 128],
                                   [(qtb[:, h * 128:(h + 1) * 128],
                                     vin[sl][:, h * 128:(h + 1) * 128])],
                                   deps=(b_qt.w + b_vin[sl].w +
                                         (b_mps.wd() if gtt == 0 else []))
                                   if h == 0 else (),
                                   start=(gtt == 0 and h % 4 == 0),
                                   stop=(gtt == NTTA - 1 and h % 4 == 3))
                b_mps.owp(t)
                b_qt.ord(t)
                b_vin[sl].ord(t)
            pr.add("t", b2_m)

        def b2_mev(q):
            t = q.op(q.eng.tensor_scalar, msb[:], mps[:], RSC, None, ALU.mult,
                     deps=b_mps.w + b_msb.wd())
            b_msb.owr(t)
            b_mps.ord(t)
        pr.add("v", b2_mev)
        pr.flush("b2")

    with contextlib.ExitStack() as st_:
        sbp = lambda n, s, d: st_.enter_context(nc.sbuf_tensor(n, s, d))
        psp = lambda n, s, d: st_.enter_context(nc.psum_tensor(n, s, d))
        wO = sbp("wO", [128, 16 * 2048], BF16)
        kin = [sbp(f"kin{i}", [128, 2048], BF16) for i in range(2)]
        outf = sbp("outf", [128, 2048], BF16)
        s_st = sbp("s_st", [128, 2048], BF16)
        p5 = [psp(f"p2p5{i}", [128, 512], F32) for i in range(2)]
        b_w, b_kin, b_of, b_sst = B(), [B(), B()], B(), B()
        b_p5 = [B(), B()]

        def p2_w(q):
            b_w.owr(q.dma("dw", wO[:].rearrange("p (a n) -> p a n", n=2048),
                          wout.rearrange("(a p) n -> p a n", p=128)))
        pr.add("g", p2_w)

        for gtt in range(NTT):
            sl = gtt % 2

            def iload(q, gtt=gtt, sl=sl):
                t = q.dma(f"dp2{sl}", kin[sl][:], kf_sp[gtt],
                          deps=b_kin[sl].wd() + b_kfsp.w)
                b_kin[sl].owr(t)
            pr.add("sy", iload)

            for hg in range(4):
                def of_t(q, sl=sl, hg=hg):
                    psl = hg % 2
                    tk = None
                    for j in range(4):
                        h = hg * 4 + j
                        tk = q.mm_group(p5[psl][:, j * 128:(j + 1) * 128],
                                        [(msb[:, h * 128:(h + 1) * 128],
                                          kin[sl][:, h * 128:(h + 1) * 128])],
                                        deps=(b_msb.w + b_kin[sl].w + b_p5[psl].wd())
                                        if j == 0 else ())
                    b_p5[psl].owr(tk)
                    b_kin[sl].ord(tk)
                pr.add("t", of_t)

                def of_v(q, hg=hg):
                    psl = hg % 2
                    t = q.op(q.eng.tensor_copy, outf[:, hg * 512:(hg + 1) * 512],
                             p5[psl][:],
                             deps=b_p5[psl].rd() + (b_of.wd() if hg == 0 else []))
                    if hg == 0:
                        b_of.owr(t)
                    else:
                        b_of.owp(t)
                    b_p5[psl].ord(t)
                pr.add("v", of_v)

            for half in range(0, 4, 2):
                def s_t(q, half=half):
                    for n in (half, half + 1):
                        psl = n % 2
                        pairs = [(outf[:, kt * 128:(kt + 1) * 128],
                                  wO[:, kt * 2048 + n * 512: kt * 2048 + n * 512 + 512])
                                 for kt in range(16)]
                        t = q.mm_group(p5[psl][:], pairs,
                                       deps=b_of.w + b_p5[psl].wd() + b_w.w)
                        b_p5[psl].owr(t)
                        b_of.ord(t)
                pr.add("t", s_t)

                def s_v(q, half=half):
                    for n in (half, half + 1):
                        psl = n % 2
                        tv = q.op(q.eng.tensor_copy, s_st[:, n * 512:(n + 1) * 512],
                                  p5[psl][:],
                                  deps=b_p5[psl].rd() + (b_sst.wd() if n == 0 else []))
                        if n == 0:
                            b_sst.owr(tv)
                        else:
                            b_sst.owp(tv)
                        b_p5[psl].ord(tv)
                pr.add("v", s_v)

            def sspill(q, gtt=gtt):
                t = q.dma("ssp", s_sp[gtt], s_st[:], deps=b_sst.w)
                b_sst.ord(t)
                b_ssp.owp(t)
            pr.add("sy", sspill)
        pr.flush("p2")

    with contextlib.ExitStack() as st_:
        sbp = lambda n, s, d: st_.enter_context(nc.sbuf_tensor(n, s, d))
        psp = lambda n, s, d: st_.enter_context(nc.psum_tensor(n, s, d))
        xn_f = sbp("xn_f", [128, 16 * Tc], BF16)
        act = sbp("act", [128, 12 * Tc], BF16)
        u16b = [sbp(f"u16b{i}", [128, 4 * 512], U16) for i in range(2)]
        wdq_ = [sbp(f"wdq{i}", [128, 16 * 512], BF16) for i in range(2)]
        sclb = sbp("sclb", [128, 4 * 512], BF16)
        gss = [sbp(f"gss{i}", [4, 4 * 512], BF16) for i in range(2)]
        t7b = sbp("t7b", [128, 512], U16)
        mbb = [sbp(f"mbb{i}", [128, 512], U16) for i in range(4)]
        sgb = [sbp(f"sgb{i}", [128, 512], U16) for i in range(4)]
        pcb = [sbp(f"pcb{i}", [128, 512], BF16) for i in range(4)]
        stg = [sbp(f"stg{i}", [128, G4 * 512], F32) for i in range(2)]
        p5 = [psp(f"p3p5{i}", [128, 512], F32) for i in range(2)]
        sclp = psp("sclp", [128, 2048], F32)
        b_xnf, b_act, b_scl = (B() for _ in range(3))
        b_u16 = [B(), B()]
        b_gss = [B(), B()]
        b_wd = [B(), B()]
        b_t7 = B()
        b_mb = [B() for _ in range(4)]
        b_sg = [B() for _ in range(4)]
        b_pc = [B() for _ in range(4)]
        b_stg = [B(), B()]
        b_sclp = B()
        b_p5 = [B(), B()]

        def ffn_xn(q):
            xv = xn_f[:].rearrange("p (a n) -> p a n", n=Tc)
            for tt in range(NTT):
                t = q.dma("dxnf", xv[:, :, tt * 128:(tt + 1) * 128],
                          xn_sp[tt].rearrange("p (a n) -> p a n", n=128),
                          deps=(b_xnf.wd() if tt == 0 else []) + b_xnsp.w)
                b_xnf.owp(t)
        pr.add("sy", ffn_xn)

        order_g = [0, 3, 1, 4, 2, 5]
        actv = act[:].rearrange("p (a n) -> p a n", n=Tc)
        xnv = xn_f[:].rearrange("p (a n) -> p a n", n=Tc)

        def deq_chunks(nu, slot):
            wd = wdq_[slot]
            bwd = b_wd[slot]
            ub_ = u16b[slot]
            bu = b_u16[slot]
            gs_ = gss[slot]
            bgs = b_gss[slot]

            def head():
                def f_sc(q):
                    tk = None
                    for u in range(nu):
                        tk = q.mm_group(sclp[:, u * 512:(u + 1) * 512],
                                        [(ind4[:], gs_[0:4, u * 512:(u + 1) * 512])],
                                        deps=(b_sclp.wd() + bgs.w + b_const.w)
                                        if u == 0 else ())
                    b_sclp.owr(tk)
                    bgs.ord(tk)
                pr.add("t", f_sc)

                def f_scv(q):
                    t = q.op(q.eng.tensor_copy, sclb[:, 0:nu * 512],
                             sclp[:, 0:nu * 512], deps=b_sclp.rd() + b_scl.wd())
                    b_scl.owr(t)
                    b_sclp.ord(t)
                pr.add("v", f_scv)

            def uchunk(u):
                tgk = [None] * 4

                def f_v1(q, u=u):
                    ub = ub_[:, u * 512:(u + 1) * 512]
                    for k in range(4):
                        t1 = q.op(q.eng.tensor_scalar, t7b[:], ub, 4 * k, 7,
                                  ALU.logical_shift_right, ALU.bitwise_and,
                                  deps=bu.w + b_t7.wd())
                        b_t7.owr(t1)
                        bu.ord(t1)
                        t2 = q.op(q.eng.tensor_scalar, mbb[k][:], t7b[:], 64, 16128,
                                  ALU.mult, ALU.add, deps=[t1] + b_mb[k].wd())
                        b_mb[k].owr(t2)
                        b_t7.ord(t2)
                        t3 = q.op(q.eng.tensor_scalar, sgb[k][:], ub, 12 - 4 * k,
                                  32768, ALU.logical_shift_left, ALU.bitwise_and,
                                  deps=b_sg[k].wd())
                        b_sg[k].owr(t3)
                        bu.ord(t3)
                pr.add("v", f_v1)

                def f_s(q, u=u):
                    for k in range(4):
                        t = q.op(q.eng.activation, pcb[k][:],
                                 mbb[k][:].bitcast(BF16), AF.Relu, bias=1.0,
                                 scale=-1.0, deps=b_mb[k].w + b_pc[k].wd())
                        b_pc[k].owr(t)
                        b_mb[k].ord(t)
                pr.add("s", f_s)

                def f_v2(q, u=u):
                    for k in range(4):
                        kt = 4 * u + k
                        wsl = wd[:, kt * 512:(kt + 1) * 512]
                        t1 = q.op(q.eng.tensor_tensor, wsl, mbb[k][:].bitcast(BF16),
                                  pcb[k][:], ALU.subtract,
                                  deps=b_pc[k].w + b_mb[k].w +
                                  (bwd.wd() if (u == 0 and k == 0) else []))
                        t2 = q.op(q.eng.tensor_tensor, wsl.bitcast(U16),
                                  wsl.bitcast(U16), sgb[k][:], ALU.bitwise_or,
                                  deps=[t1] + b_sg[k].w)
                        t3 = q.op(q.eng.tensor_tensor, wsl, wsl,
                                  sclb[:, u * 512:(u + 1) * 512], ALU.mult,
                                  deps=[t2] + b_scl.w)
                        if u == 0 and k == 0:
                            bwd.owr(t1)
                        bwd.owp(t3)
                        b_pc[k].ord(t3)
                        b_sg[k].ord(t3)
                        b_mb[k].ord(t3)
                        b_scl.ord(t3)
                pr.add("v", f_v2)

            return [head] + [(lambda u=u: uchunk(u)) for u in range(nu)]

        allblocks = [(e, kind, idx) for e in range(17)
                     for (kind, idx) in ([("gu", nb) for nb in order_g]
                                         + [("dn", hb) for hb in range(4)])]
        NB = len(allblocks)

        def emit_load(bi):
            e, kind, idx = allblocks[bi]
            slot = bi % 2
            ub_, bu = u16b[slot], b_u16[slot]
            gs_, bgs = gss[slot], b_gss[slot]
            if kind == "gu":
                def gu_load(q, e=e, nb=idx):
                    t = q.dma("du", ub_[:], gu16[e, nb], deps=bu.wd())
                    bu.owr(t)

                def gss_load(q, e=e, nb=idx):
                    tg = q.dma("dgs", gs_[:], gsc[e, nb], deps=bgs.wd())
                    bgs.owr(tg)
            else:
                def gu_load(q, e=e, hb=idx):
                    t = q.dma("du", ub_[:, 0:3 * 512], d16[e, hb],
                              deps=bu.wd())
                    bu.owr(t)

                def gss_load(q, e=e, hb=idx):
                    tg = q.dma("dgs", gs_[0:4, 0:3 * 512], dsc[e, hb],
                               deps=bgs.wd())
                    bgs.owr(tg)
            pr.add("sy", gu_load)
            pr.add("g", gss_load)

        def emit_mm(bi, deq):
            e, kind, idx = allblocks[bi]
            slot = bi % 2
            wd = wdq_[slot]
            bwd = b_wd[slot]
            steps = []
            if kind == "gu":
                nb = idx
                is_up = nb >= 3
                jbase = (nb - 3) * 4 if is_up else nb * 4
                for ntl in range(4):
                    j = jbase + ntl
                    for tc0 in range(0, TNC, 2):
                        tcs = [tc for tc in (tc0, tc0 + 1) if tc < TNC]

                        def step(ntl=ntl, tcs=tcs, j=j, is_up=is_up):
                            def gu_t(q):
                                for tc in tcs:
                                    psl = tc % 2
                                    pairs = [(wd[:, kt * 512 + ntl * 128:
                                                 kt * 512 + ntl * 128 + 128],
                                              xnv[:, kt, tc * TW:(tc + 1) * TW])
                                             for kt in range(16)]
                                    t = q.mm_group(p5[psl][:, 0:TW], pairs,
                                                   deps=bwd.w + b_xnf.w + b_p5[psl].wd())
                                    b_p5[psl].owr(t)
                                    bwd.ord(t)
                                    b_xnf.ord(t)
                            pr.add("t", gu_t)

                            if not is_up:
                                def gu_s(q):
                                    for tc in tcs:
                                        psl = tc % 2
                                        t = q.op(q.eng.activation,
                                                 actv[:, j, tc * TW:(tc + 1) * TW],
                                                 p5[psl][:, 0:TW], AF.Silu,
                                                 deps=b_p5[psl].rd() + b_act.w + b_act.r)
                                        b_act.owp(t)
                                        b_p5[psl].ord(t)
                                pr.add("s", gu_s)
                            else:
                                def gu_v(q):
                                    for tc in tcs:
                                        psl = tc % 2
                                        asl = actv[:, j, tc * TW:(tc + 1) * TW]
                                        t = q.op(q.eng.tensor_tensor, asl, asl,
                                                 p5[psl][:, 0:TW], ALU.mult,
                                                 deps=b_p5[psl].rd() + b_act.w)
                                        b_act.owp(t)
                                        b_p5[psl].ord(t)
                                pr.add("v", gu_v)
                        steps.append(step)
            else:
                hb = idx
                for tg in range(NTT // G4):
                    def step(tg=tg, hb=hb, e=e):
                        tis = list(range(G4))

                        def dn_t(q):
                            for ti in tis:
                                tt = tg * G4 + ti
                                psl = ti % 2
                                pairs = [(actv[:, kt, tt * 128:(tt + 1) * 128],
                                          wd[:, kt * 512:(kt + 1) * 512])
                                         for kt in range(12)]
                                t = q.mm_group(p5[psl][:], pairs,
                                               deps=b_act.w + bwd.w + b_p5[psl].wd())
                                b_p5[psl].owr(t)
                                bwd.ord(t)
                                b_act.ord(t)
                        pr.add("t", dn_t)

                        def dn_v(q):
                            ssl = tg % 2
                            for ti in tis:
                                tt = tg * G4 + ti
                                psl = ti % 2
                                dst = stg[ssl][:, ti * 512:(ti + 1) * 512]
                                if e < 16:
                                    t = q.op(q.eng.tensor_scalar, dst, p5[psl][:],
                                             cw[:, tt * 16 + e:tt * 16 + e + 1], None,
                                             ALU.mult,
                                             deps=b_p5[psl].rd() + (b_stg[ssl].wd() if ti == 0 else []) + b_cw.w)
                                else:
                                    t = q.op(q.eng.tensor_copy, dst, p5[psl][:],
                                             deps=b_p5[psl].rd() + (b_stg[ssl].wd() if ti == 0 else []))
                                b_stg[ssl].owr(t) if ti == 0 else b_stg[ssl].owp(t)
                                b_p5[psl].ord(t)
                        pr.add("v", dn_v)

                        def dn_acc(q):
                            ssl = tg % 2
                            kw = {} if e == 0 else {"accum_op": ALU.add}
                            gate = [Tok("ga", 16 * NACC * e)] if e > 0 else []
                            t = q.dma("ga",
                                      gacc[hb, tg * G4:(tg + 1) * G4]
                                      .rearrange("a p n -> p a n"),
                                      stg[ssl][:].rearrange("p (a n) -> p a n", n=512),
                                      deps=b_stg[ssl].w + gate, **kw)
                            b_stg[ssl].ord(t)
                            b_gacc.owp(t)
                        pr.add("g", dn_acc)
                    steps.append(step)

            if deq:
                deq[0]()
                nch = len(deq) - 1
                done = 0
                for i, st in enumerate(steps):
                    st()
                    want = min(nch, ((i + 2) * nch) // len(steps))
                    while done < want:
                        deq[1 + done]()
                        done += 1
                while done < nch:
                    deq[1 + done]()
                    done += 1
            else:
                for st in steps:
                    st()

        emit_load(0)
        emit_load(1)
        for c in deq_chunks(4 if allblocks[0][1] == "gu" else 3, 0):
            c()
        for bi in range(NB):
            if bi + 2 < NB:
                emit_load(bi + 2)
            deq = None
            if bi + 1 < NB:
                deq = deq_chunks(4 if allblocks[bi + 1][1] == "gu" else 3,
                                 (bi + 1) % 2)
            emit_mm(bi, deq)
        pr.flush("p3")
        t_ga = Tok("ga", ctx.cnt.get("ga", 0))

    with contextlib.ExitStack() as st_:
        sbp = lambda n, s, d: st_.enter_context(nc.sbuf_tensor(n, s, d))
        gac = sbp("gac", [128, H], F32)
        gsl = sbp("gsl", [128, H], BF16)
        s_in = sbp("s_in", [128, H], BF16)
        x_in = sbp("x_in", [128, H], F32)
        sgf = sbp("sgf", [128, H], F32)
        b_gac, b_gsl, b_sin, b_xin, b_sgf = (B() for _ in range(5))

        for tt in range(NTT):
            def p4_l(q, tt=tt):
                gv = gac[:].rearrange("p (a n) -> p a n", n=512)
                t1 = q.dma("dp4g", gv, gacc[:, tt].rearrange("a p n -> p a n"),
                           deps=b_gac.wd() + [t_ga])
                b_gac.owr(t1)
                t2 = q.dma("dp4s", s_in[:], s_sp[tt], deps=b_sin.wd() + b_ssp.w)
                b_sin.owr(t2)
                t3 = q.dma("dp4x", x_in[:], x_loc[tt * 128:(tt + 1) * 128, :],
                           deps=b_xin.wd())
                b_xin.owr(t3)
            pr.add("sy", p4_l)

            def p4_s(q):
                t = q.op(q.eng.activation, gsl[:], gac[:], AF.Silu,
                         deps=b_gac.w + b_gsl.wd())
                b_gsl.owr(t)
                b_gac.ord(t)
            pr.add("s", p4_s)

            def p4_v(q):
                t1 = q.op(q.eng.tensor_tensor, sgf[:], s_in[:], gsl[:], ALU.mult,
                          deps=b_gsl.w + b_sin.w + b_sgf.wd())
                b_sin.ord(t1)
                b_gsl.ord(t1)
                t2 = q.op(q.eng.tensor_tensor, sgf[:], x_in[:], sgf[:], ALU.add,
                          deps=[t1] + b_xin.w)
                b_sgf.owr(t2)
                b_xin.ord(t2)
            pr.add("v", p4_v)

            def p4_o(q, tt=tt):
                t = q.dma("dout", out_h[tt * 128:(tt + 1) * 128, :], sgf[:],
                          deps=b_sgf.w)
                b_sgf.ord(t)
            pr.add("sy", p4_o)

        def fin(q):
            q.wait(Tok("dout", ctx.cnt["dout"]))
        pr.add("sy", fin)
        pr.flush("p4")

    return nc


def _mk_pi(n):
    r = np.arange(n)
    tile, p = r // 128, r % 128
    u, k = tile // 4, tile % 4
    return 512 * u + 4 * p + k


def _pack_u16(packed):
    k8, n = packed.shape
    return np.ascontiguousarray(
        packed.view(np.uint16).reshape(k8, n, 2).transpose(0, 2, 1).reshape(k8 * 2, n))


def _resc(s, nu):
    g, n = s.shape
    return np.ascontiguousarray(
        s.reshape(nu, 4, n).transpose(1, 0, 2).reshape(4, nu * n))


def prep_inputs(inputs):
    x = np.asarray(inputs["L0_attn_x"], dtype=np.float32)
    B_, S, H_ = x.shape
    PIH = _mk_pi(H)
    PII = _mk_pi(I2)
    colsI = np.concatenate([PII, I2 + PII])

    wdkv = np.ascontiguousarray(np.asarray(inputs["L0_attn_W_dkv"])[PIH])
    wdq = np.ascontiguousarray(np.asarray(inputs["L0_attn_W_dq"])[PIH])
    wkr = np.ascontiguousarray(np.asarray(inputs["L0_attn_W_kr"])[PIH])
    wuk = np.ascontiguousarray(np.asarray(inputs["L0_attn_W_uk"]))
    wuv = np.ascontiguousarray(np.asarray(inputs["L0_attn_W_uv"]))
    wuq = np.ascontiguousarray(np.asarray(inputs["L0_attn_W_uq"]))
    wqr = np.ascontiguousarray(np.asarray(inputs["L0_attn_W_qr"]))
    wout = np.ascontiguousarray(np.asarray(inputs["L0_attn_W_out"]))
    gwT = np.ascontiguousarray(np.asarray(inputs["L0_ffn_gate_w"]).T[PIH])

    gup = np.asarray(inputs["L0_ffn_gate_up_packed"])
    gus = np.asarray(inputs["L0_ffn_gate_up_scales"])
    dnp = np.asarray(inputs["L0_ffn_down_packed"])
    dns = np.asarray(inputs["L0_ffn_down_scales"])
    sgup = np.asarray(inputs["L0_ffn_shared_gate_up_packed"])
    sgus = np.asarray(inputs["L0_ffn_shared_gate_up_scales"])
    sdnp = np.asarray(inputs["L0_ffn_shared_down_packed"])
    sdns = np.asarray(inputs["L0_ffn_shared_down_scales"])

    gu16 = np.stack([_pack_u16(gup[e])[:, colsI] for e in range(16)]
                    + [_pack_u16(sgup)[:, colsI]])
    gu16 = np.ascontiguousarray(
        gu16.reshape(17, 4, 128, 6, 512).transpose(0, 3, 2, 1, 4)
        .reshape(17, 6, 128, 2048))
    gscv = np.stack([_resc(gus[e][:, colsI], 4) for e in range(16)]
                    + [_resc(sgus[:, colsI], 4)]).astype(np.float32)
    gscv = np.ascontiguousarray(
        gscv.reshape(17, 4, 4, 6, 512).transpose(0, 3, 1, 2, 4)
        .reshape(17, 6, 4, 2048))
    d16 = np.stack([_pack_u16(dnp[e]) for e in range(16)] + [_pack_u16(sdnp)])
    d16 = np.ascontiguousarray(
        d16.reshape(17, 3, 128, 4, 512).transpose(0, 3, 2, 1, 4)
        .reshape(17, 4, 128, 1536))
    dscv = np.stack([_resc(dns[e], 3) for e in range(16)]
                    + [_resc(sdns, 3)]).astype(np.float32)
    dscv = np.ascontiguousarray(
        dscv.reshape(17, 4, 3, 4, 512).transpose(0, 3, 1, 2, 4)
        .reshape(17, 4, 4, 1536))

    half_dim = DR // 2
    inv = (1.0 / 10000.0 ** (np.arange(0, half_dim, 2, dtype=np.float32)
                             / np.float32(half_dim))).astype(np.float32)
    tpos = (np.arange(S, dtype=np.float32) / np.float32(40.0))
    fr = np.outer(tpos, inv).astype(np.float32)
    rot = np.concatenate([fr, fr], axis=1)
    cosT = np.tile(np.cos(rot), (1, NH)).astype(np.float32)
    sinT = np.tile(np.sin(rot), (1, NH)).astype(np.float32)

    identd = np.eye(128, dtype=np.float32)
    ind4d = (np.arange(128)[None, :] // 32 == np.arange(4)[:, None]
             ).astype(np.float32)

    Tc = S // 2
    shared = dict(wdkv=wdkv, wdq=wdq, wkr=wkr, wuk=wuk, wuv=wuv, wuq=wuq,
                  wqr=wqr, wout=wout, gw=gwT, gu16=gu16, gsc=gscv, d16=d16,
                  dsc=dscv, identd=identd, ind4d=ind4d)
    in_maps = []
    for c in range(8):
        b, hf = c // 2, c % 2
        oth = 1 - hf
        m = dict(shared)
        m["x_loc"] = np.ascontiguousarray(x[b, hf * Tc:(hf + 1) * Tc])
        m["x_oth"] = np.ascontiguousarray(x[b, oth * Tc:(oth + 1) * Tc])
        m["cos_a"] = np.ascontiguousarray(np.concatenate(
            [cosT[hf * Tc:(hf + 1) * Tc], cosT[oth * Tc:(oth + 1) * Tc]]))
        m["sin_a"] = np.ascontiguousarray(np.concatenate(
            [sinT[hf * Tc:(hf + 1) * Tc], sinT[oth * Tc:(oth + 1) * Tc]]))
        in_maps.append(m)

    return in_maps, S, (B_, H_)


def kernel(**inputs):
    in_maps, S, (B_, H_) = prep_inputs(inputs)
    Tc = S // 2
    nc = build_nc(S)
    res = run_bass_kernel_spmd(nc, in_maps, core_ids=list(range(8)))
    out = np.empty((B_, S, H_), dtype=np.float32)
    for c in range(8):
        b, hf = c // 2, c % 2
        out[b, hf * Tc:(hf + 1) * Tc, :] = res.results[c]["out"]
    return out


# version 14 (final)
# speedup vs baseline: 1.3917x; 1.0635x over previous
import contextlib
import numpy as np
import concourse.bass as bass
import concourse.mybir as mybir
from concourse.bass_utils import run_bass_kernel_spmd

F32 = mybir.dt.float32
BF16 = mybir.dt.bfloat16
U16 = mybir.dt.uint16
U8 = mybir.dt.uint8
AF = mybir.ActivationFunctionType
ALU = mybir.AluOpType
AX = mybir.AxisListType

H, CD, UP = 2048, 1024, 1024
NH, DH, DR = 16, 128, 64
I2 = 1536
RSC = 0.08838834764831845
ENG = {"v": "vector", "s": "scalar", "t": "tensor", "g": "gpsimd", "sy": "sync"}


class Tok:
    __slots__ = ("key", "val")

    def __init__(self, key, val):
        self.key = key
        self.val = val


class Ctx:
    def __init__(self, nc):
        self.nc = nc
        self.sems = {}
        self.cnt = {}
        self.hwm = {}
        self.guards = []

    def sem(self, key):
        if key not in self.sems:
            g = self.nc.semaphore(f"S_{key}")
            self.sems[key] = g.__enter__()
            self.guards.append(g)
            self.cnt[key] = 0
        return self.sems[key]


class Q:
    def __init__(self, ctx, name, eng):
        self.ctx = ctx
        self.name = name
        self.eng = eng
        ctx.sem(name)

    def wait(self, *deps):
        best = {}
        for d in deps:
            if d is None:
                continue
            for dd in (d if isinstance(d, (list, tuple)) else [d]):
                if dd is not None and (dd.key not in best or best[dd.key] < dd.val):
                    best[dd.key] = dd.val
        for key, val in best.items():
            hk = (self.name, key)
            if self.ctx.hwm.get(hk, 0) >= val:
                continue
            self.eng.wait_ge(self.ctx.sem(key), val)
            self.ctx.hwm[hk] = val

    def op(self, fn, *args, deps=(), inc=1, **kwargs):
        self.wait(*deps)
        inst = fn(*args, **kwargs)
        if inc:
            self.ctx.cnt[self.name] += inc
            inst.then_inc(self.ctx.sem(self.name), inc)
        return Tok(self.name, self.ctx.cnt[self.name])

    def dma(self, key, out, in_, deps=(), **kwargs):
        self.wait(*deps)
        inst = self.eng.dma_start(out, in_, **kwargs)
        self.ctx.sem(key)
        self.ctx.cnt[key] += 16
        inst.then_inc(self.ctx.sem(key), 16)
        return Tok(key, self.ctx.cnt[key])

    def mm_group(self, out_ap, pairs, deps=(), tile_position=None, start=True,
                 stop=True):
        kw = {} if tile_position is None else {"tile_position": tile_position}
        n = len(pairs)
        inst = None
        for i, (l, r) in enumerate(pairs):
            if i == 0:
                self.wait(*deps)
            inst = self.eng.matmul(out_ap, l, r, start=(start and i == 0),
                                   stop=(stop and i == n - 1), **kw)
        self.ctx.cnt[self.name] += 1
        inst.then_inc(self.ctx.sem(self.name), 1)
        return Tok(self.name, self.ctx.cnt[self.name])


class Buf:
    __slots__ = ("w", "r")

    def __init__(self):
        self.w = []
        self.r = []

    def rd(self):
        return list(self.w)

    def ord(self, tok):
        self.r.append(tok)

    def wd(self):
        return self.w + self.r

    def owr(self, tok):
        self.w = [tok]
        self.r = []

    def owp(self, tok):
        self.w.append(tok)


class Prog:
    def __init__(self, nc, ctx):
        self.nc = nc
        self.ctx = ctx
        self.steps = []

    def add(self, eng, fn):
        self.steps.append((eng, fn))

    def drain_dma(self):
        def mk(q):
            q.wait(*[Tok(k, v) for k, v in list(self.ctx.cnt.items())
                     if k not in ("v", "s", "t", "g", "sy") and v > 0])
        self.add("sy", mk)
        self.add("g", mk)

    def flush(self, name):
        self.drain_dma()
        steps, self.steps = self.steps, None
        with self.nc.Block(name) as blk:
            i = 0
            while i < len(steps):
                eng = steps[i][0]
                fns = [steps[i][1]]
                while i + 1 < len(steps) and steps[i + 1][0] == eng:
                    i += 1
                    fns.append(steps[i][1])

                def sec(engine, fns=fns, eng=eng):
                    q = Q(self.ctx, eng, engine)
                    for f in fns:
                        f(q)
                getattr(blk, ENG[eng])(sec)
                i += 1
        self.steps = []


def build_nc(S_all):
    Tc = S_all // 2
    NTT = Tc // 128
    NTTA = S_all // 128
    TW = min(512, Tc)
    TNC = Tc // TW
    G4 = min(2, NTT)
    NACC = (NTT // G4) * 4

    nc = bass.Bass("TRN2", target_bir_lowering=False)
    dt = lambda n, s, d: nc.dram_tensor(n, s, d, kind="ExternalInput").ap()
    x_loc = dt("x_loc", [Tc, H], F32)
    x_oth = dt("x_oth", [Tc, H], F32)
    wdkv = dt("wdkv", [H, CD], F32)
    wdq = dt("wdq", [H, CD], F32)
    wkr = dt("wkr", [H, NH * DR], F32)
    wuk = dt("wuk", [CD, UP], F32)
    wuv = dt("wuv", [CD, NH * DH], F32)
    wuq = dt("wuq", [CD, UP], F32)
    wqr = dt("wqr", [UP, NH * DR], F32)
    wout = dt("wout", [NH * DH, H], F32)
    gw = dt("gw", [H, 16], F32)
    gu16 = dt("gu16", [17, 6, 128, 2048], U16)
    gsc = dt("gsc", [17, 6, 4, 2048], F32)
    d16 = dt("d16", [17, 4, 128, 1536], U16)
    dsc = dt("dsc", [17, 4, 4, 1536], F32)
    cos_a = dt("cos_a", [S_all, 512], F32)
    sin_a = dt("sin_a", [S_all, 512], F32)
    identd = dt("identd", [128, 128], F32)
    ind4d = dt("ind4d", [4, 128], F32)

    out_h = nc.dram_tensor("out", [Tc, H], F32, kind="ExternalOutput").ap()

    di = lambda n, s, d: nc.dram_tensor(n, s, d, kind="Internal").ap()
    xn_sp = di("xn_sp", [NTTA, 128, H], BF16)
    ckv_sp = di("ckv_sp", [NTTA, 128, CD], BF16)
    cq_sp = di("cq_sp", [NTTA, 128, CD], BF16)
    v_sp = di("v_sp", [NTTA, 128, NH * DH], BF16)
    kf_sp = di("kf_sp", [NTT, 128, NH * 128], BF16)
    s_sp = di("s_sp", [NTT, 128, H], BF16)
    gacc = di("gacc", [4, NTT, 128, 512], F32)

    ctx = Ctx(nc)
    pr = Prog(nc, ctx)
    B = Buf
    es = contextlib.ExitStack()

    def sb(name, shape, dtyp):
        return es.enter_context(nc.sbuf_tensor(name, shape, dtyp))

    def ps_(name, shape, dtyp):
        return es.enter_context(nc.psum_tensor(name, shape, dtyp))

    ident = nc.alloc_sbuf_tensor("ident", [128, 128], BF16)
    ind4 = nc.alloc_sbuf_tensor("ind4", [4, 128], BF16)
    gw_sb = nc.alloc_sbuf_tensor("gw_sb", [128, 16 * 16], BF16)
    logit = nc.alloc_sbuf_tensor("logit", [128, NTT * 16], F32)
    cw = nc.alloc_sbuf_tensor("cw", [128, NTT * 16], F32)
    tau = nc.alloc_sbuf_tensor("tau", [128, NTT], F32)
    msb = nc.alloc_sbuf_tensor("msb", [128, NH * DH], BF16)
    neg = nc.alloc_sbuf_tensor("neg", [128, 16], F32)
    lw = nc.alloc_sbuf_tensor("lw", [128, 16], F32)
    sel = nc.alloc_sbuf_tensor("sel", [128, 16], F32)
    ela = nc.alloc_sbuf_tensor("ela", [128, 16], F32)
    mku = nc.alloc_sbuf_tensor("mku", [128, 16], U8)
    st8 = nc.alloc_sbuf_tensor("st8", [128, 8], F32)
    cst = nc.alloc_sbuf_tensor("cst", [128, 2], F32)

    (b_const, b_logit, b_cw, b_tau, b_msb, b_lw, b_sel, b_ela, b_mku,
     b_st8) = (B() for _ in range(10))
    (b_xnsp, b_ckvsp, b_cqsp, b_vsp, b_kfsp, b_ssp, b_gacc) = (B() for _ in range(7))

    def p0(q):
        q.dma("dcst", ident[:], identd)
        q.dma("dcst", ind4[:], ind4d)
        t = q.dma("dcst", gw_sb[:].rearrange("p (a n) -> p a n", n=16),
                  gw.rearrange("(a p) n -> p a n", p=128))
        b_const.owr(t)
        b_const.owp(q.op(q.eng.memset, neg[:], -1e30))
        b_const.owp(q.op(q.eng.memset, cst[:, 0:1], -0.5))
        b_const.owp(q.op(q.eng.memset, cst[:, 1:2], 1e-5))
    pr.add("g", p0)
    pr.flush("p0")

    def x_tile(gtt):
        if gtt < NTT:
            return x_loc[gtt * 128:(gtt + 1) * 128, :]
        return x_oth[(gtt - NTT) * 128:(gtt - NTT + 1) * 128, :]

    def fm_proj_steps(wbuf, b_w, inb, b_in, ob, b_o, ktn, ntile, mm2, b_m2,
                      extra=None):
        for half in range(0, ntile, 2):
            def tstep(q, half=half):
                for n in (half, half + 1):
                    if n >= ntile:
                        break
                    psl = n % 2
                    pairs = [(wbuf[:, (kt * ntile + n) * 128:(kt * ntile + n) * 128 + 128],
                              inb[:, kt * 128:(kt + 1) * 128]) for kt in range(ktn)]
                    t = q.mm_group(mm2[psl][:, 0:128], pairs,
                                   deps=b_in.w + b_m2[psl].wd() + b_w.w)
                    b_m2[psl].owr(t)
                    b_in.ord(t)
            pr.add("t", tstep)

            def vstep(q, half=half):
                for n in (half, half + 1):
                    if n >= ntile:
                        break
                    psl = n % 2
                    tv = q.op(q.eng.tensor_copy, ob[:, n * 128:(n + 1) * 128],
                              mm2[psl][:, 0:128],
                              deps=b_m2[psl].rd() + (b_o.wd() if n == 0 else []))
                    if n == 0:
                        b_o.owr(tv)
                    else:
                        b_o.owp(tv)
                    b_m2[psl].ord(tv)
            pr.add("v", vstep)
        if extra is not None:
            extra()

    def rope_steps(src, b_src, dstv, b_dst, csb, b_cs, ssbt, b_ss, rp1, b_r1,
                   rp2, b_r2):
        def f(q):
            sv = src[:].rearrange("p (h j) -> p h j", j=64)
            cv = csb[:].rearrange("p (h j) -> p h j", j=32)
            sv2 = ssbt[:].rearrange("p (h j) -> p h j", j=32)
            r1v = rp1[:].rearrange("p (h j) -> p h j", j=32)
            r2v = rp2[:].rearrange("p (h j) -> p h j", j=16)
            t0 = q.op(q.eng.tensor_copy, dstv[:, :, 32:64], sv[:, :, 32:64],
                      deps=b_src.w + b_cs.w + b_ss.w + b_dst.w + b_dst.r)
            b_dst.owp(t0)
            t1 = q.op(q.eng.tensor_tensor, r1v, sv[:, :, 0:32], cv, ALU.mult,
                      deps=b_r1.wd())
            b_r1.owr(t1)
            t2 = q.op(q.eng.tensor_tensor, r2v, sv[:, :, 16:32], sv2[:, :, 0:16],
                      ALU.mult, deps=b_r2.wd())
            b_r2.owr(t2)
            t3 = q.op(q.eng.tensor_tensor, dstv[:, :, 0:16], r1v[:, :, 0:16], r2v,
                      ALU.subtract, deps=[t1, t2])
            b_dst.owp(t3)
            b_r2.ord(t3)
            t4 = q.op(q.eng.tensor_tensor, r2v, sv[:, :, 0:16], sv2[:, :, 16:32],
                      ALU.mult, deps=b_r2.wd())
            b_r2.owr(t4)
            t5 = q.op(q.eng.tensor_tensor, dstv[:, :, 16:32], r1v[:, :, 16:32], r2v,
                      ALU.add, deps=[t4])
            b_dst.owp(t5)
            b_src.ord(t5)
            b_r1.ord(t5)
            b_r2.ord(t5)
            b_cs.ord(t5)
            b_ss.ord(t5)
        pr.add("v", f)

    with contextlib.ExitStack() as st_:
        sbp = lambda n, s, d: st_.enter_context(nc.sbuf_tensor(n, s, d))
        psp = lambda n, s, d: st_.enter_context(nc.psum_tensor(n, s, d))
        wA = sbp("wA", [128, 16 * CD], BF16)
        wA2 = sbp("wA2", [128, 16 * CD], BF16)
        xt = [sbp(f"xt{i}", [128, H], F32) for i in range(2)]
        xnt = [sbp(f"xnt{i}", [128, H], BF16) for i in range(2)]
        sqs = sbp("sqs", [128, H], BF16)
        xnp = [sbp(f"xnp{i}", [128, H], BF16) for i in range(2)]
        ckvb = sbp("ckvb", [128, CD], BF16)
        cqob = sbp("cqob", [128, CD], BF16)
        wA3 = sbp("wA3", [128, 16 * 1024], BF16)
        krt = sbp("krt", [128, 1024], BF16)
        krot = sbp("krot", [128, 1024], BF16)
        csb = sbp("csb", [128, 512], BF16)
        snb = sbp("snb", [128, 512], BF16)
        rp1 = sbp("rp1", [128, 512], BF16)
        rp2 = sbp("rp2", [128, 256], BF16)
        kch = sbp("a3kch", [128, NH * 128], BF16)
        pips = psp("pips", [128, 512], BF16)
        mm2 = [psp(f"mm2_{i}", [128, 128], F32) for i in range(2)]
        lgps = psp("lgps", [128, 16], F32)
        p5 = [psp(f"a3p5{i}", [128, 512], F32) for i in range(2)]
        tpsb = psp("a3tps", [128, 512], BF16)
        b_wA = B()
        b_wA2 = B()
        b_xt = [B(), B()]
        b_xnt = [B(), B()]
        b_sq = B()
        b_xnp = [B(), B()]
        b_ckv = B()
        b_cqo = B()
        b_pips = B()
        b_m2 = [B(), B()]
        b_lg = B()
        b_w3 = B()
        b_krt, b_krot, b_cs, b_ss, b_r1, b_r2, b_kch, b_tps = (B() for _ in range(8))
        b_p5 = [B(), B()]

        def a1_w(q):
            b_wA.owr(q.dma("dw", wA[:].rearrange("p (a n) -> p a n", n=CD),
                           wdkv.rearrange("(a p) n -> p a n", p=128)))
            b_wA2.owr(q.dma("dw", wA2[:].rearrange("p (a n) -> p a n", n=CD),
                            wdq.rearrange("(a p) n -> p a n", p=128)))
            b_w3.owr(q.dma("dw", wA3[:].rearrange("p (a n) -> p a n", n=1024),
                           wkr.rearrange("(a p) n -> p a n", p=128)))
        pr.add("g", a1_w)

        def mk_xload(gtt, sl):
            def f(q):
                t = q.dma(f"dx{sl}", xt[sl][:], x_tile(gtt), deps=b_xt[sl].wd())
                b_xt[sl].owr(t)
            return f
        pr.add("sy", mk_xload(0, 0))
        if NTTA > 1:
            pr.add("sy", mk_xload(1, 1))

        for gtt in range(NTTA):
            sl = gtt % 2

            def ln_v1(q, sl=sl):
                t = q.op(q.eng.reduce_sum, st8[:, 0:1], xt[sl][:], axis=AX.X,
                         deps=b_xt[sl].rd() + b_st8.wd())
                b_st8.owr(t)
                b_xt[sl].ord(t)
            pr.add("v", ln_v1)

            def ln_s1(q, sl=sl):
                t = q.op(q.eng.activation, sqs[:], xt[sl][:], AF.Square,
                         accum_out=st8[:, 1:2],
                         deps=b_xt[sl].rd() + b_sq.wd() + b_st8.w + b_st8.r)
                b_sq.owr(t)
                b_st8.owp(t)
                b_xt[sl].ord(t)
            pr.add("s", ln_s1)

            def ln_v2(q):
                t1 = q.op(q.eng.tensor_scalar, st8[:, 2:3], st8[:, 0:1], 1.0 / H,
                          None, ALU.mult, deps=b_st8.w)
                t2 = q.op(q.eng.tensor_scalar, st8[:, 3:4], st8[:, 2:3],
                          st8[:, 2:3], None, ALU.mult, deps=[t1])
                t3 = q.op(q.eng.scalar_tensor_tensor, st8[:, 4:5], st8[:, 1:2],
                          1.0 / H, st8[:, 3:4], ALU.mult, ALU.subtract, deps=[t2])
                b_st8.owp(t3)
            pr.add("v", ln_v2)

            def ln_s2(q):
                t1 = q.op(q.eng.activation, st8[:, 5:6], st8[:, 4:5], AF.Ln,
                          bias=cst[:, 1:2], deps=b_st8.w + b_const.w)
                t2 = q.op(q.eng.activation, st8[:, 6:7], st8[:, 5:6], AF.Exp,
                          scale=cst[:, 0:1], deps=[t1])
                b_st8.owp(t2)
            pr.add("s", ln_s2)

            def ln_v3(q, sl=sl):
                t = q.op(q.eng.tensor_scalar, xnt[sl][:], xt[sl][:], st8[:, 2:3],
                         st8[:, 6:7], ALU.subtract, ALU.mult,
                         deps=b_st8.w + b_xnt[sl].wd() + b_xt[sl].rd())
                b_xnt[sl].owr(t)
                b_xt[sl].ord(t)
                b_st8.ord(t)
            pr.add("v", ln_v3)

            if gtt + 2 < NTTA:
                pr.add("sy", mk_xload(gtt + 2, sl))

            for grp in range(4):
                def pi_t(q, sl=sl, grp=grp):
                    tk = None
                    for j in range(4):
                        tu = grp * 4 + j
                        u, k = tu // 4, tu % 4
                        tk = q.op(q.eng.transpose, pips[:, j * 128:(j + 1) * 128],
                                  xnt[sl][:, (512 * u + k):(512 * u + k + 509):4],
                                  ident[:],
                                  deps=(b_xnt[sl].w + b_pips.wd() + b_const.w)
                                  if j == 0 else (), inc=(1 if j == 3 else 0))
                    b_pips.owr(tk)
                    b_xnt[sl].ord(tk)
                pr.add("t", pi_t)

                def pi_v(q, sl=sl, grp=grp):
                    t = q.op(q.eng.tensor_copy, xnp[sl][:, grp * 512:(grp + 1) * 512],
                             pips[:],
                             deps=b_pips.rd() + (b_xnp[sl].wd() if grp == 0 else []))
                    if grp == 0:
                        b_xnp[sl].owr(t)
                    else:
                        b_xnp[sl].owp(t)
                    b_pips.ord(t)
                pr.add("v", pi_v)

            def xn_spill(q, gtt=gtt, sl=sl):
                t = q.dma(f"sxn{sl}", xn_sp[gtt], xnp[sl][:], deps=b_xnp[sl].w)
                b_xnp[sl].ord(t)
                b_xnsp.owp(t)
            pr.add("sy", xn_spill)

            for half in range(0, 8, 2):
                def ckv_t(q, sl=sl, half=half):
                    for n in (half, half + 1):
                        psl = n % 2
                        pairs = [(wA[:, (kt * 8 + n) * 128:(kt * 8 + n) * 128 + 128],
                                  xnp[sl][:, kt * 128:(kt + 1) * 128])
                                 for kt in range(16)]
                        t = q.mm_group(mm2[psl][:], pairs,
                                       deps=b_xnp[sl].w + b_m2[psl].wd() + b_wA.w)
                        b_m2[psl].owr(t)
                        b_xnp[sl].ord(t)
                pr.add("t", ckv_t)

                def ckv_v(q, half=half):
                    for n in (half, half + 1):
                        psl = n % 2
                        tv = q.op(q.eng.tensor_copy, ckvb[:, n * 128:(n + 1) * 128],
                                  mm2[psl][:],
                                  deps=b_m2[psl].rd() + (b_ckv.wd() if n == 0 else []))
                        if n == 0:
                            b_ckv.owr(tv)
                        else:
                            b_ckv.owp(tv)
                        b_m2[psl].ord(tv)
                pr.add("v", ckv_v)

            if gtt < NTT:
                def lg_t(q, gtt=gtt, sl=sl):
                    pairs = [(xnp[sl][:, kt * 128:(kt + 1) * 128],
                              gw_sb[:, kt * 16:(kt + 1) * 16]) for kt in range(16)]
                    t = q.mm_group(lgps[:], pairs,
                                   deps=b_xnp[sl].w + b_lg.wd() + b_const.w)
                    b_lg.owr(t)
                    b_xnp[sl].ord(t)
                pr.add("t", lg_t)

                def lg_v(q, gtt=gtt):
                    tv = q.op(q.eng.tensor_copy, logit[:, gtt * 16:(gtt + 1) * 16],
                              lgps[:], deps=b_lg.rd())
                    b_logit.owp(tv)
                    b_lg.ord(tv)
                pr.add("v", lg_v)

            def ckv_spill(q, gtt=gtt):
                t = q.dma("scv", ckv_sp[gtt], ckvb[:], deps=b_ckv.w)
                b_ckv.ord(t)
                b_ckvsp.owp(t)
            pr.add("sy", ckv_spill)

            fm_proj_steps(wA2, b_wA2, xnp[sl], b_xnp[sl], cqob, b_cqo, 16, 8,
                          mm2, b_m2)

            def cq_spill(q, gtt=gtt):
                t = q.dma("scq", cq_sp[gtt], cqob[:], deps=b_cqo.w)
                b_cqo.ord(t)
                b_cqsp.owp(t)
            pr.add("sy", cq_spill)

            if gtt < NTT:
                def csload(q, gtt=gtt):
                    q.dma("da3c", csb[:], cos_a[gtt * 128:(gtt + 1) * 128, :],
                          deps=b_cs.wd() + b_ss.wd())
                    ts_ = q.dma("da3c", snb[:], sin_a[gtt * 128:(gtt + 1) * 128, :])
                    b_cs.owr(ts_)
                    b_ss.owr(ts_)
                pr.add("g", csload)

                def a3_t(q, sl=sl):
                    for n in range(2):
                        pairs = [(xnp[sl][:, kt * 128:(kt + 1) * 128],
                                  wA3[:, kt * 1024 + n * 512: kt * 1024 + n * 512 + 512])
                                 for kt in range(16)]
                        t = q.mm_group(p5[n][:], pairs,
                                       deps=b_xnp[sl].w + b_p5[n].wd() + b_w3.w)
                        b_p5[n].owr(t)
                        b_xnp[sl].ord(t)
                pr.add("t", a3_t)

                def a3_v(q):
                    for n in range(2):
                        tv = q.op(q.eng.tensor_copy, krt[:, n * 512:(n + 1) * 512],
                                  p5[n][:],
                                  deps=b_p5[n].rd() + (b_krt.wd() if n == 0 else []))
                        if n == 0:
                            b_krt.owr(tv)
                        else:
                            b_krt.owp(tv)
                        b_p5[n].ord(tv)
                pr.add("v", a3_v)

                krotv = krot[:].rearrange("p (h j) -> p h j", j=64)
                rope_steps(krt, b_krt, krotv, b_krot, csb, b_cs, snb, b_ss,
                           rp1, b_r1, rp2, b_r2)

                for hg in range(4):
                    def a3_tr(q, hg=hg):
                        tk = None
                        for j in range(4):
                            h = hg * 4 + j
                            tk = q.op(q.eng.transpose,
                                      tpsb[64:128, j * 128:(j + 1) * 128],
                                      krot[:, h * 64:(h + 1) * 64], ident[:],
                                      tile_position=(0, 64),
                                      deps=(b_krot.w + b_tps.wd() + b_const.w)
                                      if j == 0 else (), inc=(1 if j == 3 else 0))
                        b_tps.owr(tk)
                        b_krot.ord(tk)
                    pr.add("t", a3_tr)

                    def a3_ev(q, hg=hg):
                        t = q.op(q.eng.tensor_copy,
                                 kch[64:128, hg * 512:(hg + 1) * 512], tpsb[64:128, :],
                                 deps=b_tps.rd() + (b_kch.wd() if hg == 0 else []))
                        if hg == 0:
                            b_kch.owr(t)
                        else:
                            b_kch.owp(t)
                        b_tps.ord(t)
                    pr.add("v", a3_ev)

                def a3_spill(q, gtt=gtt):
                    t = q.dma("skf", kf_sp[gtt, 64:128, :], kch[64:128, :],
                              deps=b_kch.w)
                    b_kch.ord(t)
                    b_kfsp.owp(t)
                pr.add("sy", a3_spill)

        def top4_v1(q):
            for tt in range(NTT):
                t0 = q.op(q.eng.tensor_copy, lw[:], logit[:, tt * 16:(tt + 1) * 16],
                          deps=b_logit.w + b_lw.wd())
                b_lw.owr(t0)
                b_logit.ord(t0)
                for r in range(4):
                    dst = st8[:, 0:1] if r < 3 else tau[:, tt:tt + 1]
                    t1 = q.op(q.eng.reduce_max, dst, lw[:], axis=AX.X,
                              deps=[t0] + (b_st8.wd() if r < 3 else b_tau.wd()))
                    if r < 3:
                        b_st8.owr(t1)
                        t2 = q.op(q.eng.tensor_scalar, mku[:], lw[:], dst, None,
                                  ALU.is_ge, deps=[t1] + b_mku.wd())
                        b_mku.owr(t2)
                        t0 = q.op(q.eng.copy_predicated, lw[:], mku[:], neg[:],
                                  deps=[t2] + b_const.w)
                        b_lw.owr(t0)
                    else:
                        b_tau.owp(t1)
                        b_lw.ord(t1)
        pr.add("v", top4_v1)

        def top4_pair(tt):
            def fs(q, tt=tt):
                t = q.op(q.eng.activation, ela[:], logit[:, tt * 16:(tt + 1) * 16],
                         AF.Exp, deps=b_logit.w + b_ela.wd() + b_tau.w)
                b_ela.owr(t)
            def fv(q, tt=tt):
                t1 = q.op(q.eng.tensor_scalar, sel[:], logit[:, tt * 16:(tt + 1) * 16],
                          tau[:, tt:tt + 1], None, ALU.is_ge,
                          deps=b_tau.w + b_logit.w + b_sel.wd())
                b_sel.owr(t1)
                t2 = q.op(q.eng.tensor_tensor, sel[:], ela[:], sel[:], ALU.mult,
                          deps=[t1] + b_ela.w)
                b_sel.owr(t2)
                b_ela.ord(t2)
                t3 = q.op(q.eng.reduce_sum, st8[:, 0:1], sel[:], axis=AX.X,
                          deps=[t2] + b_st8.wd())
                t4 = q.op(q.eng.reciprocal, st8[:, 1:2], st8[:, 0:1], deps=[t3])
                b_st8.owr(t4)
                t5 = q.op(q.eng.tensor_scalar, cw[:, tt * 16:(tt + 1) * 16], sel[:],
                          st8[:, 1:2], None, ALU.mult, deps=[t4])
                b_cw.owp(t5)
                b_sel.ord(t5)
                b_st8.ord(t5)
            return fs, fv
        for tt in range(NTT):
            fs, fv = top4_pair(tt)
            pr.add("s", fs)
            pr.add("v", fv)
        pr.flush("a1")

    with contextlib.ExitStack() as st_:
        sbp = lambda n, s, d: st_.enter_context(nc.sbuf_tensor(n, s, d))
        psp = lambda n, s, d: st_.enter_context(nc.psum_tensor(n, s, d))
        wB1 = sbp("wB1", [128, 8 * 2048], BF16)
        wB3 = sbp("wB3", [128, 8 * 1024], BF16)
        cin = [sbp(f"b1in{i}", [128, CD], BF16) for i in range(2)]
        vtb = sbp("vtb", [128, 2048], BF16)
        kch = sbp("b3kch", [128, NH * 128], BF16)
        p5 = [psp(f"b1p5{i}", [128, 512], F32) for i in range(2)]
        b_w, b_in, b_vt = B(), [B(), B()], B()
        b_w3, b_kch = B(), B()
        b_p5 = [B(), B()]

        def b1_w(q):
            b_w.owr(q.dma("dw", wB1[:].rearrange("p (a n) -> p a n", n=2048),
                          wuv.rearrange("(a p) n -> p a n", p=128)))
            b_w3.owr(q.dma("dw", wB3[:].rearrange("p (a n) -> p a n", n=1024),
                           wuk.rearrange("(a p) n -> p a n", p=128)))
        pr.add("g", b1_w)

        for gtt in range(NTTA):
            sl = gtt % 2

            def iload(q, gtt=gtt, sl=sl):
                t = q.dma(f"db1{sl}", cin[sl][:], ckv_sp[gtt],
                          deps=b_in[sl].wd() + b_ckvsp.w)
                b_in[sl].owr(t)
            pr.add("sy", iload)

            for half in range(0, 4, 2):
                def b1_t(q, sl=sl, half=half):
                    for n in (half, half + 1):
                        psl = n % 2
                        pairs = [(cin[sl][:, kt * 128:(kt + 1) * 128],
                                  wB1[:, kt * 2048 + n * 512: kt * 2048 + n * 512 + 512])
                                 for kt in range(8)]
                        t = q.mm_group(p5[psl][:], pairs,
                                       deps=b_in[sl].w + b_p5[psl].wd() + b_w.w)
                        b_p5[psl].owr(t)
                        b_in[sl].ord(t)
                pr.add("t", b1_t)

                def b1_v(q, half=half):
                    for n in (half, half + 1):
                        psl = n % 2
                        tv = q.op(q.eng.tensor_copy, vtb[:, n * 512:(n + 1) * 512],
                                  p5[psl][:],
                                  deps=b_p5[psl].rd() + (b_vt.wd() if n == 0 else []))
                        if n == 0:
                            b_vt.owr(tv)
                        else:
                            b_vt.owp(tv)
                        b_p5[psl].ord(tv)
                pr.add("v", b1_v)

            def vspill(q, gtt=gtt):
                t = q.dma("sv", v_sp[gtt], vtb[:], deps=b_vt.w)
                b_vt.ord(t)
                b_vsp.owp(t)
            pr.add("sy", vspill)

            if gtt < NTT:
                for hg in range(4):
                    def b3_t(q, sl=sl, hg=hg):
                        psl = hg % 2
                        tk = None
                        for j in range(4):
                            h = hg * 4 + j
                            pairs = [(wB3[:, kt * 1024 + h * 64: kt * 1024 + h * 64 + 64],
                                      cin[sl][:, kt * 128:(kt + 1) * 128])
                                     for kt in range(8)]
                            tk = q.mm_group(p5[psl][0:64, j * 128:(j + 1) * 128], pairs,
                                            deps=(b_in[sl].w + b_p5[psl].wd() + b_w3.w)
                                            if j == 0 else ())
                        b_p5[psl].owr(tk)
                        b_in[sl].ord(tk)
                    pr.add("t", b3_t)

                    def b3_v(q, hg=hg):
                        psl = hg % 2
                        t = q.op(q.eng.tensor_copy, kch[0:64, hg * 512:(hg + 1) * 512],
                                 p5[psl][0:64, :],
                                 deps=b_p5[psl].rd() + (b_kch.wd() if hg == 0 else []))
                        if hg == 0:
                            b_kch.owr(t)
                        else:
                            b_kch.owp(t)
                        b_p5[psl].ord(t)
                    pr.add("v", b3_v)

                def b3_spill(q, gtt=gtt):
                    t = q.dma("skf2", kf_sp[gtt, 0:64, :], kch[0:64, :], deps=b_kch.w)
                    b_kch.ord(t)
                    b_kfsp.owp(t)
                pr.add("sy", b3_spill)
        pr.flush("b1")

    with contextlib.ExitStack() as st_:
        sbp = lambda n, s, d: st_.enter_context(nc.sbuf_tensor(n, s, d))
        psp = lambda n, s, d: st_.enter_context(nc.psum_tensor(n, s, d))
        wQ = sbp("wQ", [128, 8 * 1024], BF16)
        wR = sbp("wR", [128, 8 * 1024], BF16)
        cin = [sbp(f"b2in{i}", [128, CD], BF16) for i in range(2)]
        vin = [sbp(f"vin{i}", [128, 2048], BF16) for i in range(2)]
        qcf = sbp("qcf", [128, 1024], BF16)
        qrt = sbp("qrt", [128, 1024], BF16)
        qtb = sbp("qtb", [128, 2048], BF16)
        csb = sbp("b2cs", [128, 512], BF16)
        snb = sbp("b2sn", [128, 512], BF16)
        rp1 = sbp("b2r1", [128, 512], BF16)
        rp2 = sbp("b2r2", [128, 256], BF16)
        mm2 = [psp(f"b2mm{i}", [128, 128], F32) for i in range(2)]
        p5 = [psp(f"b2p5{i}", [128, 512], F32) for i in range(2)]
        mps = psp("mps", [128, 2048], F32)
        b_wq, b_wr = B(), B()
        b_in, b_vin = [B(), B()], [B(), B()]
        b_qcf, b_qrt, b_qt, b_cs, b_ss, b_r1, b_r2, b_mps = (B() for _ in range(8))
        b_m2 = [B(), B()]
        b_p5 = [B(), B()]

        def b2_w(q):
            q.dma("dw", wQ[:].rearrange("p (a n) -> p a n", n=1024),
                  wuq.rearrange("(a p) n -> p a n", p=128))
            t = q.dma("dw", wR[:].rearrange("p (a n) -> p a n", n=1024),
                      wqr.rearrange("(a p) n -> p a n", p=128))
            b_wq.owr(t)
            b_wr.owr(t)
        pr.add("g", b2_w)

        for gtt in range(NTTA):
            sl = gtt % 2

            def iload(q, gtt=gtt, sl=sl):
                t = q.dma(f"db2{sl}", cin[sl][:], cq_sp[gtt],
                          deps=b_in[sl].wd() + b_cqsp.w)
                b_in[sl].owr(t)
                tv = q.dma(f"db2v{sl}", vin[sl][:], v_sp[gtt],
                           deps=b_vin[sl].wd() + b_vsp.w)
                b_vin[sl].owr(tv)
            pr.add("sy", iload)

            def csload(q, gtt=gtt):
                q.dma("db2c", csb[:], cos_a[gtt * 128:(gtt + 1) * 128, :],
                      deps=b_cs.wd() + b_ss.wd())
                ts_ = q.dma("db2c", snb[:], sin_a[gtt * 128:(gtt + 1) * 128, :])
                b_cs.owr(ts_)
                b_ss.owr(ts_)
            pr.add("g", csload)

            fm_proj_steps(wQ, b_wq, cin[sl], b_in[sl], qcf, b_qcf, 8, 8, mm2, b_m2)

            def qr_t(q):
                for n in range(2):
                    pairs = [(qcf[:, kt * 128:(kt + 1) * 128],
                              wR[:, kt * 1024 + n * 512: kt * 1024 + n * 512 + 512])
                             for kt in range(8)]
                    t = q.mm_group(p5[n][:], pairs,
                                   deps=b_qcf.w + b_p5[n].wd() + b_wr.w)
                    b_p5[n].owr(t)
                    b_qcf.ord(t)
            pr.add("t", qr_t)

            def qr_v(q):
                for n in range(2):
                    tv = q.op(q.eng.tensor_copy, qrt[:, n * 512:(n + 1) * 512],
                              p5[n][:],
                              deps=b_p5[n].rd() + (b_qrt.wd() if n == 0 else []))
                    if n == 0:
                        b_qrt.owr(tv)
                    else:
                        b_qrt.owp(tv)
                    b_p5[n].ord(tv)
            pr.add("v", qr_v)

            def qc_t(q, sl=sl):
                for n in range(2):
                    pairs = [(cin[sl][:, kt * 128:(kt + 1) * 128],
                              wQ[:, kt * 1024 + n * 512: kt * 1024 + n * 512 + 512])
                             for kt in range(8)]
                    t = q.mm_group(p5[n][:], pairs,
                                   deps=b_in[sl].w + b_p5[n].wd() + b_wq.w)
                    b_p5[n].owr(t)
                    b_in[sl].ord(t)
            pr.add("t", qc_t)

            def qc_v(q):
                for n in range(2):
                    src = p5[n][:].rearrange("p (h j) -> p h j", j=64)
                    dstv = qtb[:].rearrange("p (h j) -> p h j", j=128)[:, n * 8:(n + 1) * 8, 0:64]
                    tv = q.op(q.eng.tensor_copy, dstv, src,
                              deps=b_p5[n].rd() + (b_qt.wd() if n == 0 else []))
                    if n == 0:
                        b_qt.owr(tv)
                    else:
                        b_qt.owp(tv)
                    b_p5[n].ord(tv)
            pr.add("v", qc_v)

            qtv = qtb[:].rearrange("p (h j) -> p h j", j=128)[:, :, 64:128]
            rope_steps(qrt, b_qrt, qtv, b_qt, csb, b_cs, snb, b_ss, rp1, b_r1,
                       rp2, b_r2)

            def b2_m(q, gtt=gtt, sl=sl):
                t = None
                for h in range(NH):
                    t = q.mm_group(mps[:, h * 128:(h + 1) * 128],
                                   [(qtb[:, h * 128:(h + 1) * 128],
                                     vin[sl][:, h * 128:(h + 1) * 128])],
                                   deps=(b_qt.w + b_vin[sl].w +
                                         (b_mps.wd() if gtt == 0 else []))
                                   if h == 0 else (),
                                   start=(gtt == 0 and h % 4 == 0),
                                   stop=(gtt == NTTA - 1 and h % 4 == 3))
                b_mps.owp(t)
                b_qt.ord(t)
                b_vin[sl].ord(t)
            pr.add("t", b2_m)

        def b2_mev(q):
            t = q.op(q.eng.tensor_scalar, msb[:], mps[:], RSC, None, ALU.mult,
                     deps=b_mps.w + b_msb.wd())
            b_msb.owr(t)
            b_mps.ord(t)
        pr.add("v", b2_mev)
        pr.flush("b2")

    with contextlib.ExitStack() as st_:
        sbp = lambda n, s, d: st_.enter_context(nc.sbuf_tensor(n, s, d))
        psp = lambda n, s, d: st_.enter_context(nc.psum_tensor(n, s, d))
        wO = sbp("wO", [128, 16 * 2048], BF16)
        kin = [sbp(f"kin{i}", [128, 2048], BF16) for i in range(2)]
        outf = sbp("outf", [128, 2048], BF16)
        s_st = sbp("s_st", [128, 2048], BF16)
        p5 = [psp(f"p2p5{i}", [128, 512], F32) for i in range(2)]
        b_w, b_kin, b_of, b_sst = B(), [B(), B()], B(), B()
        b_p5 = [B(), B()]

        def p2_w(q):
            b_w.owr(q.dma("dw", wO[:].rearrange("p (a n) -> p a n", n=2048),
                          wout.rearrange("(a p) n -> p a n", p=128)))
        pr.add("g", p2_w)

        for gtt in range(NTT):
            sl = gtt % 2

            def iload(q, gtt=gtt, sl=sl):
                t = q.dma(f"dp2{sl}", kin[sl][:], kf_sp[gtt],
                          deps=b_kin[sl].wd() + b_kfsp.w)
                b_kin[sl].owr(t)
            pr.add("sy", iload)

            for hg in range(4):
                def of_t(q, sl=sl, hg=hg):
                    psl = hg % 2
                    tk = None
                    for j in range(4):
                        h = hg * 4 + j
                        tk = q.mm_group(p5[psl][:, j * 128:(j + 1) * 128],
                                        [(msb[:, h * 128:(h + 1) * 128],
                                          kin[sl][:, h * 128:(h + 1) * 128])],
                                        deps=(b_msb.w + b_kin[sl].w + b_p5[psl].wd())
                                        if j == 0 else ())
                    b_p5[psl].owr(tk)
                    b_kin[sl].ord(tk)
                pr.add("t", of_t)

                def of_v(q, hg=hg):
                    psl = hg % 2
                    t = q.op(q.eng.tensor_copy, outf[:, hg * 512:(hg + 1) * 512],
                             p5[psl][:],
                             deps=b_p5[psl].rd() + (b_of.wd() if hg == 0 else []))
                    if hg == 0:
                        b_of.owr(t)
                    else:
                        b_of.owp(t)
                    b_p5[psl].ord(t)
                pr.add("v", of_v)

            for half in range(0, 4, 2):
                def s_t(q, half=half):
                    for n in (half, half + 1):
                        psl = n % 2
                        pairs = [(outf[:, kt * 128:(kt + 1) * 128],
                                  wO[:, kt * 2048 + n * 512: kt * 2048 + n * 512 + 512])
                                 for kt in range(16)]
                        t = q.mm_group(p5[psl][:], pairs,
                                       deps=b_of.w + b_p5[psl].wd() + b_w.w)
                        b_p5[psl].owr(t)
                        b_of.ord(t)
                pr.add("t", s_t)

                def s_v(q, half=half):
                    for n in (half, half + 1):
                        psl = n % 2
                        tv = q.op(q.eng.tensor_copy, s_st[:, n * 512:(n + 1) * 512],
                                  p5[psl][:],
                                  deps=b_p5[psl].rd() + (b_sst.wd() if n == 0 else []))
                        if n == 0:
                            b_sst.owr(tv)
                        else:
                            b_sst.owp(tv)
                        b_p5[psl].ord(tv)
                pr.add("v", s_v)

            def sspill(q, gtt=gtt):
                t = q.dma("ssp", s_sp[gtt], s_st[:], deps=b_sst.w)
                b_sst.ord(t)
                b_ssp.owp(t)
            pr.add("sy", sspill)
        pr.flush("p2")

    with contextlib.ExitStack() as st_:
        sbp = lambda n, s, d: st_.enter_context(nc.sbuf_tensor(n, s, d))
        psp = lambda n, s, d: st_.enter_context(nc.psum_tensor(n, s, d))
        xn_f = sbp("xn_f", [128, 16 * Tc], BF16)
        act = sbp("act", [128, 12 * Tc], BF16)
        u16b = [sbp(f"u16b{i}", [128, 4 * 512], U16) for i in range(2)]
        wdq_ = [sbp(f"wdq{i}", [128, 16 * 512], BF16) for i in range(2)]
        sclb = sbp("sclb", [128, 4 * 512], BF16)
        gss = [sbp(f"gss{i}", [4, 4 * 512], BF16) for i in range(2)]
        t7b = sbp("t7b", [128, 512], U16)
        mbb = [sbp(f"mbb{i}", [128, 512], U16) for i in range(4)]
        sgb = [sbp(f"sgb{i}", [128, 512], U16) for i in range(4)]
        pcb = [sbp(f"pcb{i}", [128, 512], BF16) for i in range(4)]
        stg = [sbp(f"stg{i}", [128, G4 * 512], F32) for i in range(2)]
        p5 = [psp(f"p3p5{i}", [128, 512], F32) for i in range(2)]
        sclp = psp("sclp", [128, 2048], F32)
        b_xnf, b_act, b_scl = (B() for _ in range(3))
        b_u16 = [B(), B()]
        b_gss = [B(), B()]
        b_wd = [B(), B()]
        b_t7 = B()
        b_mb = [B() for _ in range(4)]
        b_sg = [B() for _ in range(4)]
        b_pc = [B() for _ in range(4)]
        b_stg = [B(), B()]
        b_sclp = B()
        b_p5 = [B(), B()]

        def ffn_xn(q):
            xv = xn_f[:].rearrange("p (a n) -> p a n", n=Tc)
            for tt in range(NTT):
                t = q.dma("dxnf", xv[:, :, tt * 128:(tt + 1) * 128],
                          xn_sp[tt].rearrange("p (a n) -> p a n", n=128),
                          deps=(b_xnf.wd() if tt == 0 else []) + b_xnsp.w)
                b_xnf.owp(t)
        pr.add("sy", ffn_xn)

        order_g = [0, 3, 1, 4, 2, 5]
        actv = act[:].rearrange("p (a n) -> p a n", n=Tc)
        xnv = xn_f[:].rearrange("p (a n) -> p a n", n=Tc)

        def deq_chunks(nu, slot):
            wd = wdq_[slot]
            bwd = b_wd[slot]
            ub_ = u16b[slot]
            bu = b_u16[slot]
            gs_ = gss[slot]
            bgs = b_gss[slot]

            def head():
                def f_sc(q):
                    tk = None
                    for u in range(nu):
                        tk = q.mm_group(sclp[:, u * 512:(u + 1) * 512],
                                        [(ind4[:], gs_[0:4, u * 512:(u + 1) * 512])],
                                        deps=(b_sclp.wd() + bgs.w + b_const.w)
                                        if u == 0 else ())
                    b_sclp.owr(tk)
                    bgs.ord(tk)
                pr.add("t", f_sc)

                def f_scv(q):
                    t = q.op(q.eng.tensor_copy, sclb[:, 0:nu * 512],
                             sclp[:, 0:nu * 512], deps=b_sclp.rd() + b_scl.wd())
                    b_scl.owr(t)
                    b_sclp.ord(t)
                pr.add("v", f_scv)

            def uchunk(u):
                tgk = [None] * 4

                def f_v1(q, u=u):
                    ub = ub_[:, u * 512:(u + 1) * 512]
                    for k in range(4):
                        t1 = q.op(q.eng.tensor_scalar, t7b[:], ub, 4 * k, 7,
                                  ALU.logical_shift_right, ALU.bitwise_and,
                                  deps=bu.w + b_t7.wd())
                        b_t7.owr(t1)
                        bu.ord(t1)
                        t2 = q.op(q.eng.tensor_scalar, mbb[k][:], t7b[:], 64, 16128,
                                  ALU.mult, ALU.add, deps=[t1] + b_mb[k].wd())
                        b_mb[k].owr(t2)
                        b_t7.ord(t2)
                        t3 = q.op(q.eng.tensor_scalar, sgb[k][:], ub, 12 - 4 * k,
                                  32768, ALU.logical_shift_left, ALU.bitwise_and,
                                  deps=b_sg[k].wd())
                        b_sg[k].owr(t3)
                        bu.ord(t3)
                pr.add("v", f_v1)

                def f_s(q, u=u):
                    for k in range(4):
                        t = q.op(q.eng.activation, pcb[k][:],
                                 mbb[k][:].bitcast(BF16), AF.Relu, bias=1.0,
                                 scale=-1.0, deps=b_mb[k].w + b_pc[k].wd())
                        b_pc[k].owr(t)
                        b_mb[k].ord(t)
                pr.add("s", f_s)

                def f_v2(q, u=u):
                    for k in range(4):
                        kt = 4 * u + k
                        wsl = wd[:, kt * 512:(kt + 1) * 512]
                        t1 = q.op(q.eng.tensor_tensor, wsl, mbb[k][:].bitcast(BF16),
                                  pcb[k][:], ALU.subtract,
                                  deps=b_pc[k].w + b_mb[k].w +
                                  (bwd.wd() if (u == 0 and k == 0) else []))
                        t2 = q.op(q.eng.tensor_tensor, wsl.bitcast(U16),
                                  wsl.bitcast(U16), sgb[k][:], ALU.bitwise_or,
                                  deps=[t1] + b_sg[k].w)
                        t3 = q.op(q.eng.tensor_tensor, wsl, wsl,
                                  sclb[:, u * 512:(u + 1) * 512], ALU.mult,
                                  deps=[t2] + b_scl.w)
                        if u == 0 and k == 0:
                            bwd.owr(t1)
                        bwd.owp(t3)
                        b_pc[k].ord(t3)
                        b_sg[k].ord(t3)
                        b_mb[k].ord(t3)
                        b_scl.ord(t3)
                pr.add("v", f_v2)

            return [head] + [(lambda u=u: uchunk(u)) for u in range(nu)]

        allblocks = [(e, kind, idx) for e in range(17)
                     for (kind, idx) in ([("gu", nb) for nb in order_g]
                                         + [("dn", hb) for hb in range(4)])]
        NB = len(allblocks)

        def emit_load(bi):
            e, kind, idx = allblocks[bi]
            slot = bi % 2
            ub_, bu = u16b[slot], b_u16[slot]
            gs_, bgs = gss[slot], b_gss[slot]
            if kind == "gu":
                def gu_load(q, e=e, nb=idx):
                    t = q.dma("du", ub_[:], gu16[e, nb], deps=bu.wd())
                    bu.owr(t)

                def gss_load(q, e=e, nb=idx):
                    tg = q.dma("dgs", gs_[:], gsc[e, nb], deps=bgs.wd())
                    bgs.owr(tg)
            else:
                def gu_load(q, e=e, hb=idx):
                    t = q.dma("du", ub_[:, 0:3 * 512], d16[e, hb],
                              deps=bu.wd())
                    bu.owr(t)

                def gss_load(q, e=e, hb=idx):
                    tg = q.dma("dgs", gs_[0:4, 0:3 * 512], dsc[e, hb],
                               deps=bgs.wd())
                    bgs.owr(tg)
            pr.add("sy", gu_load)
            pr.add("g", gss_load)

        def emit_mm(bi, deq):
            e, kind, idx = allblocks[bi]
            slot = bi % 2
            wd = wdq_[slot]
            bwd = b_wd[slot]
            steps = []
            if kind == "gu":
                nb = idx
                is_up = nb >= 3
                jbase = (nb - 3) * 4 if is_up else nb * 4
                for ntl in range(4):
                    j = jbase + ntl
                    for tc0 in range(0, TNC, 2):
                        tcs = [tc for tc in (tc0, tc0 + 1) if tc < TNC]

                        def step(ntl=ntl, tcs=tcs, j=j, is_up=is_up):
                            def gu_t(q):
                                for tc in tcs:
                                    psl = tc % 2
                                    pairs = [(wd[:, kt * 512 + ntl * 128:
                                                 kt * 512 + ntl * 128 + 128],
                                              xnv[:, kt, tc * TW:(tc + 1) * TW])
                                             for kt in range(16)]
                                    t = q.mm_group(p5[psl][:, 0:TW], pairs,
                                                   deps=bwd.w + b_xnf.w + b_p5[psl].wd())
                                    b_p5[psl].owr(t)
                                    bwd.ord(t)
                                    b_xnf.ord(t)
                            pr.add("t", gu_t)

                            if not is_up:
                                def gu_s(q):
                                    for tc in tcs:
                                        psl = tc % 2
                                        t = q.op(q.eng.activation,
                                                 actv[:, j, tc * TW:(tc + 1) * TW],
                                                 p5[psl][:, 0:TW], AF.Silu,
                                                 deps=b_p5[psl].rd() + b_act.w + b_act.r)
                                        b_act.owp(t)
                                        b_p5[psl].ord(t)
                                pr.add("s", gu_s)
                            else:
                                def gu_v(q):
                                    for tc in tcs:
                                        psl = tc % 2
                                        asl = actv[:, j, tc * TW:(tc + 1) * TW]
                                        t = q.op(q.eng.tensor_tensor, asl, asl,
                                                 p5[psl][:, 0:TW], ALU.mult,
                                                 deps=b_p5[psl].rd() + b_act.w)
                                        b_act.owp(t)
                                        b_p5[psl].ord(t)
                                pr.add("v", gu_v)
                        steps.append(step)
            else:
                hb = idx
                for tg in range(NTT // G4):
                    def step(tg=tg, hb=hb, e=e):
                        tis = list(range(G4))

                        def dn_t(q):
                            for ti in tis:
                                tt = tg * G4 + ti
                                psl = ti % 2
                                pairs = [(actv[:, kt, tt * 128:(tt + 1) * 128],
                                          wd[:, kt * 512:(kt + 1) * 512])
                                         for kt in range(12)]
                                t = q.mm_group(p5[psl][:], pairs,
                                               deps=b_act.w + bwd.w + b_p5[psl].wd())
                                b_p5[psl].owr(t)
                                bwd.ord(t)
                                b_act.ord(t)
                        pr.add("t", dn_t)

                        def dn_v(q):
                            ssl = tg % 2
                            for ti in tis:
                                tt = tg * G4 + ti
                                psl = ti % 2
                                dst = stg[ssl][:, ti * 512:(ti + 1) * 512]
                                if e < 16:
                                    t = q.op(q.eng.tensor_scalar, dst, p5[psl][:],
                                             cw[:, tt * 16 + e:tt * 16 + e + 1], None,
                                             ALU.mult,
                                             deps=b_p5[psl].rd() + (b_stg[ssl].wd() if ti == 0 else []) + b_cw.w)
                                else:
                                    t = q.op(q.eng.tensor_copy, dst, p5[psl][:],
                                             deps=b_p5[psl].rd() + (b_stg[ssl].wd() if ti == 0 else []))
                                b_stg[ssl].owr(t) if ti == 0 else b_stg[ssl].owp(t)
                                b_p5[psl].ord(t)
                        pr.add("v", dn_v)

                        def dn_acc(q):
                            ssl = tg % 2
                            kw = {} if e == 0 else {"accum_op": ALU.add}
                            gate = [Tok("ga", 16 * NACC * e)] if e > 0 else []
                            t = q.dma("ga",
                                      gacc[hb, tg * G4:(tg + 1) * G4]
                                      .rearrange("a p n -> p a n"),
                                      stg[ssl][:].rearrange("p (a n) -> p a n", n=512),
                                      deps=b_stg[ssl].w + gate, **kw)
                            b_stg[ssl].ord(t)
                            b_gacc.owp(t)
                        pr.add("g", dn_acc)
                    steps.append(step)

            if deq:
                deq[0]()
                nch = len(deq) - 1
                done = 0
                for i, st in enumerate(steps):
                    st()
                    want = min(nch, ((i + 2) * nch) // len(steps))
                    while done < want:
                        deq[1 + done]()
                        done += 1
                while done < nch:
                    deq[1 + done]()
                    done += 1
            else:
                for st in steps:
                    st()

        emit_load(0)
        emit_load(1)
        for c in deq_chunks(4 if allblocks[0][1] == "gu" else 3, 0):
            c()
        for bi in range(NB):
            if bi + 2 < NB:
                emit_load(bi + 2)
            deq = None
            if bi + 1 < NB:
                deq = deq_chunks(4 if allblocks[bi + 1][1] == "gu" else 3,
                                 (bi + 1) % 2)
            emit_mm(bi, deq)
        pr.flush("p3")
        t_ga = Tok("ga", ctx.cnt.get("ga", 0))

    with contextlib.ExitStack() as st_:
        sbp = lambda n, s, d: st_.enter_context(nc.sbuf_tensor(n, s, d))
        gac = sbp("gac", [128, H], F32)
        gsl = sbp("gsl", [128, H], BF16)
        s_in = sbp("s_in", [128, H], BF16)
        x_in = sbp("x_in", [128, H], F32)
        sgf = sbp("sgf", [128, H], F32)
        b_gac, b_gsl, b_sin, b_xin, b_sgf = (B() for _ in range(5))

        for tt in range(NTT):
            def p4_l(q, tt=tt):
                gv = gac[:].rearrange("p (a n) -> p a n", n=512)
                t1 = q.dma("dp4g", gv, gacc[:, tt].rearrange("a p n -> p a n"),
                           deps=b_gac.wd() + [t_ga])
                b_gac.owr(t1)
                t2 = q.dma("dp4s", s_in[:], s_sp[tt], deps=b_sin.wd() + b_ssp.w)
                b_sin.owr(t2)
                t3 = q.dma("dp4x", x_in[:], x_loc[tt * 128:(tt + 1) * 128, :],
                           deps=b_xin.wd())
                b_xin.owr(t3)
            pr.add("sy", p4_l)

            def p4_s(q):
                t = q.op(q.eng.activation, gsl[:], gac[:], AF.Silu,
                         deps=b_gac.w + b_gsl.wd())
                b_gsl.owr(t)
                b_gac.ord(t)
            pr.add("s", p4_s)

            def p4_v(q):
                t1 = q.op(q.eng.tensor_tensor, sgf[:], s_in[:], gsl[:], ALU.mult,
                          deps=b_gsl.w + b_sin.w + b_sgf.wd())
                b_sin.ord(t1)
                b_gsl.ord(t1)
                t2 = q.op(q.eng.tensor_tensor, sgf[:], x_in[:], sgf[:], ALU.add,
                          deps=[t1] + b_xin.w)
                b_sgf.owr(t2)
                b_xin.ord(t2)
            pr.add("v", p4_v)

            def p4_o(q, tt=tt):
                t = q.dma("dout", out_h[tt * 128:(tt + 1) * 128, :], sgf[:],
                          deps=b_sgf.w)
                b_sgf.ord(t)
            pr.add("sy", p4_o)

        def fin(q):
            q.wait(Tok("dout", ctx.cnt["dout"]))
        pr.add("sy", fin)
        pr.flush("p4")

    return nc


def _mk_pi(n):
    r = np.arange(n)
    tile, p = r // 128, r % 128
    u, k = tile // 4, tile % 4
    return 512 * u + 4 * p + k


def _pack_u16(packed):
    k8, n = packed.shape
    return np.ascontiguousarray(
        packed.view(np.uint16).reshape(k8, n, 2).transpose(0, 2, 1).reshape(k8 * 2, n))


def _resc(s, nu):
    g, n = s.shape
    return np.ascontiguousarray(
        s.reshape(nu, 4, n).transpose(1, 0, 2).reshape(4, nu * n))


def prep_inputs(inputs):
    x = np.asarray(inputs["L0_attn_x"], dtype=np.float32)
    B_, S, H_ = x.shape
    PIH = _mk_pi(H)
    PII = _mk_pi(I2)
    colsI = np.concatenate([PII, I2 + PII])

    wdkv = np.ascontiguousarray(np.asarray(inputs["L0_attn_W_dkv"])[PIH])
    wdq = np.ascontiguousarray(np.asarray(inputs["L0_attn_W_dq"])[PIH])
    wkr = np.ascontiguousarray(np.asarray(inputs["L0_attn_W_kr"])[PIH])
    wuk = np.ascontiguousarray(np.asarray(inputs["L0_attn_W_uk"]))
    wuv = np.ascontiguousarray(np.asarray(inputs["L0_attn_W_uv"]))
    wuq = np.ascontiguousarray(np.asarray(inputs["L0_attn_W_uq"]))
    wqr = np.ascontiguousarray(np.asarray(inputs["L0_attn_W_qr"]))
    wout = np.ascontiguousarray(np.asarray(inputs["L0_attn_W_out"]))
    gwT = np.ascontiguousarray(np.asarray(inputs["L0_ffn_gate_w"]).T[PIH])

    gup = np.asarray(inputs["L0_ffn_gate_up_packed"])
    gus = np.asarray(inputs["L0_ffn_gate_up_scales"])
    dnp = np.asarray(inputs["L0_ffn_down_packed"])
    dns = np.asarray(inputs["L0_ffn_down_scales"])
    sgup = np.asarray(inputs["L0_ffn_shared_gate_up_packed"])
    sgus = np.asarray(inputs["L0_ffn_shared_gate_up_scales"])
    sdnp = np.asarray(inputs["L0_ffn_shared_down_packed"])
    sdns = np.asarray(inputs["L0_ffn_shared_down_scales"])

    gu16 = np.stack([_pack_u16(gup[e])[:, colsI] for e in range(16)]
                    + [_pack_u16(sgup)[:, colsI]])
    gu16 = np.ascontiguousarray(
        gu16.reshape(17, 4, 128, 6, 512).transpose(0, 3, 2, 1, 4)
        .reshape(17, 6, 128, 2048))
    gscv = np.stack([_resc(gus[e][:, colsI], 4) for e in range(16)]
                    + [_resc(sgus[:, colsI], 4)]).astype(np.float32)
    gscv = np.ascontiguousarray(
        gscv.reshape(17, 4, 4, 6, 512).transpose(0, 3, 1, 2, 4)
        .reshape(17, 6, 4, 2048))
    d16 = np.stack([_pack_u16(dnp[e]) for e in range(16)] + [_pack_u16(sdnp)])
    d16 = np.ascontiguousarray(
        d16.reshape(17, 3, 128, 4, 512).transpose(0, 3, 2, 1, 4)
        .reshape(17, 4, 128, 1536))
    dscv = np.stack([_resc(dns[e], 3) for e in range(16)]
                    + [_resc(sdns, 3)]).astype(np.float32)
    dscv = np.ascontiguousarray(
        dscv.reshape(17, 4, 3, 4, 512).transpose(0, 3, 1, 2, 4)
        .reshape(17, 4, 4, 1536))

    half_dim = DR // 2
    inv = (1.0 / 10000.0 ** (np.arange(0, half_dim, 2, dtype=np.float32)
                             / np.float32(half_dim))).astype(np.float32)
    tpos = (np.arange(S, dtype=np.float32) / np.float32(40.0))
    fr = np.outer(tpos, inv).astype(np.float32)
    rot = np.concatenate([fr, fr], axis=1)
    cosT = np.tile(np.cos(rot), (1, NH)).astype(np.float32)
    sinT = np.tile(np.sin(rot), (1, NH)).astype(np.float32)

    identd = np.eye(128, dtype=np.float32)
    ind4d = (np.arange(128)[None, :] // 32 == np.arange(4)[:, None]
             ).astype(np.float32)

    Tc = S // 2
    shared = dict(wdkv=wdkv, wdq=wdq, wkr=wkr, wuk=wuk, wuv=wuv, wuq=wuq,
                  wqr=wqr, wout=wout, gw=gwT, gu16=gu16, gsc=gscv, d16=d16,
                  dsc=dscv, identd=identd, ind4d=ind4d)
    in_maps = []
    for c in range(8):
        b, hf = c // 2, c % 2
        oth = 1 - hf
        m = dict(shared)
        m["x_loc"] = np.ascontiguousarray(x[b, hf * Tc:(hf + 1) * Tc])
        m["x_oth"] = np.ascontiguousarray(x[b, oth * Tc:(oth + 1) * Tc])
        m["cos_a"] = np.ascontiguousarray(np.concatenate(
            [cosT[hf * Tc:(hf + 1) * Tc], cosT[oth * Tc:(oth + 1) * Tc]]))
        m["sin_a"] = np.ascontiguousarray(np.concatenate(
            [sinT[hf * Tc:(hf + 1) * Tc], sinT[oth * Tc:(oth + 1) * Tc]]))
        in_maps.append(m)

    return in_maps, S, (B_, H_)


def kernel(**inputs):
    in_maps, S, (B_, H_) = prep_inputs(inputs)
    Tc = S // 2
    nc = build_nc(S)
    res = run_bass_kernel_spmd(nc, in_maps, core_ids=list(range(8)))
    out = np.empty((B_, S, H_), dtype=np.float32)
    for c in range(8):
        b, hf = c // 2, c % 2
        out[b, hf * Tc:(hf + 1) * Tc, :] = res.results[c]["out"]
    return out
